# version 18
# baseline (speedup 1.0000x reference)
import numpy as np
import ml_dtypes
import concourse.bass as bass
import concourse.mybir as mybir
from concourse.bass_utils import run_bass_kernel_spmd
from contextlib import ExitStack

bfloat16 = ml_dtypes.bfloat16
F32 = mybir.dt.float32
BF16 = mybir.dt.bfloat16
AF = mybir.ActivationFunctionType
ALU = mybir.AluOpType
AXX = mybir.AxisListType.X

T = 4096
D = 1024
DI = 2048
NS = 16
NE = 8
NSEQ = 2
NCORES = 8
TC = 512
PC = 1024
NCH = T // TC
NPAIR = T // PC
SQ = 4
NSQ = NS // SQ
LN_EPS = 1e-5
U_CLAMP = -80.0

PASS_ORDER = ["pre_a", "pre_b", "pre_c", "f1f", "f2f", "f1b", "f2b", "c1", "c2"]


class HW:

    def __init__(self, h, kb, eng):
        self._h = h
        self._kb = kb
        self._eng = eng
        self.last = None

    def fence(self):
        if self.last is None:
            return
        kb, eng = self._kb, self._eng
        kb.mark(eng, self.last)
        sem = kb._msem[eng]
        self._h.wait_ge(kb.sems[sem], kb.cnt[sem])
        kb.hi[eng][sem] = kb.cnt[sem]
        self.last = None

    def __getattr__(self, name):
        if name in ("_h", "_kb", "_eng", "last"):
            raise AttributeError(name)
        f = getattr(self._h, name)
        if not callable(f):
            return f
        kb, eng = self._kb, self._eng

        def g(*a, **k):
            r = f(*a, **k)
            if hasattr(r, "then_inc"):
                if name.startswith("dma_start"):
                    kb.dmark(eng, r)
                else:
                    self.last = r
            return r
        return g


class KB:
    def __init__(self, nc, block):
        self.nc = nc
        self.block = block
        self.st = ExitStack()
        self.sems = {}
        for name in ("mT", "mV", "mS", "mG", "dY", "dG"):
            self.sems[name] = self.st.enter_context(nc.semaphore(name))
        self.cnt = {k: 0 for k in self.sems}
        self.hi = {e: {} for e in ("T", "V", "S", "G", "Y")}
        self._m = {"T": block.tensor, "V": block.vector, "S": block.scalar,
                   "G": block.gpsimd, "Y": block.sync}
        self._msem = {"T": "mT", "V": "mV", "S": "mS", "G": "mG"}
        self._dsem = {"Y": "dY", "G": "dG", "S": "dY", "V": "dY", "T": "dY"}
        self.nsect = 0

    def dmark(self, eng, ins):
        sem = self._dsem[eng]
        self.cnt[sem] += 16
        ins.then_inc(self.sems[sem], 16)

    def mark(self, eng, ins):
        sem = self._msem[eng]
        self.cnt[sem] += 1
        ins.then_inc(self.sems[sem], 1)

    def S(self, eng, fn, deps=None):
        if deps is None:
            toks = [(s, self.cnt[s]) for s in ("mT", "mV", "mS", "mG", "dY", "dG")]
        else:
            flat = []
            for t in deps:
                if t is None:
                    continue
                if isinstance(t, list):
                    flat.extend(x for x in t if x is not None)
                else:
                    flat.append(t)
            best = {}
            for sem, val in flat:
                best[sem] = max(best.get(sem, 0), val)
            toks = sorted(best.items())
        self.nsect += 1

        def wrapper(h):
            for sem, val in toks:
                if val > 0 and self.hi[eng].get(sem, 0) < val:
                    h.wait_ge(self.sems[sem], val)
                    self.hi[eng][sem] = val
            hw = HW(h, self, eng)
            fn(hw)
            if hw.last is not None and eng in self._msem:
                self.mark(eng, hw.last)
        before = dict(self.cnt)
        self._m[eng](wrapper)
        out = []
        if eng in self._msem:
            s = self._msem[eng]
            if self.cnt[s] > before[s]:
                out.append((s, self.cnt[s]))
        s = self._dsem.get(eng)
        if s is not None and self.cnt[s] > before[s]:
            out.append((s, self.cnt[s]))
        if len(out) == 1:
            return out[0]
        return out if out else None

    def barrier(self):
        toks = [(s, self.cnt[s]) for s in ("mT", "mV", "mS", "mG", "dY", "dG")]
        for eng in ("T", "V", "S", "G", "Y"):
            def fn(h, _t=toks):
                for sem, val in _t:
                    if val > 0 and self.hi[eng].get(sem, 0) < val:
                        h.wait_ge(self.sems[sem], val)
                        self.hi[eng][sem] = val
            self._m[eng](fn)

    def final_wait(self):
        toks = [(s, self.cnt[s]) for s in ("mT", "mV", "mS", "mG", "dY", "dG")]
        for eng in ("T", "V", "S", "G", "Y"):
            def fn(h, _t=toks):
                for sem, val in _t:
                    if val > 0 and self.hi[eng].get(sem, 0) < val:
                        h.wait_ge(self.sems[sem], val)
                        self.hi[eng][sem] = val
            self._m[eng](fn)


def build_program(dbg=(), upto=None):
    nc = bass.Bass("TRN2", target_bir_lowering=False, debug=False)
    passes = PASS_ORDER if upto is None else PASS_ORDER[:PASS_ORDER.index(upto) + 1]

    def din(name, shape, dt=F32):
        return nc.dram_tensor(name, shape, dt, kind="ExternalInput")

    x_in = din("x", [NSEQ, T, D])
    inwT = {d: din(f"inwT_{d}", [D, 2 * DI]) for d in "fb"}
    xwT = {d: din(f"xwT_{d}", [DI, 96]) for d in "fb"}
    dtwT = {d: din(f"dtwT_{d}", [64, DI]) for d in "fb"}
    outwT = {d: din(f"outwT_{d}", [DI, D]) for d in "fb"}
    convw = {d: din(f"convw_{d}", [DI, 4]) for d in "fb"}
    convb = {d: din(f"convb_{d}", [DI]) for d in "fb"}
    dtb = {d: din(f"dtb_{d}", [DI]) for d in "fb"}
    alog = {d: din(f"alog_{d}", [DI, NS]) for d in "fb"}
    dvec = {d: din(f"dvec_{d}", [DI]) for d in "fb"}
    ff1T = din("ff1T", [D, 4 * D])
    ff2T = din("ff2T", [4 * D, D])
    ffb1_t = din("ffb1", [4 * D])
    ffb2_t = din("ffb2", [D])
    lng_t = din("lng", [D])
    lnb_t = din("lnb", [D])
    gw_t = din("gw", [D, NE])
    experts_t = din("experts", [NE, D, D])
    ident_in = din("ident", [128, 128], BF16)
    ident32_in = din("ident32", [128, 128], F32)
    ident8_in = din("ident8", [8, 8], F32)
    sel8_in = din("sel8", [8, 8 * 128], F32)
    sel8b_in = din("sel8b", [8, 8 * 128], BF16)
    sel32b_in = din("sel32b", [32, 32 * 128], BF16)
    nsel32b_in = din("nsel32b", [32, 32 * 128], BF16)

    out_t = nc.dram_tensor("out", [NSEQ, T, D], F32, kind="ExternalOutput")

    def scratch(name, shape, dt):
        kind = "ExternalOutput" if name in dbg else "Internal"
        return nc.dram_tensor(name, shape, dt, kind=kind)

    xn_fm = scratch("xn_fm", [NSEQ, 8, 128, T], BF16)
    moe_fm = scratch("moe_fm", [NSEQ, 8, 128, T], BF16)
    xc_sp = scratch("xc_sp", [NSEQ, 16, 128, T], BF16)
    u_sp = scratch("u_sp", [NSEQ, 16, 128, T], BF16)
    sz_sp = scratch("sz_sp", [NSEQ, 16, 128, T], BF16)
    ut_sp = scratch("ut_sp", [NSEQ, 16, 128, T], BF16)
    bc_sp = scratch("bc_sp", [NSEQ, 2, NS, T], BF16)
    mix = {d: scratch(f"mix_{d}", [NSEQ, 8, 128, T], BF16) for d in "fb"}
    wm_sp = scratch("wm_sp", [NSEQ, NE, T], BF16)
    st_dbg = scratch("st_dbg", [8, 128, 32], F32)
    f2_dbg = scratch("f2_dbg", [4, 128, 4 * 1028], F32)
    g_dbg = scratch("g_dbg", [NSEQ, NE, T], F32)
    wtm_dbg = scratch("wtm_dbg", [NSEQ, 128, 256], F32)
    cnt_dbg = scratch("cnt_dbg", [NSEQ, 128, 256], F32)
    x2_sp = scratch("x2_sp", [NSEQ, 8, 128, T], BF16)
    ffh_sp = scratch("ffh_sp", [NSEQ, 8, 128, T], BF16)

    with nc.Block() as block:
        kb = KB(nc, block)
        sb = kb.st.enter_context

        ident = sb(nc.sbuf_tensor("identt", [128, 128], BF16))
        ident32 = sb(nc.sbuf_tensor("identt32", [128, 128], F32))
        ident8 = sb(nc.sbuf_tensor("identt8", [8, 8], F32))

        ones1b = sb(nc.sbuf_tensor("ones1b", [1, 128], BF16))
        nones1b = sb(nc.sbuf_tensor("nones1b", [1, 128], BF16))
        ones1f = sb(nc.sbuf_tensor("ones1f", [1, 128], F32))
        ones8f = sb(nc.sbuf_tensor("ones8f", [8, 8], F32))
        ones128b = sb(nc.sbuf_tensor("ones128b", [128, 1], BF16))
        econst = sb(nc.sbuf_tensor("econst", [128, NE], F32))
        expA = {d: sb(nc.sbuf_tensor(f"expA{d}", [128, 16 * NS], F32)) for d in "fb"}
        cw_sb = {d: sb(nc.sbuf_tensor(f"cw{d}", [128, 16 * 4], F32)) for d in "fb"}
        cb_sb = {d: sb(nc.sbuf_tensor(f"cb{d}", [128, 16], F32)) for d in "fb"}
        ndtb_sb = {d: sb(nc.sbuf_tensor(f"ndtb{d}", [128, 16], F32)) for d in "fb"}
        dv_sb = {d: sb(nc.sbuf_tensor(f"dv{d}", [128, 16], F32)) for d in "fb"}
        ffb1_sb = sb(nc.sbuf_tensor("ffb1s", [128, 32], F32))
        ffb2_sb = sb(nc.sbuf_tensor("ffb2s", [128, 8], F32))
        lng_sb = sb(nc.sbuf_tensor("lngs", [128, 8], F32))
        lnb_sb = sb(nc.sbuf_tensor("lnbs", [128, 8], F32))
        gw_sb = sb(nc.sbuf_tensor("gws", [128, 8 * NE], BF16))

        PS = [sb(nc.psum_tensor(f"ps{i}", [128, 512], F32)) for i in range(8)]

        def setup_g(h):
            for d in "fb":
                h.dma_start(out=expA[d][:].rearrange("p (k s) -> p k s", s=NS),
                            in_=alog[d].ap().rearrange("(k p) s -> p k s", p=128))
                h.dma_start(out=cw_sb[d][:].rearrange("p (k j) -> p k j", j=4),
                            in_=convw[d].ap().rearrange("(k p) j -> p k j", p=128))
                h.dma_start(out=cb_sb[d][:],
                            in_=convb[d].ap().rearrange("(k p) -> p k", p=128), allow_slow_non_contiguous=True)
                h.dma_start(out=ndtb_sb[d][:],
                            in_=dtb[d].ap().rearrange("(k p) -> p k", p=128), allow_slow_non_contiguous=True)
                h.dma_start(out=dv_sb[d][:],
                            in_=dvec[d].ap().rearrange("(k p) -> p k", p=128), allow_slow_non_contiguous=True)
            h.dma_start(out=ffb1_sb[:], in_=ffb1_t.ap().rearrange("(k p) -> p k", p=128), allow_slow_non_contiguous=True)
            h.dma_start(out=ffb2_sb[:], in_=ffb2_t.ap().rearrange("(k p) -> p k", p=128), allow_slow_non_contiguous=True)
            h.dma_start(out=lng_sb[:], in_=lng_t.ap().rearrange("(k p) -> p k", p=128), allow_slow_non_contiguous=True)
            h.dma_start(out=lnb_sb[:], in_=lnb_t.ap().rearrange("(k p) -> p k", p=128), allow_slow_non_contiguous=True)
            h.dma_start(out=gw_sb[:].rearrange("p (k e) -> p k e", e=NE),
                        in_=gw_t.ap().rearrange("(k p) e -> p k e", p=128))
            h.dma_start(out=ident[:], in_=ident_in.ap())
            h.dma_start(out=ident32[:], in_=ident32_in.ap())
            h.dma_start(out=ident8[:], in_=ident8_in.ap())

        kb.S("G", setup_g)

        def setup_v(h):
            h.memset(ones1b[:], 1.0)
            h.memset(nones1b[:], -1.0)
            h.memset(ones1f[:], 1.0)
            h.memset(ones8f[:], 1.0)
            h.memset(ones128b[:], 1.0)
            for e in range(NE):
                h.memset(econst[:, e:e + 1], float(e + 1))
            for d in "fb":
                h.tensor_scalar_mul(ndtb_sb[d][:], ndtb_sb[d][:], -1.0)
        kb.S("V", setup_v)

        def setup_s(h):
            for d in "fb":
                h.activation(expA[d][:], expA[d][:], AF.Exp)
        kb.S("S", setup_s)

        if "pre_a" in passes:
            with nc.sbuf_tensor("pa_glog", [NE, NSEQ * T], F32) as pa_glog:
                with nc.sbuf_tensor("pa_x0", [128, 4 * D], F32) as pa_x0, \
                     nc.sbuf_tensor("pa_x1", [128, 4 * D], F32) as pa_x1, \
                     nc.sbuf_tensor("pa_sq0", [128, 4 * D], F32) as pa_sq0, \
                     nc.sbuf_tensor("pa_sq1", [128, 4 * D], F32) as pa_sq1, \
                     nc.sbuf_tensor("pa_st0", [128, 32], F32) as pa_st0, \
                     nc.sbuf_tensor("pa_st1", [128, 32], F32) as pa_st1, \
                     nc.sbuf_tensor("pa_xn0", [128, 4 * D], F32) as pa_xn0, \
                     nc.sbuf_tensor("pa_xn1", [128, 4 * D], F32) as pa_xn1, \
                     nc.sbuf_tensor("pa_stage", [128, 8 * T], BF16) as pa_stage:
                    PX = [pa_x0, pa_x1]
                    PQ = [pa_sq0, pa_sq1]
                    PT = [pa_st0, pa_st1]
                    PN = [pa_xn0, pa_xn1]
                    t_ld = {}
                    t_sq = {}
                    t_stats = {}
                    t_apply = {}
                    t_tpev = {}
                    t_gmm = {}
                    t_gevac = {}
                    t_spill = None
                    gi = 0
                    for seq in range(NSEQ):
                        for xt in range(8):
                            gp_ = gi % 2

                            def ld(h, seq=seq, xt=xt, gp_=gp_):
                                h.dma_start(
                                    out=PX[gp_][:].rearrange("p (g d) -> p g d", d=D),
                                    in_=x_in.ap()[seq, xt * 512:(xt + 1) * 512, :]
                                    .rearrange("(g p) d -> p g d", p=128))
                            t_ld[gi] = kb.S("Y", ld, deps=[
                                t_apply.get(gi - 2), t_sq.get(gi - 2),
                                t_stats.get(gi - 2)])

                            def sq(h, gp_=gp_):
                                h.activation(PQ[gp_][:], PX[gp_][:], AF.Square)
                            t_sq[gi] = kb.S("S", sq, deps=[
                                t_ld[gi], t_stats.get(gi - 2)])

                            def stats(h, gp_=gp_):
                                st_ = PT[gp_]
                                for g in range(4):
                                    h.reduce_sum(st_[:, g:g + 1],
                                                 PX[gp_][:, g * D:(g + 1) * D],
                                                 axis=AXX)
                                    h.reduce_sum(st_[:, 4 + g:5 + g],
                                                 PQ[gp_][:, g * D:(g + 1) * D],
                                                 axis=AXX)
                                h.fence()
                                h.tensor_scalar_mul(st_[:, 8:16], st_[:, 0:8], 1.0 / D)
                                h.fence()
                                h.tensor_tensor(st_[:, 16:20], st_[:, 8:12],
                                                st_[:, 8:12], ALU.mult)
                                h.fence()
                                h.tensor_tensor(st_[:, 16:20], st_[:, 12:16],
                                                st_[:, 16:20], ALU.subtract)
                                h.fence()
                                h.tensor_scalar_add(st_[:, 16:20], st_[:, 16:20],
                                                    LN_EPS)
                            t_stats[gi] = kb.S("V", stats, deps=[
                                t_ld[gi], t_sq[gi], t_apply.get(gi - 2)])

                            t_sqrt = kb.S("S", lambda h, gp_=gp_: h.activation(
                                PT[gp_][:, 20:24], PT[gp_][:, 16:20], AF.Sqrt),
                                deps=[t_stats[gi]])

                            def apply(h, gp_=gp_):
                                st_ = PT[gp_]
                                h.reciprocal(st_[:, 24:28], st_[:, 20:24])
                                h.fence()
                                for g in range(4):
                                    h.tensor_scalar(
                                        PN[gp_][:, g * D:(g + 1) * D],
                                        PX[gp_][:, g * D:(g + 1) * D],
                                        st_[:, 8 + g:9 + g], st_[:, 24 + g:25 + g],
                                        ALU.subtract, ALU.mult)
                            t_apply[gi] = kb.S("V", apply, deps=[
                                t_sqrt, t_tpev.get(gi - 2)])

                            t_tp_prev = {}
                            for dt8 in range(8):
                                bank = 4 + dt8 % 2

                                def tp(h, dt8=dt8, bank=bank, gp_=gp_):
                                    for g in range(4):
                                        h.transpose(
                                            PS[bank][:, g * 128:(g + 1) * 128],
                                            PN[gp_][:, g * D + dt8 * 128:
                                                    g * D + (dt8 + 1) * 128],
                                            ident32[:])
                                ttp = kb.S("T", tp, deps=[
                                    t_apply[gi], t_tp_prev.get(dt8 - 2),
                                    t_gmm.get(gi - 1) if dt8 < 2 else None])

                                def ev(h, dt8=dt8, bank=bank, xt=xt):
                                    h.tensor_copy(
                                        pa_stage[:, dt8 * T + xt * 512:
                                                 dt8 * T + (xt + 1) * 512],
                                        PS[bank][:, 0:512])
                                t_tp_prev[dt8] = kb.S("V", ev, deps=[
                                    ttp, t_spill])
                            t_tpev[gi] = t_tp_prev[7]

                            def gmm(h, xt=xt):
                                for k8 in range(8):
                                    h.matmul(
                                        PS[6][0:NE, 0:512],
                                        gw_sb[:, k8 * NE:(k8 + 1) * NE],
                                        pa_stage[:, k8 * T + xt * 512:
                                                 k8 * T + (xt + 1) * 512],
                                        start=(k8 == 0), stop=(k8 == 7),
                                        skip_group_check=True)
                            t_gmm[gi] = kb.S("T", gmm, deps=[
                                t_tp_prev[7],
                                [t_tp_prev[i] for i in range(8)],
                                t_gevac.get(gi - 1)])

                            def gevac(h, seq=seq, xt=xt):
                                h.tensor_copy(
                                    pa_glog[0:NE, seq * T + xt * 512:
                                            seq * T + (xt + 1) * 512],
                                    PS[6][0:NE, 0:512])
                            t_gevac[gi] = kb.S("V", gevac, deps=[t_gmm[gi]])
                            gi += 1

                        def spill(h, seq=seq):
                            h.dma_start(
                                out=xn_fm.ap()[seq].rearrange("k p t -> p k t"),
                                in_=pa_stage[:].rearrange("p (k t) -> p k t", t=T))
                        t_spill = kb.S("Y", spill, deps=[
                            [t_tpev[seq * 8 + i] for i in range(8)],
                            t_gmm[seq * 8 + 7]])

                if "pre_b" in passes:
                    kb.barrier()
                    with nc.sbuf_tensor("sel8sb", [8, 8 * 128], F32) as sel8, \
                         nc.sbuf_tensor("pb_ex", [NE, T], F32) as pb_ex, \
                         nc.sbuf_tensor("pb_sc", [NE, 4], F32) as pb_sc, \
                         nc.sbuf_tensor("pb_lse", [NE, T], F32) as pb_lse, \
                         nc.sbuf_tensor("pb_gb", [128, 2 * T], F32) as pb_gb, \
                         nc.sbuf_tensor("pb_gt", [128, 32 * 8], F32) as pb_gt, \
                         nc.sbuf_tensor("pb_cnt", [128, 32 * 8], F32) as pb_cnt, \
                         nc.sbuf_tensor("pb_junk", [128, T], F32) as pb_junk, \
                         nc.sbuf_tensor("pb_wk", [128, 48], F32) as pb_wk, \
                         nc.sbuf_tensor("pb_wtm", [128, 32 * 8], F32) as pb_wtm, \
                         nc.sbuf_tensor("pb_wm", [NE, T], BF16) as pb_wm:
                        t_s8 = kb.S("G", lambda h: h.dma_start(out=sel8[:],
                                                               in_=sel8_in.ap()))
                        for seq in range(NSEQ):
                            gsl = pa_glog[0:NE, seq * T:(seq + 1) * T]

                            kb.S("V", lambda h, gsl=gsl:
                                 h.tensor_scalar_max(gsl, gsl, 1e-6))
                            kb.S("S", lambda h, gsl=gsl: h.activation(gsl, gsl, AF.Ln))

                            for it in range(8):
                                def lse_t(h, gsl=gsl):
                                    h.activation(pb_ex[:, 0:T], gsl, AF.Exp,
                                                 accum_out=pb_sc[:, 0:1])
                                    h.fence()
                                    h.activation(pb_sc[:, 1:2], pb_sc[:, 0:1], AF.Ln)
                                kb.S("S", lse_t)

                                kb.S("V", lambda h, gsl=gsl: h.tensor_scalar(
                                    gsl, gsl, pb_sc[:, 1:2], None, ALU.subtract))

                                kb.S("S", lambda h, gsl=gsl:
                                     h.activation(pb_ex[:, 0:T], gsl, AF.Exp))

                                for sl in range(8):
                                    def mm1(h, sl=sl):
                                        h.matmul(PS[6][0:NE, 0:512], ones8f[:],
                                                 pb_ex[:, sl * 512:(sl + 1) * 512],
                                                 start=True, stop=True)
                                    kb.S("T", mm1)

                                    def lnev(h, sl=sl):
                                        h.activation(
                                            pb_lse[:, sl * 512:(sl + 1) * 512],
                                            PS[6][0:NE, 0:512], AF.Ln)
                                    kb.S("S", lnev)

                                kb.S("V", lambda h, gsl=gsl: h.tensor_tensor(
                                    gsl, gsl, pb_lse[:, 0:T], ALU.subtract))

                            t_gates = kb.S("S", lambda h, gsl=gsl:
                                           h.activation(pb_ex[:, 0:T], gsl, AF.Exp))

                            if "g_dbg" in dbg:
                                def gdmp(h, seq=seq):
                                    h.dma_start(out=g_dbg.ap()[seq], in_=pb_ex[0:NE, 0:T])
                                kb.S("Y", gdmp)

                            t_gtev_last = None
                            for tt in range(32):
                                bank = 6 + tt % 2

                                def gtp(h, tt=tt, bank=bank):
                                    h.transpose(PS[bank][:, 0:NE],
                                                pb_ex[0:NE, tt * 128:(tt + 1) * 128],
                                                ident8[:])
                                tgt = kb.S("T", gtp, deps=[t_gates, t_gtev_last])

                                def gtev(h, tt=tt, bank=bank):
                                    h.tensor_copy(pb_gt[:, tt * 8:(tt + 1) * 8],
                                                  PS[bank][:, 0:NE])
                                t_gtev_last = kb.S("V", gtev, deps=[tgt])

                            t_gbev = {}
                            t_cnts = {}
                            t_bank = {6: t_gtev_last, 7: t_gtev_last}
                            for e in range(NE):
                                ep = e % 2
                                gbuf = pb_gb[:, ep * T:(ep + 1) * T]
                                for sl in range(8):
                                    bank = 6 + sl % 2

                                    def gbmm(h, e=e, sl=sl, bank=bank):
                                        h.matmul(PS[bank][:, 0:512],
                                                 sel8[:, e * 128:(e + 1) * 128],
                                                 pb_ex[0:NE, sl * 512:(sl + 1) * 512],
                                                 start=True, stop=True,
                                                 skip_group_check=True)
                                    tg = kb.S("T", gbmm, deps=[
                                        t_bank.get(bank),
                                        t_cnts.get(e - 2), t_gates, t_s8])

                                    def gbev(h, sl=sl, bank=bank, gbuf=gbuf):
                                        h.copy(gbuf[:, sl * 512:(sl + 1) * 512],
                                               PS[bank][:, 0:512])
                                    t_gbev[(e, sl)] = kb.S("S", gbev, deps=[tg])
                                    t_bank[bank] = t_gbev[(e, sl)]

                                def cnts(h, e=e, gbuf=gbuf):
                                    for tt in range(32):
                                        h.tensor_scalar(
                                            pb_junk[:, 0:T], gbuf,
                                            pb_gt[:, tt * 8 + e:tt * 8 + e + 1], 0.0,
                                            ALU.is_gt, ALU.add,
                                            accum_out=pb_cnt[:, tt * 8 + e:
                                                             tt * 8 + e + 1])
                                t_cnts[e] = kb.S("V", cnts, deps=[
                                    t_gbev[(e, 7)], t_gtev_last])

                            if "cnt_dbg" in dbg:
                                def cdmp(h, seq=seq):
                                    h.dma_start(out=cnt_dbg.ap()[seq], in_=pb_cnt[:])
                                kb.S("Y", cdmp)

                            for tt in range(32):
                                def winner(h, tt=tt):
                                    m = pb_cnt[:, tt * 8:(tt + 1) * 8]
                                    sel = pb_wk[:, 0:8]
                                    key = pb_wk[:, 8:16]
                                    win = pb_wk[:, 16:24]
                                    gv = pb_wk[:, 24:32]
                                    kmax = pb_wk[:, 32:33]
                                    gwin = pb_wk[:, 33:34]
                                    hard = pb_wk[:, 34:35]
                                    h.tensor_single_scalar(sel, m, 512.0, ALU.is_lt)
                                    h.tensor_single_scalar(key, m, 8.0, ALU.mult)
                                    h.fence()
                                    h.tensor_tensor(key, key, econst[:], ALU.add)
                                    h.fence()
                                    h.tensor_tensor(key, key, sel, ALU.mult)
                                    h.fence()
                                    h.reduce_max(kmax, key, axis=AXX)
                                    h.fence()
                                    h.tensor_scalar(win, key, kmax, None, ALU.is_equal)
                                    h.fence()
                                    h.tensor_tensor(win, win, sel, ALU.mult)
                                    h.fence()
                                    h.tensor_tensor(gv, win,
                                                    pb_gt[:, tt * 8:(tt + 1) * 8],
                                                    ALU.mult)
                                    h.fence()
                                    h.reduce_sum(gwin, gv, axis=AXX)
                                    h.fence()
                                    h.tensor_single_scalar(hard, gwin, 0.5, ALU.is_gt)
                                    h.fence()
                                    h.tensor_scalar(pb_wtm[:, tt * 8:(tt + 1) * 8],
                                                    win, hard, None, ALU.mult)
                                kb.S("V", winner)

                            if "wtm_dbg" in dbg:
                                def wtdmp(h, seq=seq):
                                    h.dma_start(out=wtm_dbg.ap()[seq], in_=pb_wtm[:])
                                kb.S("Y", wtdmp)

                            for tt in range(32):
                                def wtp(h, tt=tt):
                                    h.transpose(PS[6][0:NE, 0:128],
                                                pb_wtm[:, tt * 8:(tt + 1) * 8],
                                                ident32[:])
                                kb.S("T", wtp)

                                def wtev(h, tt=tt):
                                    h.copy(pb_wm[0:NE, tt * 128:(tt + 1) * 128],
                                           PS[6][0:NE, 0:128])
                                kb.S("S", wtev)

                            def wspill(h, seq=seq):
                                h.dma_start(out=wm_sp.ap()[seq], in_=pb_wm[0:NE, 0:T])
                            kb.S("Y", wspill)

        if "pre_c" in passes:
            kb.barrier()
            with nc.sbuf_tensor("pc_ex", [128, 64 * 1024], BF16) as pc_ex, \
                 nc.sbuf_tensor("pc_xn0", [128, 8 * TC], BF16) as pc_xn0, \
                 nc.sbuf_tensor("pc_xn1", [128, 8 * TC], BF16) as pc_xn1, \
                 nc.sbuf_tensor("pc_mb0", [128, NE * TC], BF16) as pc_mb0, \
                 nc.sbuf_tensor("pc_mb1", [128, NE * TC], BF16) as pc_mb1, \
                 nc.sbuf_tensor("pc_xnm0", [128, 8 * TC], BF16) as pc_xnm0, \
                 nc.sbuf_tensor("pc_xnm1", [128, 8 * TC], BF16) as pc_xnm1, \
                 nc.sbuf_tensor("pc_out", [128, 8 * TC], BF16) as pc_out:
                XNB = [pc_xn0, pc_xn1]
                MBB = [pc_mb0, pc_mb1]
                XNM = [pc_xnm0, pc_xnm1]

                def ldex(h):
                    h.dma_start(
                        out=pc_ex[:].rearrange("p (e i o) -> p e i o", e=NE, i=8),
                        in_=experts_t.ap().rearrange("e (i p) o -> p e i o", p=128))
                t_ldex = kb.S("G", ldex)

                NCHT = NSEQ * NCH
                t_mbb = {}
                t_xnm = {}
                t_emm = {}
                t_moev = {}
                t_moesp = {}
                t_ldxn = {}

                for gi in range(NCHT):
                    seq, ch = divmod(gi, NCH)
                    c0 = ch * TC
                    cp = gi % 2

                    def ldxn(h, seq=seq, c0=c0, cp=cp):
                        h.dma_start(
                            out=XNB[cp][:].rearrange("p (k t) -> p k t", t=TC),
                            in_=xn_fm.ap()[seq, :, :, c0:c0 + TC]
                            .rearrange("k p t -> p k t"))
                    t_ldxn[gi] = kb.S("Y", ldxn, deps=[
                        t_xnm.get((gi - 2) * 8 + 7)])

                    def mbb(h, seq=seq, c0=c0, cp=cp):
                        h.dma_start(
                            out=MBB[cp][:].rearrange("p (e t) -> p e t", t=TC),
                            in_=wm_sp.ap()[seq, :, c0:c0 + TC]
                            .rearrange("e t -> () e t")
                            .broadcast_to((128, NE, TC)))
                    t_mbb[gi] = kb.S("Y", mbb, deps=[
                        t_xnm.get((gi - 2) * 8 + 7)])

                    for e in range(NE):
                        ge = gi * 8 + e
                        ep = e % 2

                        def xnm(h, e=e, cp=cp, ep=ep):
                            for k8 in range(8):
                                h.tensor_tensor(
                                    XNM[ep][:, k8 * TC:(k8 + 1) * TC],
                                    XNB[cp][:, k8 * TC:(k8 + 1) * TC],
                                    MBB[cp][:, e * TC:(e + 1) * TC], ALU.mult)
                        t_xnm[ge] = kb.S("V", xnm, deps=[
                            t_ldxn[gi], t_mbb[gi], t_emm.get(ge - 2),
                            t_moev.get(gi - 1) if e == 0 else None])

                        def emm(h, e=e, ep=ep):
                            for ot in range(8):
                                for k8 in range(8):
                                    h.matmul(
                                        PS[ot][:, 0:512],
                                        pc_ex[:, (e * 64 + k8 * 8 + ot) * 128:
                                              (e * 64 + k8 * 8 + ot + 1) * 128],
                                        XNM[ep][:, k8 * TC:(k8 + 1) * TC],
                                        start=(e == 0 and k8 == 0),
                                        stop=(e == NE - 1 and k8 == 7),
                                        skip_group_check=True)
                        t_emm[ge] = kb.S("T", emm, deps=[
                            t_xnm[ge], t_ldex,
                            (t_moev.get(gi - 1)) if e == 0 else None])

                    def moev(h):
                        for ot in range(8):
                            h.tensor_copy(pc_out[:, ot * TC:(ot + 1) * TC],
                                          PS[ot][:, 0:512])
                    t_moev[gi] = kb.S("V", moev, deps=[t_emm[gi * 8 + 7],
                                                       t_moesp.get(gi - 1)])

                    def moesp(h, seq=seq, c0=c0):
                        h.dma_start(
                            out=moe_fm.ap()[seq, :, :, c0:c0 + TC]
                            .rearrange("k p t -> p k t"),
                            in_=pc_out[:].rearrange("p (k t) -> p k t", t=TC))
                    t_moesp[gi] = kb.S("Y", moesp, deps=[t_moev[gi]])

        def pass_f1(d):
            kb.barrier()
            with ExitStack() as _es:
                def AL(nm, shape, dt):
                    return _es.enter_context(nc.sbuf_tensor(nm + d, shape, dt))
                f1_inw = AL("f1_inw", [128, 8 * 4096], BF16)
                f1_xw = AL("f1_xw", [128, 16 * 96], BF16)
                f1_dtw = AL("f1_dtw", [64, 16 * 128], BF16)
                f1_xn0 = AL("f1_xn0", [128, 8 * TC], BF16)
                f1_xn1 = AL("f1_xn1", [128, 8 * TC], BF16)
                f1_xi = AL("f1_xi", [128, 16 * 515], BF16)
                f1_ca = AL("f1_ca", [128, 16 * TC], BF16)
                f1_xc = AL("f1_xc", [128, 16 * TC], BF16)
                f1_sz = AL("f1_sz", [128, 16 * TC], BF16)
                f1_xdbl = AL("f1_xdbl", [96, TC], BF16)
                f1_dtt = AL("f1_dtt", [128, 16 * TC], F32)
                f1_u = AL("f1_u", [128, 16 * TC], BF16)
                XN = [f1_xn0, f1_xn1]

                def ldw(h, d=d):
                    h.dma_start(out=f1_inw[:].rearrange("p (k f) -> p k f", f=4096),
                                in_=inwT[d].ap().rearrange("(k p) f -> p k f", p=128))
                    h.dma_start(out=f1_xw[:].rearrange("p (k f) -> p k f", f=96),
                                in_=xwT[d].ap().rearrange("(k p) f -> p k f", p=128))
                    h.dma_start(out=f1_dtw[:].rearrange("p (k f) -> p k f", f=128),
                                in_=dtwT[d].ap().rearrange("r (k p) -> r k p", p=128))
                t_ldw = kb.S("G", ldw)

                fwd = (d == "f")
                t_ldxn = {}
                t_mm31 = {}
                t_csilu = {}
                t_zev = {}
                t_xpmm = None
                t_xdblev = None
                t_bcspill = None
                t_dt15 = None
                t_sig = {}
                t_clamp15 = None
                t_spills = None
                ci = 0
                for seq in range(NSEQ):
                    def ztail(h):
                        if fwd:
                            h.memset(f1_xi[:].rearrange("p (k c) -> p k c", c=515)
                                     [:, :, 0:3], 0.0)
                        else:
                            h.memset(f1_xi[:].rearrange("p (k c) -> p k c", c=515)
                                     [:, :, 512:515], 0.0)
                    kb.S("V", ztail, deps=None)

                    chs = range(NCH) if fwd else range(NCH - 1, -1, -1)
                    first = True
                    for ch in chs:
                        c0 = ch * TC
                        cp = ci % 2

                        def ldxn(h, seq=seq, c0=c0, cp=cp):
                            h.dma_start(
                                out=XN[cp][:].rearrange("p (k t) -> p k t", t=TC),
                                in_=xn_fm.ap()[seq, :, :, c0:c0 + TC]
                                .rearrange("k p t -> p k t"))
                        t_ldxn[ci] = kb.S("Y", ldxn, deps=[t_mm31.get(ci - 2)])

                        if not first:
                            def tails(h):
                                xiv = f1_xi[:].rearrange("p (k c) -> p k c", c=515)
                                if fwd:
                                    h.tensor_copy(xiv[:, :, 0:3], xiv[:, :, 512:515])
                                else:
                                    h.tensor_copy(xiv[:, :, 512:515], xiv[:, :, 0:3])
                            kb.S("V", tails, deps=[])
                        first = False

                        t_ev = {}
                        for f in range(32):
                            def ipmm(h, f=f, cp=cp):
                                for k8 in range(8):
                                    h.matmul(PS[f % 2][:, 0:512],
                                             f1_inw[:, k8 * 4096 + f * 128:
                                                    k8 * 4096 + (f + 1) * 128],
                                             XN[cp][:, k8 * TC:(k8 + 1) * TC],
                                             start=(k8 == 0), stop=(k8 == 7),
                                             skip_group_check=True)
                            tmm = kb.S("T", ipmm, deps=[t_ldxn[ci], t_ldw,
                                                        t_ev.get(f - 2)])
                            if f == 31:
                                t_mm31[ci] = tmm

                            if f < 16:
                                def ipev(h, f=f):
                                    xiv = f1_xi[:].rearrange("p (k c) -> p k c", c=515)
                                    dst = xiv[:, f, 3:515] if fwd else xiv[:, f, 0:512]
                                    h.tensor_copy(dst, PS[f % 2][:, 0:512])
                                t_ev[f] = kb.S("V", ipev, deps=[tmm])
                            else:
                                def zev(h, f=f):
                                    h.tensor_copy(f1_sz[:, (f - 16) * TC:(f - 15) * TC],
                                                  PS[f % 2][:, 0:512])
                                t_ev[f] = kb.S("V", zev, deps=[tmm, t_spills])
                                t_zev[f - 16] = t_ev[f]

                        t_conv_last = None
                        for k in range(16):
                            def conv(h, k=k, d=d):
                                xiv = f1_xi[:].rearrange("p (k c) -> p k c", c=515)
                                cak = f1_ca[:, k * TC:(k + 1) * TC]
                                for j in range(4):
                                    off = j if fwd else (3 - j)
                                    src = xiv[:, k, off:off + 512]
                                    if j == 0:
                                        h.tensor_scalar(
                                            cak, src,
                                            cw_sb[d][:, k * 4 + j:k * 4 + j + 1],
                                            None, ALU.mult)
                                    else:
                                        h.tensor_scalar(
                                            pb_scratch[:], src,
                                            cw_sb[d][:, k * 4 + j:k * 4 + j + 1],
                                            None, ALU.mult)
                                        h.tensor_tensor(cak, cak,
                                                        pb_scratch[:], ALU.add)
                            t_conv_last = kb.S("V", conv, deps=[
                                t_csilu.get(0) if k == 0 else None,
                                t_spills if k == 0 else None])

                        def csilu_b(h, d=d):
                            for k in range(16):
                                h.activation(f1_xc[:, k * TC:(k + 1) * TC],
                                             f1_ca[:, k * TC:(k + 1) * TC], AF.Silu,
                                             bias=cb_sb[d][:, k:k + 1])
                            for k in range(16):
                                h.activation(f1_sz[:, k * TC:(k + 1) * TC],
                                             f1_sz[:, k * TC:(k + 1) * TC], AF.Silu)
                        t_csilu[0] = kb.S("S", csilu_b, deps=[
                            t_conv_last, t_ev[31], t_xpmm, t_spills])

                        def xpmm(h):
                            for k in range(16):
                                h.matmul(PS[2][0:96, 0:512],
                                         f1_xw[:, k * 96:(k + 1) * 96],
                                         f1_xc[:, k * TC:(k + 1) * TC],
                                         start=(k == 0), stop=(k == 15),
                                         skip_group_check=True)
                        t_xpmm = kb.S("T", xpmm, deps=[t_csilu[0], t_xdblev, t_ldw])

                        def xdblev(h):
                            h.tensor_copy(f1_xdbl[0:96, 0:TC], PS[2][0:96, 0:512])
                        t_xdblev = kb.S("V", xdblev, deps=[
                            t_xpmm, t_bcspill, t_dt15])

                        def bcspill(h, seq=seq, c0=c0, d=d):
                            h.dma_start(
                                out=bc_sp.ap()[seq, :, :, c0:c0 + TC]
                                .rearrange("c s t -> (c s) t"),
                                in_=f1_xdbl[64:96, 0:TC])
                        t_bcspill = kb.S("Y", bcspill, deps=[t_xdblev])

                        DTBANK = [3, 4, 5, 7]
                        t_dtmm = {}
                        for k in range(16):
                            def dtmm(h, k=k):
                                h.matmul(PS[DTBANK[k % 4]][:, 0:512],
                                         f1_dtw[:, k * 128:(k + 1) * 128],
                                         f1_xdbl[0:64, 0:TC],
                                         start=True, stop=True,
                                         skip_group_check=True)
                            t_dtmm[k] = kb.S("T", dtmm, deps=[
                                t_xdblev, t_sig.get(k // 4 - 1), t_ldw])
                            if k == 15:
                                t_dt15 = t_dtmm[k]

                            if k % 4 == 3:
                                def sig(h, k0=k - 3, d=d):
                                    for kk in range(k0, k0 + 4):
                                        h.activation(
                                            f1_dtt[:, kk * TC:(kk + 1) * TC],
                                            PS[DTBANK[kk % 4]][:, 0:512],
                                            AF.Sigmoid, scale=-1.0,
                                            bias=ndtb_sb[d][:, kk:kk + 1])
                                t_sig[k // 4] = kb.S("S", sig, deps=[
                                    t_dtmm[k], t_spills])

                        def lnb(h):
                            for k in range(16):
                                h.activation(f1_u[:, k * TC:(k + 1) * TC],
                                             f1_dtt[:, k * TC:(k + 1) * TC], AF.Ln)
                        t_lnb = kb.S("S", lnb, deps=[t_sig[3], t_spills])

                        def clamp(h):
                            h.tensor_scalar_max(f1_u[:], f1_u[:], U_CLAMP)
                            h.scalar_tensor_tensor(f1_ca[:], f1_u[:], -1.0,
                                                   f1_xc[:], ALU.mult, ALU.mult)
                        t_clamp15 = kb.S("V", clamp, deps=[t_lnb, t_csilu[0]])

                        def spills(h, seq=seq, c0=c0):
                            h.dma_start(
                                out=xc_sp.ap()[seq, :, :, c0:c0 + TC]
                                .rearrange("k p t -> p k t"),
                                in_=f1_xc[:].rearrange("p (k t) -> p k t", t=TC))
                            h.dma_start(
                                out=u_sp.ap()[seq, :, :, c0:c0 + TC]
                                .rearrange("k p t -> p k t"),
                                in_=f1_u[:].rearrange("p (k t) -> p k t", t=TC))
                            h.dma_start(
                                out=sz_sp.ap()[seq, :, :, c0:c0 + TC]
                                .rearrange("k p t -> p k t"),
                                in_=f1_sz[:].rearrange("p (k t) -> p k t", t=TC))
                            h.dma_start(
                                out=ut_sp.ap()[seq, :, :, c0:c0 + TC]
                                .rearrange("k p t -> p k t"),
                                in_=f1_ca[:].rearrange("p (k t) -> p k t", t=TC))
                        t_spills = kb.S("Y", spills, deps=[
                            t_clamp15, t_csilu[0]])
                        ci += 1

        def pass_f2(d):
            kb.barrier()
            fwd = (d == "f")
            W = PC + 2
            with ExitStack() as _es:
                def AL(nm, shape, dt):
                    return _es.enter_context(nc.sbuf_tensor(nm + d, shape, dt))
                f2_ow = AL("f2_ow", [128, 2 * 16 * 128], BF16)
                f2_bcB = AL("f2_bcB", [128, NS * PC], BF16)
                f2_bcC = AL("f2_bcC", [128, NS * PC], BF16)
                f2_a0 = AL("f2_a0", [128, SQ * W], BF16)
                f2_a1 = AL("f2_a1", [128, SQ * W], BF16)
                f2_r0 = AL("f2_r0", [128, SQ * W], BF16)
                f2_r1 = AL("f2_r1", [128, SQ * W], BF16)
                f2_u0 = AL("f2_u0", [128, PC], BF16)
                f2_u1 = AL("f2_u1", [128, PC], BF16)
                f2_u2 = AL("f2_u2", [128, PC], BF16)
                f2_u3 = AL("f2_u3", [128, PC], BF16)
                f2_xc0 = AL("f2_xc0", [128, PC], BF16)
                f2_xc1 = AL("f2_xc1", [128, PC], BF16)
                f2_xc2 = AL("f2_xc2", [128, PC], BF16)
                f2_xc3 = AL("f2_xc3", [128, PC], BF16)
                f2_sz0 = AL("f2_sz0", [128, PC], BF16)
                f2_sz1 = AL("f2_sz1", [128, PC], BF16)
                f2_sz2 = AL("f2_sz2", [128, PC], BF16)
                f2_sz3 = AL("f2_sz3", [128, PC], BF16)
                f2_ut0 = AL("f2_ut0", [128, PC], BF16)
                f2_ut1 = AL("f2_ut1", [128, PC], BF16)
                f2_tb = AL("f2_tb", [128, PC], BF16)
                f2_yf = AL("f2_yf", [128, 16 * PC], BF16)
                f2_yg = AL("f2_yg", [128, 16 * PC], BF16)
                f2_carry = AL("f2_carry", [128, 16 * NS], F32)
                f2_mx0 = AL("f2_mx0", [128, 512], BF16)
                f2_mx1 = AL("f2_mx1", [128, 512], BF16)

                AB = [f2_a0, f2_a1]
                RB = [f2_r0, f2_r1]
                UB = [f2_u0, f2_u1, f2_u2, f2_u3]
                XCB = [f2_xc0, f2_xc1, f2_xc2, f2_xc3]
                SZB = [f2_sz0, f2_sz1, f2_sz2, f2_sz3]
                UTB = [f2_ut0, f2_ut1]
                MXB = [f2_mx0, f2_mx1]
                YF = [f2_yf, f2_yg]

                def zspc(h):
                    for ab in (f2_a0, f2_a1):
                        v = ab[:].rearrange("p (s c) -> p s c", c=W)
                        h.memset(v[:, :, 0:1], 0.0)
                        h.memset(v[:, :, 1:2], 1.0)
                    for rb_ in (f2_r0, f2_r1):
                        v = rb_[:].rearrange("p (s c) -> p s c", c=W)
                        h.memset(v[:, :, 1:2], 0.0)
                t_zspc = kb.S("V", zspc)

                t_aact = {}
                t_V = {}
                t_ymm = {}
                t_eplg = {}
                t_ldt = {}
                lastV_par = {}
                lastV_q = {}
                lastY_par = {}
                last_aact_par = {}
                last_eplg_par = {}
                last_eplg_any = [None]
                t_owr = {}
                last_hc_q = {}

                OPB = [0, 1, 6, 7]
                gidx = [0]
                t_mm_g = {}
                t_opev_g = {}
                t_opsp_g = {}
                last_op_par = {}
                op_pend = [None]

                def emit_ldws(bat, ot):
                    op_ = ot % 2

                    def ldws(h, ot=ot, op_=op_, d=d):
                        h.dma_start(
                            out=f2_ow[:, op_ * 2048:(op_ + 1) * 2048]
                            .rearrange("p (k o) -> p k o", o=128),
                            in_=outwT[d].ap()[:, ot * 128:(ot + 1) * 128]
                            .rearrange("(k p) o -> p k o", p=128))
                    bat["ow"][ot] = kb.S("G", ldws, deps=[t_owr.get(op_)])

                def emit_opmm(bat, idx):
                    ot, hf = idx // 2, idx % 2
                    g = gidx[0]
                    gidx[0] += 1
                    bank = OPB[g % 4]
                    op_ = ot % 2

                    def opmm(h, bat=bat, hf=hf, op_=op_, bank=bank):
                        pp = bat["pp"]
                        for k in range(16):
                            h.matmul(
                                PS[bank][:, 0:512],
                                f2_ow[:, op_ * 2048 + k * 128:
                                      op_ * 2048 + (k + 1) * 128],
                                YF[pp][:, k * PC + hf * 512:
                                       k * PC + (hf + 1) * 512],
                                start=(k == 0), stop=(k == 15),
                                skip_group_check=True)
                    t_mm_g[g] = kb.S("T", opmm, deps=[
                        bat["t_ep"], bat["ow"][ot], t_opev_g.get(g - 4)])
                    bat["mm"][idx] = (g, bank)
                    if hf == 1:
                        t_owr[op_] = t_mm_g[g]
                    if idx == 15:
                        last_op_par[bat["pp"]] = t_mm_g[g]

                def emit_opev(bat, idx):
                    g, bank = bat["mm"][idx]
                    mxi = g % 2
                    ot, hf = idx // 2, idx % 2

                    def opev(h, mxi=mxi, bank=bank):
                        h.tensor_copy(MXB[mxi][:], PS[bank][:, 0:512])
                    t_opev_g[g] = kb.S("V", opev, deps=[
                        t_mm_g[g], t_opsp_g.get(g - 2)])

                    def opsp(h, bat=bat, ot=ot, hf=hf, mxi=mxi, d=d):
                        h.dma_start(
                            out=mix[d].ap()[bat["seq"], ot, :,
                                            bat["p0"] + hf * 512:
                                            bat["p0"] + (hf + 1) * 512],
                            in_=MXB[mxi][:])
                    t_opsp_g[g] = kb.S("Y", opsp, deps=[t_opev_g[g]])

                def op_step(k):
                    bat = op_pend[0]
                    if bat is None:
                        return
                    if k >= 4:
                        emit_opev(bat, k - 4)
                    emit_opmm(bat, k)
                    if k == 0:
                        emit_ldws(bat, 1)
                    elif k % 2 == 1 and (k + 3) // 2 <= 7:
                        emit_ldws(bat, (k + 3) // 2)

                def op_tail():
                    bat = op_pend[0]
                    if bat is None:
                        return
                    for idx in range(12, 16):
                        emit_opev(bat, idx)
                    op_pend[0] = None

                def op_drain():
                    bat = op_pend[0]
                    if bat is None:
                        return
                    for idx in range(16):
                        if idx >= 4:
                            emit_opev(bat, idx - 4)
                        emit_opmm(bat, idx)
                        if idx == 0:
                            emit_ldws(bat, 1)
                        elif idx % 2 == 1 and (idx + 3) // 2 <= 7:
                            emit_ldws(bat, (idx + 3) // 2)
                    for idx in range(12, 16):
                        emit_opev(bat, idx)
                    op_pend[0] = None

                for seq in range(NSEQ):
                    kb.S("V", lambda h: h.memset(f2_carry[:], 0.0))

                    prs = range(NPAIR) if fwd else range(NPAIR - 1, -1, -1)
                    for pi, p in enumerate(prs):
                        p0 = p * PC

                        t_bcq = {}
                        for qq in range(NSQ):
                            def ldbq(h, seq=seq, p0=p0, qq=qq):
                                s0_ = qq * SQ
                                for ci, dst_ in ((0, f2_bcB), (1, f2_bcC)):
                                    h.dma_start(
                                        out=dst_[:, s0_ * PC:(s0_ + SQ) * PC]
                                        .rearrange("p (s t) -> p s t", t=PC),
                                        in_=bc_sp.ap()[seq, ci, s0_:s0_ + SQ,
                                                       p0:p0 + PC]
                                        .rearrange("s t -> () s t")
                                        .broadcast_to((128, SQ, PC)))
                            t_bcq[qq] = kb.S("Y", ldbq, deps=[
                                lastV_q.get(qq), last_hc_q.get(qq)])

                        pend_eplg = [None]
                        pp = pi % 2

                        def emit_eplg(kk):
                            if kk is None:
                                return
                            kp2 = kk % 2

                            def eplg(h, k=kk, k4=kk % 4, d=d, pp=pp):
                                xcv = XCB[k4][:] if fwd else XCB[k4][:, ::-1]
                                szv = SZB[k4][:] if fwd else SZB[k4][:, ::-1]
                                h.tensor_scalar(f2_tb[:], xcv,
                                                dv_sb[d][:, k:k + 1], None, ALU.mult)
                                for hf in range(2):
                                    h.tensor_tensor(
                                        f2_tb[:, hf * 512:(hf + 1) * 512],
                                        PS[2 + hf][:, 0:512],
                                        f2_tb[:, hf * 512:(hf + 1) * 512], ALU.add)
                                h.tensor_tensor(
                                    YF[pp][:, k * PC:(k + 1) * PC],
                                    f2_tb[:], szv, ALU.mult)
                            t_eplg[kk] = kb.S("V", eplg, deps=[t_ymm[(kk, NSQ - 1)],
                                                               last_op_par.get(pp)])
                            last_eplg_par[kk % 4] = t_eplg[kk]
                            last_eplg_any[0] = t_eplg[kk]

                        for k in range(16):
                            kp = k % 2
                            k4 = k % 4

                            def ldt(h, seq=seq, p0=p0, k=k, k4=k4):
                                h.dma_start(out=UB[k4][:],
                                            in_=u_sp.ap()[seq, k, :, p0:p0 + PC])
                                h.dma_start(out=XCB[k4][:],
                                            in_=xc_sp.ap()[seq, k, :, p0:p0 + PC])
                                h.dma_start(out=SZB[k4][:],
                                            in_=sz_sp.ap()[seq, k, :, p0:p0 + PC])
                            t_ldt[k] = kb.S("G", ldt, deps=[
                                last_aact_par.get(k4), last_eplg_par.get(k4)])

                            def ldu(h, seq=seq, p0=p0, k=k, kp=kp):
                                h.dma_start(out=UTB[kp][:],
                                            in_=ut_sp.ap()[seq, k, :, p0:p0 + PC])
                            t_ldu = kb.S("G", ldu, deps=[
                                t_V.get(((k - 2) % 16, NSQ - 1))])

                            for q in range(NSQ):
                                qp = q % 2

                                def aact(h, k=k, q=q, k4=k4, qp=qp, d=d):
                                    for si in range(SQ):
                                        s = q * SQ + si
                                        srcv = (UB[k4][:] if fwd
                                                else UB[k4][:, ::-1])
                                        h.activation(
                                            AB[qp][:, si * W + 2:(si + 1) * W],
                                            srcv, AF.Exp,
                                            scale=expA[d][:, k * NS + s:
                                                          k * NS + s + 1])
                                t_aact[(k, q)] = kb.S("S", aact, deps=[
                                    t_ldt[k], lastV_par.get(qp) or t_zspc])
                                if q == NSQ - 1:
                                    last_aact_par[k4] = t_aact[(k, q)]

                                def bigV(h, k=k, q=q, kp=kp, qp=qp):
                                    utv = (UTB[kp][:] if fwd else UTB[kp][:, ::-1]) \
                                        .rearrange("p (o t) -> p o t", o=1) \
                                        .broadcast_to((128, SQ, PC))
                                    bv = f2_bcB[:].rearrange("p (s t) -> p s t", t=PC)
                                    bv = bv[:, q * SQ:(q + 1) * SQ, :]
                                    if not fwd:
                                        bv = bv[:, :, ::-1]
                                    rv = RB[qp][:].rearrange("p (s c) -> p s c", c=W)
                                    h.tensor_tensor(rv[:, :, 2:W], utv, bv, ALU.mult)
                                    h.memset(rv[:, :, 1:2], 0.0)
                                    h.tensor_copy(
                                        rv[:, :, 0:1].rearrange("p s o -> p (s o)"),
                                        f2_carry[:, k * NS + q * SQ:
                                                 k * NS + (q + 1) * SQ])
                                    h.fence()
                                    h.tensor_tensor_scan(
                                        RB[qp][:], AB[qp][:], RB[qp][:], 0.0,
                                        ALU.mult, ALU.add)
                                    h.fence()
                                    h.tensor_copy(
                                        f2_carry[:, k * NS + q * SQ:
                                                 k * NS + (q + 1) * SQ],
                                        rv[:, :, W - 1:W].rearrange("p s o -> p (s o)"))
                                t_V[(k, q)] = kb.S("V", bigV, deps=[
                                    t_aact[(k, q)], lastY_par.get(qp),
                                    t_bcq[q], t_ldu])
                                lastV_par[qp] = t_V[(k, q)]
                                lastV_q[q] = t_V[(k, q)]
                                if q == 0 and pend_eplg[0] is not None:
                                    emit_eplg(pend_eplg[0])
                                    pend_eplg[0] = None

                                t_dmp = None
                                if ("f2_dbg" in dbg and d == "b" and seq == 0
                                        and pi == 0 and k == 0 and q == 0):
                                    def f2dmp0(h, qp=qp):
                                        h.dma_start(out=f2_dbg.ap()[0, :, 0:SQ * W],
                                                    in_=AB[qp][:])
                                        h.dma_start(out=f2_dbg.ap()[2, :, 0:SQ * W],
                                                    in_=RB[qp][:])
                                        h.dma_start(out=f2_dbg.ap()[3, :, 0:PC],
                                                    in_=UTB[0][:])
                                        h.dma_start(out=f2_dbg.ap()[3, :, PC:2 * PC],
                                                    in_=f2_bcB[:, 0:PC])
                                        h.dma_start(out=f2_dbg.ap()[3, :, 2 * PC:2 * PC + 256],
                                                    in_=f2_carry[:])
                                        h.dma_start(out=f2_dbg.ap()[3, :, 3 * PC:4 * PC],
                                                    in_=f2_bcC[:, 0:PC])
                                    t_dmp = kb.S("G", f2dmp0, deps=[t_V[(k, q)]])

                                def hcg(h, k=k, q=q, qp=qp):
                                    rv = RB[qp][:].rearrange("p (s c) -> p s c", c=W)
                                    cv = f2_bcC[:].rearrange("p (s t) -> p s t", t=PC)
                                    cv = cv[:, q * SQ:(q + 1) * SQ, :]
                                    if not fwd:
                                        cv = cv[:, :, ::-1]
                                    h.tensor_tensor(rv[:, :, 2:W], rv[:, :, 2:W],
                                                    cv, ALU.mult)
                                t_hc = kb.S("V", hcg, deps=[
                                    t_V[(k, q)], t_dmp, t_bcq[q]])
                                last_hc_q[q] = t_hc

                                def ymm(h, k=k, q=q, qp=qp):
                                    for si in range(SQ):
                                        h.matmul(PS[2][:, 0:512], ident[:],
                                                 RB[qp][:, si * W + 2:si * W + 514],
                                                 start=(q == 0 and si == 0),
                                                 stop=(q == NSQ - 1 and si == SQ - 1),
                                                 skip_group_check=True)
                                        h.matmul(PS[3][:, 0:512], ident[:],
                                                 RB[qp][:, si * W + 514:si * W + 1026],
                                                 start=(q == 0 and si == 0),
                                                 stop=(q == NSQ - 1 and si == SQ - 1),
                                                 skip_group_check=True)
                                t_ymm[(k, q)] = kb.S("T", ymm, deps=[
                                    t_hc,
                                    last_eplg_any[0] if q == 0 else None])
                                lastY_par[qp] = t_ymm[(k, q)]

                            op_step(k)
                            pend_eplg[0] = k

                        emit_eplg(pend_eplg[0])
                        pend_eplg[0] = None
                        op_tail()

                        nb = {"seq": seq, "p0": p0, "pp": pp,
                              "t_ep": t_eplg[15], "ow": {}, "mm": {}}
                        emit_ldws(nb, 0)
                        op_pend[0] = nb
                    op_drain()
                    kb.S("V", lambda h: h.memset(f2_carry[0:1, 0:1], 0.0),
                         deps=[t_opsp_g.get(gidx[0] - 1),
                               t_opsp_g.get(gidx[0] - 2)])

        def pass_c1():
            kb.barrier()
            with ExitStack() as _es:
                def AL(nm, shape, dt):
                    return _es.enter_context(nc.sbuf_tensor(nm, shape, dt))
                c1_w1 = AL("c1_w1", [128, 8 * 4096], BF16)
                c1_w2a = AL("c1_w2a", [128, 32 * 128], BF16)
                c1_w2b = AL("c1_w2b", [128, 32 * 128], BF16)
                c1_xn = AL("c1_xn", [128, 8 * TC], BF16)
                c1_yf = AL("c1_yf", [128, 8 * TC], BF16)
                c1_yb = AL("c1_yb", [128, 8 * TC], BF16)
                c1_sqt = AL("c1_sqt", [128, TC], BF16)
                c1_xln = AL("c1_xln", [128, 8 * TC], BF16)
                c1_h1 = AL("c1_h1", [128, 32 * TC], BF16)
                c1_h1r = AL("c1_h1r", [128, 32 * TC], BF16)
                c1_st = AL("c1_st", [1, 5 * TC], F32)
                c1_mb = AL("c1_mb", [128, 2 * TC], F32)
                c1_ffh = AL("c1_ffh", [128, 8 * TC], BF16)
                W2 = [c1_w2a, c1_w2b]

                def ldw(h):
                    h.dma_start(out=c1_w1[:].rearrange("p (k f) -> p k f", f=4096),
                                in_=ff1T.ap().rearrange("(k p) f -> p k f", p=128))
                t_ldwc1 = kb.S("G", ldw)

                t_lds = None
                t_x2c = None
                t_stat1 = None
                t_sq7 = None
                t_lnap = None
                t_spl = None
                t_f2mm = {}
                t_f2ev = {}
                t_f1ev = {}
                t_ldw2 = {}
                t_mbev = None
                t_silu_b = None
                for seq in range(NSEQ):
                    for ch in range(NCH):
                        c0 = ch * TC
                        p = ch // 2
                        rev0 = p * PC + (512 if ch % 2 == 0 else 0)

                        def lds(h, seq=seq, c0=c0, rev0=rev0):
                            h.dma_start(
                                out=c1_xn[:].rearrange("p (k t) -> p k t", t=TC),
                                in_=xn_fm.ap()[seq, :, :, c0:c0 + TC]
                                .rearrange("k p t -> p k t"))
                            h.dma_start(
                                out=c1_yf[:].rearrange("p (k t) -> p k t", t=TC),
                                in_=mix["f"].ap()[seq, :, :, c0:c0 + TC]
                                .rearrange("k p t -> p k t"))
                            h.dma_start(
                                out=c1_yb[:].rearrange("p (k t) -> p k t", t=TC),
                                in_=mix["b"].ap()[seq, :, :, rev0:rev0 + TC]
                                .rearrange("k p t -> p k t"))
                        t_lds = kb.S("Y", lds, deps=[t_spl, t_lnap, t_stat1, t_sq7])

                        def x2c(h):
                            yfv = c1_yf[:].rearrange("p (k t) -> p k t", t=TC)
                            ybv = c1_yb[:].rearrange("p (k t) -> p k t", t=TC)[:, :, ::-1]
                            h.tensor_tensor(yfv, yfv, ybv, ALU.add)
                            h.tensor_scalar_mul(c1_yb[:], c1_yf[:], 0.5)
                            h.tensor_tensor(c1_yb[:], c1_yb[:], c1_xn[:], ALU.add)
                        t_x2c = kb.S("V", x2c, deps=[t_lds])
                        c1_x2 = c1_yb

                        def statmm1(h):
                            for k8 in range(8):
                                h.matmul(PS[4][0:1, 0:512], ones128b[:],
                                         c1_x2[:, k8 * TC:(k8 + 1) * TC],
                                         start=(k8 == 0), stop=(k8 == 7),
                                         skip_group_check=True)
                        t_stat1 = kb.S("T", statmm1, deps=[t_x2c, t_mbev])

                        t_sq = None
                        t_stat2 = None
                        for k8 in range(8):
                            def sqk(h, k8=k8):
                                h.activation(c1_sqt[:], c1_x2[:, k8 * TC:(k8 + 1) * TC],
                                             AF.Square)
                            t_sq = kb.S("S", sqk, deps=[t_x2c, t_stat2])

                            def statmm2(h, k8=k8):
                                h.matmul(PS[5][0:1, 0:512], ones128b[:], c1_sqt[:],
                                         start=(k8 == 0), stop=(k8 == 7),
                                         skip_group_check=True)
                            t_stat2 = kb.S("T", statmm2, deps=[t_sq, t_mbev])
                        t_sq7 = t_sq

                        def statv(h):
                            h.tensor_scalar_mul(c1_st[0:1, 0:TC], PS[4][0:1, 0:512],
                                                1.0 / D)
                            h.tensor_scalar_mul(c1_st[0:1, TC:2 * TC],
                                                PS[5][0:1, 0:512], 1.0 / D)
                            h.fence()
                            h.tensor_tensor(c1_st[0:1, 2 * TC:3 * TC],
                                            c1_st[0:1, 0:TC], c1_st[0:1, 0:TC],
                                            ALU.mult)
                            h.fence()
                            h.tensor_tensor(c1_st[0:1, 2 * TC:3 * TC],
                                            c1_st[0:1, TC:2 * TC],
                                            c1_st[0:1, 2 * TC:3 * TC], ALU.subtract)
                            h.fence()
                            h.tensor_scalar_add(c1_st[0:1, 2 * TC:3 * TC],
                                                c1_st[0:1, 2 * TC:3 * TC], LN_EPS)
                        t_sv = kb.S("V", statv, deps=[t_stat1, t_stat2])

                        t_sqr = kb.S("S", lambda h: h.activation(
                            c1_st[0:1, 3 * TC:4 * TC],
                            c1_st[0:1, 2 * TC:3 * TC], AF.Sqrt), deps=[t_sv])
                        t_rc = kb.S("V", lambda h: h.reciprocal(
                            c1_st[0:1, 4 * TC:5 * TC],
                            c1_st[0:1, 3 * TC:4 * TC]), deps=[t_sqr])

                        def mbmm(h):
                            h.matmul(PS[4][:, 0:512], ones1f[:], c1_st[0:1, 0:TC],
                                     start=True, stop=True, skip_group_check=True)
                            h.matmul(PS[5][:, 0:512], ones1f[:],
                                     c1_st[0:1, 4 * TC:5 * TC],
                                     start=True, stop=True, skip_group_check=True)
                        t_mbm = kb.S("T", mbmm, deps=[t_rc])

                        def mbev(h):
                            h.copy(c1_mb[:, 0:TC], PS[4][:, 0:512])
                            h.copy(c1_mb[:, TC:2 * TC], PS[5][:, 0:512])
                        t_mbev = kb.S("S", mbev, deps=[t_mbm, t_lnap])

                        def lnap(h):
                            for k8 in range(8):
                                sl = slice(k8 * TC, (k8 + 1) * TC)
                                h.tensor_tensor(c1_xln[:, sl], c1_x2[:, sl],
                                                c1_mb[:, 0:TC], ALU.subtract)
                                h.tensor_tensor(c1_xln[:, sl], c1_xln[:, sl],
                                                c1_mb[:, TC:2 * TC], ALU.mult)
                                h.tensor_scalar(c1_xln[:, sl], c1_xln[:, sl],
                                                lng_sb[:, k8:k8 + 1],
                                                lnb_sb[:, k8:k8 + 1],
                                                ALU.mult, ALU.add)
                        t_lnap = kb.S("V", lnap, deps=[t_mbev, t_f1ev.get(31)])

                        for f in range(32):
                            def f1mm(h, f=f):
                                for k8 in range(8):
                                    h.matmul(PS[f % 2][:, 0:512],
                                             c1_w1[:, k8 * 4096 + f * 128:
                                                   k8 * 4096 + (f + 1) * 128],
                                             c1_xln[:, k8 * TC:(k8 + 1) * TC],
                                             start=(k8 == 0), stop=(k8 == 7),
                                             skip_group_check=True)
                            tfm = kb.S("T", f1mm, deps=[t_lnap, t_f1ev.get(f - 2),
                                                        t_ldwc1])

                            def f1ev(h, f=f):
                                h.tensor_copy(c1_h1r[:, f * TC:(f + 1) * TC],
                                              PS[f % 2][:, 0:512])
                            t_f1ev[f] = kb.S("V", f1ev, deps=[
                                tfm, t_silu_b])

                        def silu_b(h):
                            for f in range(32):
                                h.activation(c1_h1[:, f * TC:(f + 1) * TC],
                                             c1_h1r[:, f * TC:(f + 1) * TC], AF.Silu,
                                             bias=ffb1_sb[:, f:f + 1])
                        t_silu_b = kb.S("S", silu_b, deps=[t_f1ev[31],
                                                           t_f2mm.get(7)])

                        for ot in range(8):
                            op_ = ot % 2

                            def ldw2(h, ot=ot, op_=op_):
                                h.dma_start(
                                    out=W2[op_][:].rearrange("p (k o) -> p k o", o=128),
                                    in_=ff2T.ap()[:, ot * 128:(ot + 1) * 128]
                                    .rearrange("(k p) o -> p k o", p=128))
                            t_ldw2[op_] = kb.S("G", ldw2, deps=[t_f2mm.get(
                                7 if ot < 2 else ot - 2)])

                            def f2mm(h, ot=ot, op_=op_):
                                for k in range(32):
                                    h.matmul(PS[2][:, 0:512],
                                             W2[op_][:, k * 128:(k + 1) * 128],
                                             c1_h1[:, k * TC:(k + 1) * TC],
                                             start=(k == 0), stop=(k == 31),
                                             skip_group_check=True)
                            t_f2mm[ot] = kb.S("T", f2mm, deps=[
                                t_silu_b, t_ldw2[op_], t_f2ev.get(ot - 1)])

                            def f2ev(h, ot=ot):
                                h.tensor_scalar(c1_ffh[:, ot * TC:(ot + 1) * TC],
                                                PS[2][:, 0:512],
                                                ffb2_sb[:, ot:ot + 1], None, ALU.add)
                            t_f2ev[ot] = kb.S("V", f2ev, deps=[t_f2mm[ot], t_spl])

                        def spl(h, seq=seq, c0=c0):
                            h.dma_start(
                                out=x2_sp.ap()[seq, :, :, c0:c0 + TC]
                                .rearrange("k p t -> p k t"),
                                in_=c1_x2[:].rearrange("p (k t) -> p k t", t=TC))
                            h.dma_start(
                                out=ffh_sp.ap()[seq, :, :, c0:c0 + TC]
                                .rearrange("k p t -> p k t"),
                                in_=c1_ffh[:].rearrange("p (k t) -> p k t", t=TC))
                        t_spl = kb.S("Y", spl, deps=[t_f2ev[7]])

        def pass_c2():
            kb.barrier()
            with nc.sbuf_tensor("c2_x2", [128, 8 * TC], BF16) as c2_x2, \
                 nc.sbuf_tensor("c2_ffh", [128, 8 * TC], BF16) as c2_ffh, \
                 nc.sbuf_tensor("c2_moe", [128, 8 * TC], BF16) as c2_moe, \
                 nc.sbuf_tensor("c2_fs", [128, 8 * TC], F32) as c2_fs, \
                 nc.sbuf_tensor("c2_tm", [128, 4 * D], F32) as c2_tm, \
                 nc.sbuf_tensor("c2_x", [128, 4 * D], F32) as c2_x:
                for seq in range(NSEQ):
                    for ch in range(NCH):
                        c0 = ch * TC

                        def lds(h, seq=seq, c0=c0):
                            h.dma_start(
                                out=c2_x2[:].rearrange("p (k t) -> p k t", t=TC),
                                in_=x2_sp.ap()[seq, :, :, c0:c0 + TC]
                                .rearrange("k p t -> p k t"))
                            h.dma_start(
                                out=c2_ffh[:].rearrange("p (k t) -> p k t", t=TC),
                                in_=ffh_sp.ap()[seq, :, :, c0:c0 + TC]
                                .rearrange("k p t -> p k t"))
                            h.dma_start(
                                out=c2_moe[:].rearrange("p (k t) -> p k t", t=TC),
                                in_=moe_fm.ap()[seq, :, :, c0:c0 + TC]
                                .rearrange("k p t -> p k t"))
                            h.dma_start(
                                out=c2_x[:].rearrange("p (g d) -> p g d", d=D),
                                in_=x_in.ap()[seq, c0:c0 + TC, :]
                                .rearrange("(g p) d -> p g d", p=128))
                        kb.S("Y", lds)

                        def fsum(h):
                            h.tensor_tensor(c2_fs[:], c2_x2[:], c2_moe[:], ALU.add)
                            h.tensor_tensor(c2_fs[:], c2_fs[:], c2_ffh[:], ALU.add)
                        kb.S("V", fsum)

                        for g in range(4):
                            def tp(h, g=g):
                                for dt8 in range(8):
                                    bank = 4 + dt8 // 4
                                    h.transpose(
                                        PS[bank][:, (dt8 % 4) * 128:
                                                 (dt8 % 4 + 1) * 128],
                                        c2_fs[:, dt8 * TC + g * 128:
                                              dt8 * TC + (g + 1) * 128],
                                        ident32[:])
                            kb.S("T", tp)

                            def tev(h, g=g):
                                h.tensor_copy(c2_tm[:, g * D:g * D + 512],
                                              PS[4][:, 0:512])
                                h.tensor_copy(c2_tm[:, g * D + 512:(g + 1) * D],
                                              PS[5][:, 0:512])
                            kb.S("V", tev)

                        def addx(h):
                            h.tensor_tensor(c2_tm[:], c2_tm[:], c2_x[:], ALU.add)
                        kb.S("V", addx)

                        def sto(h, seq=seq, c0=c0):
                            h.dma_start(
                                out=out_t.ap()[seq, c0:c0 + TC, :]
                                .rearrange("(g p) d -> p g d", p=128),
                                in_=c2_tm[:].rearrange("p (g d) -> p g d", d=D))
                        kb.S("Y", sto)

        pb_scratch = sb(nc.sbuf_tensor("convtmp", [128, TC], BF16))

        if "f1f" in passes:
            pass_f1("f")
        if "f2f" in passes:
            pass_f2("f")
        if "f1b" in passes:
            pass_f1("b")
        if "f2b" in passes:
            pass_f2("b")
        if "c1" in passes:
            pass_c1()
        if "c2" in passes:
            pass_c2()

        kb.final_wait()
        kb.st.close()

    return nc


def _host_stage(inputs):
    def T2(a):
        return np.ascontiguousarray(np.asarray(a).T)
    g = inputs
    ins = {}
    for d, p in (("f", "fwd"), ("b", "bwd")):
        ins[f"inwT_{d}"] = T2(g[f"L0_attn_{p}_in_w"])
        ins[f"xwT_{d}"] = T2(g[f"L0_attn_{p}_x_w"])
        ins[f"dtwT_{d}"] = np.ascontiguousarray(np.asarray(g[f"L0_attn_{p}_dt_w"]).T)
        ins[f"outwT_{d}"] = T2(g[f"L0_attn_{p}_out_w"])
        ins[f"convw_{d}"] = np.ascontiguousarray(g[f"L0_attn_{p}_conv_w"])
        ins[f"convb_{d}"] = np.ascontiguousarray(g[f"L0_attn_{p}_conv_b"])
        ins[f"dtb_{d}"] = np.ascontiguousarray(g[f"L0_attn_{p}_dt_b"])
        ins[f"alog_{d}"] = np.ascontiguousarray(g[f"L0_attn_{p}_A_log"])
        ins[f"dvec_{d}"] = np.ascontiguousarray(g[f"L0_attn_{p}_D"])
    ins["ff1T"] = T2(g["L0_attn_ff_w1"])
    ins["ff2T"] = T2(g["L0_attn_ff_w2"])
    ins["ffb1"] = np.ascontiguousarray(g["L0_attn_ff_b1"])
    ins["ffb2"] = np.ascontiguousarray(g["L0_attn_ff_b2"])
    ins["lng"] = np.ascontiguousarray(g["L0_attn_ff_ln_g"])
    ins["lnb"] = np.ascontiguousarray(g["L0_attn_ff_ln_b"])
    ins["gw"] = np.ascontiguousarray(np.asarray(g["L0_ffn_gate_w"])[0])
    ins["experts"] = np.ascontiguousarray(g["L0_ffn_experts"])
    ins["ident"] = np.eye(128, dtype=bfloat16)
    ins["ident32"] = np.eye(128, dtype=np.float32)
    ins["ident8"] = np.eye(8, dtype=np.float32)
    s8 = np.zeros((8, 8 * 128), dtype=np.float32)
    for e in range(8):
        s8[e, e * 128:(e + 1) * 128] = 1.0
    ins["sel8"] = s8
    ins["sel8b"] = s8.astype(bfloat16)
    s32 = np.zeros((32, 32 * 128), dtype=np.float32)
    for s in range(32):
        s32[s, s * 128:(s + 1) * 128] = 1.0
    ins["sel32b"] = s32.astype(bfloat16)
    ins["nsel32b"] = (-s32).astype(bfloat16)
    return ins


def kernel(**inputs):
    common = _host_stage(inputs)
    x = np.asarray(inputs["L0_attn_x"], dtype=np.float32)
    nc = build_program()
    in_maps = []
    for c in range(NCORES):
        m = dict(common)
        m["x"] = np.ascontiguousarray(x[2 * c:2 * c + 2])
        in_maps.append(m)
    res = run_bass_kernel_spmd(nc, in_maps, core_ids=list(range(NCORES)))
    return np.concatenate([r["out"] for r in res.results], axis=0)


# version 19 (final)
# speedup vs baseline: 1.0016x; 1.0016x over previous
import numpy as np
import ml_dtypes
import concourse.bass as bass
import concourse.mybir as mybir
from concourse.bass_utils import run_bass_kernel_spmd
from contextlib import ExitStack

bfloat16 = ml_dtypes.bfloat16
F32 = mybir.dt.float32
BF16 = mybir.dt.bfloat16
AF = mybir.ActivationFunctionType
ALU = mybir.AluOpType
AXX = mybir.AxisListType.X

T = 4096
D = 1024
DI = 2048
NS = 16
NE = 8
NSEQ = 2
NCORES = 8
TC = 512
PC = 1024
NCH = T // TC
NPAIR = T // PC
SQ = 4
NSQ = NS // SQ
LN_EPS = 1e-5
U_CLAMP = -80.0

PASS_ORDER = ["pre_a", "pre_b", "pre_c", "f1f", "f2f", "f1b", "f2b", "c1", "c2"]


class HW:

    def __init__(self, h, kb, eng):
        self._h = h
        self._kb = kb
        self._eng = eng
        self.last = None

    def fence(self):
        if self.last is None:
            return
        kb, eng = self._kb, self._eng
        kb.mark(eng, self.last)
        sem = kb._msem[eng]
        self._h.wait_ge(kb.sems[sem], kb.cnt[sem])
        kb.hi[eng][sem] = kb.cnt[sem]
        self.last = None

    def __getattr__(self, name):
        if name in ("_h", "_kb", "_eng", "last"):
            raise AttributeError(name)
        f = getattr(self._h, name)
        if not callable(f):
            return f
        kb, eng = self._kb, self._eng

        def g(*a, **k):
            r = f(*a, **k)
            if hasattr(r, "then_inc"):
                if name.startswith("dma_start"):
                    kb.dmark(eng, r)
                else:
                    self.last = r
            return r
        return g


class KB:
    def __init__(self, nc, block):
        self.nc = nc
        self.block = block
        self.st = ExitStack()
        self.sems = {}
        for name in ("mT", "mV", "mS", "mG", "dY", "dG"):
            self.sems[name] = self.st.enter_context(nc.semaphore(name))
        self.cnt = {k: 0 for k in self.sems}
        self.hi = {e: {} for e in ("T", "V", "S", "G", "Y")}
        self._m = {"T": block.tensor, "V": block.vector, "S": block.scalar,
                   "G": block.gpsimd, "Y": block.sync}
        self._msem = {"T": "mT", "V": "mV", "S": "mS", "G": "mG"}
        self._dsem = {"Y": "dY", "G": "dG", "S": "dY", "V": "dY", "T": "dY"}
        self.nsect = 0

    def dmark(self, eng, ins):
        sem = self._dsem[eng]
        self.cnt[sem] += 16
        ins.then_inc(self.sems[sem], 16)

    def mark(self, eng, ins):
        sem = self._msem[eng]
        self.cnt[sem] += 1
        ins.then_inc(self.sems[sem], 1)

    def S(self, eng, fn, deps=None):
        if deps is None:
            toks = [(s, self.cnt[s]) for s in ("mT", "mV", "mS", "mG", "dY", "dG")]
        else:
            flat = []
            for t in deps:
                if t is None:
                    continue
                if isinstance(t, list):
                    flat.extend(x for x in t if x is not None)
                else:
                    flat.append(t)
            best = {}
            for sem, val in flat:
                best[sem] = max(best.get(sem, 0), val)
            toks = sorted(best.items())
        self.nsect += 1

        def wrapper(h):
            for sem, val in toks:
                if val > 0 and self.hi[eng].get(sem, 0) < val:
                    h.wait_ge(self.sems[sem], val)
                    self.hi[eng][sem] = val
            hw = HW(h, self, eng)
            fn(hw)
            if hw.last is not None and eng in self._msem:
                self.mark(eng, hw.last)
        before = dict(self.cnt)
        self._m[eng](wrapper)
        out = []
        if eng in self._msem:
            s = self._msem[eng]
            if self.cnt[s] > before[s]:
                out.append((s, self.cnt[s]))
        s = self._dsem.get(eng)
        if s is not None and self.cnt[s] > before[s]:
            out.append((s, self.cnt[s]))
        if len(out) == 1:
            return out[0]
        return out if out else None

    def barrier(self):
        toks = [(s, self.cnt[s]) for s in ("mT", "mV", "mS", "mG", "dY", "dG")]
        for eng in ("T", "V", "S", "G", "Y"):
            def fn(h, _t=toks):
                for sem, val in _t:
                    if val > 0 and self.hi[eng].get(sem, 0) < val:
                        h.wait_ge(self.sems[sem], val)
                        self.hi[eng][sem] = val
            self._m[eng](fn)

    def final_wait(self):
        toks = [(s, self.cnt[s]) for s in ("mT", "mV", "mS", "mG", "dY", "dG")]
        for eng in ("T", "V", "S", "G", "Y"):
            def fn(h, _t=toks):
                for sem, val in _t:
                    if val > 0 and self.hi[eng].get(sem, 0) < val:
                        h.wait_ge(self.sems[sem], val)
                        self.hi[eng][sem] = val
            self._m[eng](fn)


def build_program(dbg=(), upto=None):
    nc = bass.Bass("TRN2", target_bir_lowering=False, debug=False)
    passes = PASS_ORDER if upto is None else PASS_ORDER[:PASS_ORDER.index(upto) + 1]

    def din(name, shape, dt=F32):
        return nc.dram_tensor(name, shape, dt, kind="ExternalInput")

    x_in = din("x", [NSEQ, T, D])
    inwT = {d: din(f"inwT_{d}", [D, 2 * DI]) for d in "fb"}
    xwT = {d: din(f"xwT_{d}", [DI, 96]) for d in "fb"}
    dtwT = {d: din(f"dtwT_{d}", [64, DI]) for d in "fb"}
    outwT = {d: din(f"outwT_{d}", [DI, D]) for d in "fb"}
    convw = {d: din(f"convw_{d}", [DI, 4]) for d in "fb"}
    convb = {d: din(f"convb_{d}", [DI]) for d in "fb"}
    dtb = {d: din(f"dtb_{d}", [DI]) for d in "fb"}
    alog = {d: din(f"alog_{d}", [DI, NS]) for d in "fb"}
    dvec = {d: din(f"dvec_{d}", [DI]) for d in "fb"}
    ff1T = din("ff1T", [D, 4 * D])
    ff2T = din("ff2T", [4 * D, D])
    ffb1_t = din("ffb1", [4 * D])
    ffb2_t = din("ffb2", [D])
    lng_t = din("lng", [D])
    lnb_t = din("lnb", [D])
    gw_t = din("gw", [D, NE])
    experts_t = din("experts", [NE, D, D])
    ident_in = din("ident", [128, 128], BF16)
    ident32_in = din("ident32", [128, 128], F32)
    ident8_in = din("ident8", [8, 8], F32)
    sel8_in = din("sel8", [8, 8 * 128], F32)
    sel8b_in = din("sel8b", [8, 8 * 128], BF16)
    sel32b_in = din("sel32b", [32, 32 * 128], BF16)
    nsel32b_in = din("nsel32b", [32, 32 * 128], BF16)

    out_t = nc.dram_tensor("out", [NSEQ, T, D], F32, kind="ExternalOutput")

    def scratch(name, shape, dt):
        kind = "ExternalOutput" if name in dbg else "Internal"
        return nc.dram_tensor(name, shape, dt, kind=kind)

    xn_fm = scratch("xn_fm", [NSEQ, 8, 128, T], BF16)
    moe_fm = scratch("moe_fm", [NSEQ, 8, 128, T], BF16)
    xc_sp = scratch("xc_sp", [NSEQ, 16, 128, T], BF16)
    u_sp = scratch("u_sp", [NSEQ, 16, 128, T], BF16)
    sz_sp = scratch("sz_sp", [NSEQ, 16, 128, T], BF16)
    bc_sp = scratch("bc_sp", [NSEQ, 2, NS, T], BF16)
    mix = {d: scratch(f"mix_{d}", [NSEQ, 8, 128, T], BF16) for d in "fb"}
    wm_sp = scratch("wm_sp", [NSEQ, NE, T], BF16)
    st_dbg = scratch("st_dbg", [8, 128, 32], F32)
    f2_dbg = scratch("f2_dbg", [4, 128, 4 * 1028], F32)
    g_dbg = scratch("g_dbg", [NSEQ, NE, T], F32)
    wtm_dbg = scratch("wtm_dbg", [NSEQ, 128, 256], F32)
    cnt_dbg = scratch("cnt_dbg", [NSEQ, 128, 256], F32)
    x2_sp = scratch("x2_sp", [NSEQ, 8, 128, T], BF16)
    ffh_sp = scratch("ffh_sp", [NSEQ, 8, 128, T], BF16)

    with nc.Block() as block:
        kb = KB(nc, block)
        sb = kb.st.enter_context

        ident = sb(nc.sbuf_tensor("identt", [128, 128], BF16))
        ident32 = sb(nc.sbuf_tensor("identt32", [128, 128], F32))
        ident8 = sb(nc.sbuf_tensor("identt8", [8, 8], F32))

        ones1b = sb(nc.sbuf_tensor("ones1b", [1, 128], BF16))
        nones1b = sb(nc.sbuf_tensor("nones1b", [1, 128], BF16))
        ones1f = sb(nc.sbuf_tensor("ones1f", [1, 128], F32))
        ones8f = sb(nc.sbuf_tensor("ones8f", [8, 8], F32))
        ones128b = sb(nc.sbuf_tensor("ones128b", [128, 1], BF16))
        econst = sb(nc.sbuf_tensor("econst", [128, NE], F32))
        expA = {d: sb(nc.sbuf_tensor(f"expA{d}", [128, 16 * NS], F32)) for d in "fb"}
        cw_sb = {d: sb(nc.sbuf_tensor(f"cw{d}", [128, 16 * 4], F32)) for d in "fb"}
        cb_sb = {d: sb(nc.sbuf_tensor(f"cb{d}", [128, 16], F32)) for d in "fb"}
        ndtb_sb = {d: sb(nc.sbuf_tensor(f"ndtb{d}", [128, 16], F32)) for d in "fb"}
        dv_sb = {d: sb(nc.sbuf_tensor(f"dv{d}", [128, 16], F32)) for d in "fb"}
        ffb1_sb = sb(nc.sbuf_tensor("ffb1s", [128, 32], F32))
        ffb2_sb = sb(nc.sbuf_tensor("ffb2s", [128, 8], F32))
        lng_sb = sb(nc.sbuf_tensor("lngs", [128, 8], F32))
        lnb_sb = sb(nc.sbuf_tensor("lnbs", [128, 8], F32))
        gw_sb = sb(nc.sbuf_tensor("gws", [128, 8 * NE], BF16))

        PS = [sb(nc.psum_tensor(f"ps{i}", [128, 512], F32)) for i in range(8)]

        def setup_g(h):
            for d in "fb":
                h.dma_start(out=expA[d][:].rearrange("p (k s) -> p k s", s=NS),
                            in_=alog[d].ap().rearrange("(k p) s -> p k s", p=128))
                h.dma_start(out=cw_sb[d][:].rearrange("p (k j) -> p k j", j=4),
                            in_=convw[d].ap().rearrange("(k p) j -> p k j", p=128))
                h.dma_start(out=cb_sb[d][:],
                            in_=convb[d].ap().rearrange("(k p) -> p k", p=128), allow_slow_non_contiguous=True)
                h.dma_start(out=ndtb_sb[d][:],
                            in_=dtb[d].ap().rearrange("(k p) -> p k", p=128), allow_slow_non_contiguous=True)
                h.dma_start(out=dv_sb[d][:],
                            in_=dvec[d].ap().rearrange("(k p) -> p k", p=128), allow_slow_non_contiguous=True)
            h.dma_start(out=ffb1_sb[:], in_=ffb1_t.ap().rearrange("(k p) -> p k", p=128), allow_slow_non_contiguous=True)
            h.dma_start(out=ffb2_sb[:], in_=ffb2_t.ap().rearrange("(k p) -> p k", p=128), allow_slow_non_contiguous=True)
            h.dma_start(out=lng_sb[:], in_=lng_t.ap().rearrange("(k p) -> p k", p=128), allow_slow_non_contiguous=True)
            h.dma_start(out=lnb_sb[:], in_=lnb_t.ap().rearrange("(k p) -> p k", p=128), allow_slow_non_contiguous=True)
            h.dma_start(out=gw_sb[:].rearrange("p (k e) -> p k e", e=NE),
                        in_=gw_t.ap().rearrange("(k p) e -> p k e", p=128))
            h.dma_start(out=ident[:], in_=ident_in.ap())
            h.dma_start(out=ident32[:], in_=ident32_in.ap())
            h.dma_start(out=ident8[:], in_=ident8_in.ap())

        kb.S("G", setup_g)

        def setup_v(h):
            h.memset(ones1b[:], 1.0)
            h.memset(nones1b[:], -1.0)
            h.memset(ones1f[:], 1.0)
            h.memset(ones8f[:], 1.0)
            h.memset(ones128b[:], 1.0)
            for e in range(NE):
                h.memset(econst[:, e:e + 1], float(e + 1))
            for d in "fb":
                h.tensor_scalar_mul(ndtb_sb[d][:], ndtb_sb[d][:], -1.0)
        kb.S("V", setup_v)

        def setup_s(h):
            for d in "fb":
                h.activation(expA[d][:], expA[d][:], AF.Exp)
        kb.S("S", setup_s)

        if "pre_a" in passes:
            with nc.sbuf_tensor("pa_glog", [NE, NSEQ * T], F32) as pa_glog:
                with nc.sbuf_tensor("pa_x0", [128, 4 * D], F32) as pa_x0, \
                     nc.sbuf_tensor("pa_x1", [128, 4 * D], F32) as pa_x1, \
                     nc.sbuf_tensor("pa_sq0", [128, 4 * D], F32) as pa_sq0, \
                     nc.sbuf_tensor("pa_sq1", [128, 4 * D], F32) as pa_sq1, \
                     nc.sbuf_tensor("pa_st0", [128, 32], F32) as pa_st0, \
                     nc.sbuf_tensor("pa_st1", [128, 32], F32) as pa_st1, \
                     nc.sbuf_tensor("pa_xn0", [128, 4 * D], F32) as pa_xn0, \
                     nc.sbuf_tensor("pa_xn1", [128, 4 * D], F32) as pa_xn1, \
                     nc.sbuf_tensor("pa_stage", [128, 8 * T], BF16) as pa_stage:
                    PX = [pa_x0, pa_x1]
                    PQ = [pa_sq0, pa_sq1]
                    PT = [pa_st0, pa_st1]
                    PN = [pa_xn0, pa_xn1]
                    t_ld = {}
                    t_sq = {}
                    t_stats = {}
                    t_apply = {}
                    t_tpev = {}
                    t_gmm = {}
                    t_gevac = {}
                    t_spill = None
                    gi = 0
                    for seq in range(NSEQ):
                        for xt in range(8):
                            gp_ = gi % 2

                            def ld(h, seq=seq, xt=xt, gp_=gp_):
                                h.dma_start(
                                    out=PX[gp_][:].rearrange("p (g d) -> p g d", d=D),
                                    in_=x_in.ap()[seq, xt * 512:(xt + 1) * 512, :]
                                    .rearrange("(g p) d -> p g d", p=128))
                            t_ld[gi] = kb.S("Y", ld, deps=[
                                t_apply.get(gi - 2), t_sq.get(gi - 2),
                                t_stats.get(gi - 2)])

                            def sq(h, gp_=gp_):
                                h.activation(PQ[gp_][:], PX[gp_][:], AF.Square)
                            t_sq[gi] = kb.S("S", sq, deps=[
                                t_ld[gi], t_stats.get(gi - 2)])

                            def stats(h, gp_=gp_):
                                st_ = PT[gp_]
                                for g in range(4):
                                    h.reduce_sum(st_[:, g:g + 1],
                                                 PX[gp_][:, g * D:(g + 1) * D],
                                                 axis=AXX)
                                    h.reduce_sum(st_[:, 4 + g:5 + g],
                                                 PQ[gp_][:, g * D:(g + 1) * D],
                                                 axis=AXX)
                                h.fence()
                                h.tensor_scalar_mul(st_[:, 8:16], st_[:, 0:8], 1.0 / D)
                                h.fence()
                                h.tensor_tensor(st_[:, 16:20], st_[:, 8:12],
                                                st_[:, 8:12], ALU.mult)
                                h.fence()
                                h.tensor_tensor(st_[:, 16:20], st_[:, 12:16],
                                                st_[:, 16:20], ALU.subtract)
                                h.fence()
                                h.tensor_scalar_add(st_[:, 16:20], st_[:, 16:20],
                                                    LN_EPS)
                            t_stats[gi] = kb.S("V", stats, deps=[
                                t_ld[gi], t_sq[gi], t_apply.get(gi - 2)])

                            t_sqrt = kb.S("S", lambda h, gp_=gp_: h.activation(
                                PT[gp_][:, 20:24], PT[gp_][:, 16:20], AF.Sqrt),
                                deps=[t_stats[gi]])

                            def apply(h, gp_=gp_):
                                st_ = PT[gp_]
                                h.reciprocal(st_[:, 24:28], st_[:, 20:24])
                                h.fence()
                                for g in range(4):
                                    h.tensor_scalar(
                                        PN[gp_][:, g * D:(g + 1) * D],
                                        PX[gp_][:, g * D:(g + 1) * D],
                                        st_[:, 8 + g:9 + g], st_[:, 24 + g:25 + g],
                                        ALU.subtract, ALU.mult)
                            t_apply[gi] = kb.S("V", apply, deps=[
                                t_sqrt, t_tpev.get(gi - 2)])

                            t_tp_prev = {}
                            for dt8 in range(8):
                                bank = 4 + dt8 % 2

                                def tp(h, dt8=dt8, bank=bank, gp_=gp_):
                                    for g in range(4):
                                        h.transpose(
                                            PS[bank][:, g * 128:(g + 1) * 128],
                                            PN[gp_][:, g * D + dt8 * 128:
                                                    g * D + (dt8 + 1) * 128],
                                            ident32[:])
                                ttp = kb.S("T", tp, deps=[
                                    t_apply[gi], t_tp_prev.get(dt8 - 2),
                                    t_gmm.get(gi - 1) if dt8 < 2 else None])

                                def ev(h, dt8=dt8, bank=bank, xt=xt):
                                    h.tensor_copy(
                                        pa_stage[:, dt8 * T + xt * 512:
                                                 dt8 * T + (xt + 1) * 512],
                                        PS[bank][:, 0:512])
                                t_tp_prev[dt8] = kb.S("V", ev, deps=[
                                    ttp, t_spill])
                            t_tpev[gi] = t_tp_prev[7]

                            def gmm(h, xt=xt):
                                for k8 in range(8):
                                    h.matmul(
                                        PS[6][0:NE, 0:512],
                                        gw_sb[:, k8 * NE:(k8 + 1) * NE],
                                        pa_stage[:, k8 * T + xt * 512:
                                                 k8 * T + (xt + 1) * 512],
                                        start=(k8 == 0), stop=(k8 == 7),
                                        skip_group_check=True)
                            t_gmm[gi] = kb.S("T", gmm, deps=[
                                t_tp_prev[7],
                                [t_tp_prev[i] for i in range(8)],
                                t_gevac.get(gi - 1)])

                            def gevac(h, seq=seq, xt=xt):
                                h.tensor_copy(
                                    pa_glog[0:NE, seq * T + xt * 512:
                                            seq * T + (xt + 1) * 512],
                                    PS[6][0:NE, 0:512])
                            t_gevac[gi] = kb.S("V", gevac, deps=[t_gmm[gi]])
                            gi += 1

                        def spill(h, seq=seq):
                            h.dma_start(
                                out=xn_fm.ap()[seq].rearrange("k p t -> p k t"),
                                in_=pa_stage[:].rearrange("p (k t) -> p k t", t=T))
                        t_spill = kb.S("Y", spill, deps=[
                            [t_tpev[seq * 8 + i] for i in range(8)],
                            t_gmm[seq * 8 + 7]])

                if "pre_b" in passes:
                    kb.barrier()
                    with nc.sbuf_tensor("sel8sb", [8, 8 * 128], F32) as sel8, \
                         nc.sbuf_tensor("pb_ex", [NE, T], F32) as pb_ex, \
                         nc.sbuf_tensor("pb_sc", [NE, 4], F32) as pb_sc, \
                         nc.sbuf_tensor("pb_lse", [NE, T], F32) as pb_lse, \
                         nc.sbuf_tensor("pb_gb", [128, 2 * T], F32) as pb_gb, \
                         nc.sbuf_tensor("pb_gt", [128, 32 * 8], F32) as pb_gt, \
                         nc.sbuf_tensor("pb_cnt", [128, 32 * 8], F32) as pb_cnt, \
                         nc.sbuf_tensor("pb_junk", [128, T], F32) as pb_junk, \
                         nc.sbuf_tensor("pb_wk", [128, 48], F32) as pb_wk, \
                         nc.sbuf_tensor("pb_wtm", [128, 32 * 8], F32) as pb_wtm, \
                         nc.sbuf_tensor("pb_wm", [NE, T], BF16) as pb_wm:
                        t_s8 = kb.S("G", lambda h: h.dma_start(out=sel8[:],
                                                               in_=sel8_in.ap()))
                        for seq in range(NSEQ):
                            gsl = pa_glog[0:NE, seq * T:(seq + 1) * T]

                            kb.S("V", lambda h, gsl=gsl:
                                 h.tensor_scalar_max(gsl, gsl, 1e-6))
                            kb.S("S", lambda h, gsl=gsl: h.activation(gsl, gsl, AF.Ln))

                            for it in range(8):
                                def lse_t(h, gsl=gsl):
                                    h.activation(pb_ex[:, 0:T], gsl, AF.Exp,
                                                 accum_out=pb_sc[:, 0:1])
                                    h.fence()
                                    h.activation(pb_sc[:, 1:2], pb_sc[:, 0:1], AF.Ln)
                                kb.S("S", lse_t)

                                kb.S("V", lambda h, gsl=gsl: h.tensor_scalar(
                                    gsl, gsl, pb_sc[:, 1:2], None, ALU.subtract))

                                kb.S("S", lambda h, gsl=gsl:
                                     h.activation(pb_ex[:, 0:T], gsl, AF.Exp))

                                for sl in range(8):
                                    def mm1(h, sl=sl):
                                        h.matmul(PS[6][0:NE, 0:512], ones8f[:],
                                                 pb_ex[:, sl * 512:(sl + 1) * 512],
                                                 start=True, stop=True)
                                    kb.S("T", mm1)

                                    def lnev(h, sl=sl):
                                        h.activation(
                                            pb_lse[:, sl * 512:(sl + 1) * 512],
                                            PS[6][0:NE, 0:512], AF.Ln)
                                    kb.S("S", lnev)

                                kb.S("V", lambda h, gsl=gsl: h.tensor_tensor(
                                    gsl, gsl, pb_lse[:, 0:T], ALU.subtract))

                            t_gates = kb.S("S", lambda h, gsl=gsl:
                                           h.activation(pb_ex[:, 0:T], gsl, AF.Exp))

                            if "g_dbg" in dbg:
                                def gdmp(h, seq=seq):
                                    h.dma_start(out=g_dbg.ap()[seq], in_=pb_ex[0:NE, 0:T])
                                kb.S("Y", gdmp)

                            t_gtev_last = None
                            for tt in range(32):
                                bank = 6 + tt % 2

                                def gtp(h, tt=tt, bank=bank):
                                    h.transpose(PS[bank][:, 0:NE],
                                                pb_ex[0:NE, tt * 128:(tt + 1) * 128],
                                                ident8[:])
                                tgt = kb.S("T", gtp, deps=[t_gates, t_gtev_last])

                                def gtev(h, tt=tt, bank=bank):
                                    h.tensor_copy(pb_gt[:, tt * 8:(tt + 1) * 8],
                                                  PS[bank][:, 0:NE])
                                t_gtev_last = kb.S("V", gtev, deps=[tgt])

                            t_gbev = {}
                            t_cnts = {}
                            t_bank = {6: t_gtev_last, 7: t_gtev_last}
                            for e in range(NE):
                                ep = e % 2
                                gbuf = pb_gb[:, ep * T:(ep + 1) * T]
                                for sl in range(8):
                                    bank = 6 + sl % 2

                                    def gbmm(h, e=e, sl=sl, bank=bank):
                                        h.matmul(PS[bank][:, 0:512],
                                                 sel8[:, e * 128:(e + 1) * 128],
                                                 pb_ex[0:NE, sl * 512:(sl + 1) * 512],
                                                 start=True, stop=True,
                                                 skip_group_check=True)
                                    tg = kb.S("T", gbmm, deps=[
                                        t_bank.get(bank),
                                        t_cnts.get(e - 2), t_gates, t_s8])

                                    def gbev(h, sl=sl, bank=bank, gbuf=gbuf):
                                        h.copy(gbuf[:, sl * 512:(sl + 1) * 512],
                                               PS[bank][:, 0:512])
                                    t_gbev[(e, sl)] = kb.S("S", gbev, deps=[tg])
                                    t_bank[bank] = t_gbev[(e, sl)]

                                def cnts(h, e=e, gbuf=gbuf):
                                    for tt in range(32):
                                        h.tensor_scalar(
                                            pb_junk[:, 0:T], gbuf,
                                            pb_gt[:, tt * 8 + e:tt * 8 + e + 1], 0.0,
                                            ALU.is_gt, ALU.add,
                                            accum_out=pb_cnt[:, tt * 8 + e:
                                                             tt * 8 + e + 1])
                                t_cnts[e] = kb.S("V", cnts, deps=[
                                    t_gbev[(e, 7)], t_gtev_last])

                            if "cnt_dbg" in dbg:
                                def cdmp(h, seq=seq):
                                    h.dma_start(out=cnt_dbg.ap()[seq], in_=pb_cnt[:])
                                kb.S("Y", cdmp)

                            for tt in range(32):
                                def winner(h, tt=tt):
                                    m = pb_cnt[:, tt * 8:(tt + 1) * 8]
                                    sel = pb_wk[:, 0:8]
                                    key = pb_wk[:, 8:16]
                                    win = pb_wk[:, 16:24]
                                    gv = pb_wk[:, 24:32]
                                    kmax = pb_wk[:, 32:33]
                                    gwin = pb_wk[:, 33:34]
                                    hard = pb_wk[:, 34:35]
                                    h.tensor_single_scalar(sel, m, 512.0, ALU.is_lt)
                                    h.tensor_single_scalar(key, m, 8.0, ALU.mult)
                                    h.fence()
                                    h.tensor_tensor(key, key, econst[:], ALU.add)
                                    h.fence()
                                    h.tensor_tensor(key, key, sel, ALU.mult)
                                    h.fence()
                                    h.reduce_max(kmax, key, axis=AXX)
                                    h.fence()
                                    h.tensor_scalar(win, key, kmax, None, ALU.is_equal)
                                    h.fence()
                                    h.tensor_tensor(win, win, sel, ALU.mult)
                                    h.fence()
                                    h.tensor_tensor(gv, win,
                                                    pb_gt[:, tt * 8:(tt + 1) * 8],
                                                    ALU.mult)
                                    h.fence()
                                    h.reduce_sum(gwin, gv, axis=AXX)
                                    h.fence()
                                    h.tensor_single_scalar(hard, gwin, 0.5, ALU.is_gt)
                                    h.fence()
                                    h.tensor_scalar(pb_wtm[:, tt * 8:(tt + 1) * 8],
                                                    win, hard, None, ALU.mult)
                                kb.S("V", winner)

                            if "wtm_dbg" in dbg:
                                def wtdmp(h, seq=seq):
                                    h.dma_start(out=wtm_dbg.ap()[seq], in_=pb_wtm[:])
                                kb.S("Y", wtdmp)

                            for tt in range(32):
                                def wtp(h, tt=tt):
                                    h.transpose(PS[6][0:NE, 0:128],
                                                pb_wtm[:, tt * 8:(tt + 1) * 8],
                                                ident32[:])
                                kb.S("T", wtp)

                                def wtev(h, tt=tt):
                                    h.copy(pb_wm[0:NE, tt * 128:(tt + 1) * 128],
                                           PS[6][0:NE, 0:128])
                                kb.S("S", wtev)

                            def wspill(h, seq=seq):
                                h.dma_start(out=wm_sp.ap()[seq], in_=pb_wm[0:NE, 0:T])
                            kb.S("Y", wspill)

        if "pre_c" in passes:
            kb.barrier()
            with nc.sbuf_tensor("pc_ex", [128, 64 * 1024], BF16) as pc_ex, \
                 nc.sbuf_tensor("pc_xn0", [128, 8 * TC], BF16) as pc_xn0, \
                 nc.sbuf_tensor("pc_xn1", [128, 8 * TC], BF16) as pc_xn1, \
                 nc.sbuf_tensor("pc_mb0", [128, NE * TC], BF16) as pc_mb0, \
                 nc.sbuf_tensor("pc_mb1", [128, NE * TC], BF16) as pc_mb1, \
                 nc.sbuf_tensor("pc_xnm0", [128, 8 * TC], BF16) as pc_xnm0, \
                 nc.sbuf_tensor("pc_xnm1", [128, 8 * TC], BF16) as pc_xnm1, \
                 nc.sbuf_tensor("pc_out", [128, 8 * TC], BF16) as pc_out:
                XNB = [pc_xn0, pc_xn1]
                MBB = [pc_mb0, pc_mb1]
                XNM = [pc_xnm0, pc_xnm1]

                def ldex(h):
                    h.dma_start(
                        out=pc_ex[:].rearrange("p (e i o) -> p e i o", e=NE, i=8),
                        in_=experts_t.ap().rearrange("e (i p) o -> p e i o", p=128))
                t_ldex = kb.S("G", ldex)

                NCHT = NSEQ * NCH
                t_mbb = {}
                t_xnm = {}
                t_emm = {}
                t_moev = {}
                t_moesp = {}
                t_ldxn = {}

                for gi in range(NCHT):
                    seq, ch = divmod(gi, NCH)
                    c0 = ch * TC
                    cp = gi % 2

                    def ldxn(h, seq=seq, c0=c0, cp=cp):
                        h.dma_start(
                            out=XNB[cp][:].rearrange("p (k t) -> p k t", t=TC),
                            in_=xn_fm.ap()[seq, :, :, c0:c0 + TC]
                            .rearrange("k p t -> p k t"))
                    t_ldxn[gi] = kb.S("Y", ldxn, deps=[
                        t_xnm.get((gi - 2) * 8 + 7)])

                    def mbb(h, seq=seq, c0=c0, cp=cp):
                        h.dma_start(
                            out=MBB[cp][:].rearrange("p (e t) -> p e t", t=TC),
                            in_=wm_sp.ap()[seq, :, c0:c0 + TC]
                            .rearrange("e t -> () e t")
                            .broadcast_to((128, NE, TC)))
                    t_mbb[gi] = kb.S("Y", mbb, deps=[
                        t_xnm.get((gi - 2) * 8 + 7)])

                    for e in range(NE):
                        ge = gi * 8 + e
                        ep = e % 2

                        def xnm(h, e=e, cp=cp, ep=ep):
                            for k8 in range(8):
                                h.tensor_tensor(
                                    XNM[ep][:, k8 * TC:(k8 + 1) * TC],
                                    XNB[cp][:, k8 * TC:(k8 + 1) * TC],
                                    MBB[cp][:, e * TC:(e + 1) * TC], ALU.mult)
                        t_xnm[ge] = kb.S("V", xnm, deps=[
                            t_ldxn[gi], t_mbb[gi], t_emm.get(ge - 2),
                            t_moev.get(gi - 1) if e == 0 else None])

                        def emm(h, e=e, ep=ep):
                            for ot in range(8):
                                for k8 in range(8):
                                    h.matmul(
                                        PS[ot][:, 0:512],
                                        pc_ex[:, (e * 64 + k8 * 8 + ot) * 128:
                                              (e * 64 + k8 * 8 + ot + 1) * 128],
                                        XNM[ep][:, k8 * TC:(k8 + 1) * TC],
                                        start=(e == 0 and k8 == 0),
                                        stop=(e == NE - 1 and k8 == 7),
                                        skip_group_check=True)
                        t_emm[ge] = kb.S("T", emm, deps=[
                            t_xnm[ge], t_ldex,
                            (t_moev.get(gi - 1)) if e == 0 else None])

                    def moev(h):
                        for ot in range(8):
                            h.tensor_copy(pc_out[:, ot * TC:(ot + 1) * TC],
                                          PS[ot][:, 0:512])
                    t_moev[gi] = kb.S("V", moev, deps=[t_emm[gi * 8 + 7],
                                                       t_moesp.get(gi - 1)])

                    def moesp(h, seq=seq, c0=c0):
                        h.dma_start(
                            out=moe_fm.ap()[seq, :, :, c0:c0 + TC]
                            .rearrange("k p t -> p k t"),
                            in_=pc_out[:].rearrange("p (k t) -> p k t", t=TC))
                    t_moesp[gi] = kb.S("Y", moesp, deps=[t_moev[gi]])

        def pass_f1(d):
            kb.barrier()
            with ExitStack() as _es:
                def AL(nm, shape, dt):
                    return _es.enter_context(nc.sbuf_tensor(nm + d, shape, dt))
                f1_inw = AL("f1_inw", [128, 8 * 4096], BF16)
                f1_xw = AL("f1_xw", [128, 16 * 96], BF16)
                f1_dtw = AL("f1_dtw", [64, 16 * 128], BF16)
                f1_xn0 = AL("f1_xn0", [128, 8 * TC], BF16)
                f1_xn1 = AL("f1_xn1", [128, 8 * TC], BF16)
                f1_xi = AL("f1_xi", [128, 16 * 515], BF16)
                f1_ca = AL("f1_ca", [128, 16 * TC], BF16)
                f1_xc = AL("f1_xc", [128, 16 * TC], BF16)
                f1_sz = AL("f1_sz", [128, 16 * TC], BF16)
                f1_xdbl = AL("f1_xdbl", [96, TC], BF16)
                f1_dtt = AL("f1_dtt", [128, 16 * TC], F32)
                f1_u = AL("f1_u", [128, 16 * TC], BF16)
                XN = [f1_xn0, f1_xn1]

                def ldw(h, d=d):
                    h.dma_start(out=f1_inw[:].rearrange("p (k f) -> p k f", f=4096),
                                in_=inwT[d].ap().rearrange("(k p) f -> p k f", p=128))
                    h.dma_start(out=f1_xw[:].rearrange("p (k f) -> p k f", f=96),
                                in_=xwT[d].ap().rearrange("(k p) f -> p k f", p=128))
                    h.dma_start(out=f1_dtw[:].rearrange("p (k f) -> p k f", f=128),
                                in_=dtwT[d].ap().rearrange("r (k p) -> r k p", p=128))
                t_ldw = kb.S("G", ldw)

                fwd = (d == "f")
                t_ldxn = {}
                t_mm31 = {}
                t_csilu = {}
                t_zev = {}
                t_xpmm = None
                t_xdblev = None
                t_bcspill = None
                t_dt15 = None
                t_sig = {}
                t_clamp15 = None
                t_spills = None
                ci = 0
                for seq in range(NSEQ):
                    def ztail(h):
                        if fwd:
                            h.memset(f1_xi[:].rearrange("p (k c) -> p k c", c=515)
                                     [:, :, 0:3], 0.0)
                        else:
                            h.memset(f1_xi[:].rearrange("p (k c) -> p k c", c=515)
                                     [:, :, 512:515], 0.0)
                    kb.S("V", ztail, deps=None)

                    chs = range(NCH) if fwd else range(NCH - 1, -1, -1)
                    first = True
                    for ch in chs:
                        c0 = ch * TC
                        cp = ci % 2

                        def ldxn(h, seq=seq, c0=c0, cp=cp):
                            h.dma_start(
                                out=XN[cp][:].rearrange("p (k t) -> p k t", t=TC),
                                in_=xn_fm.ap()[seq, :, :, c0:c0 + TC]
                                .rearrange("k p t -> p k t"))
                        t_ldxn[ci] = kb.S("Y", ldxn, deps=[t_mm31.get(ci - 2)])

                        if not first:
                            def tails(h):
                                xiv = f1_xi[:].rearrange("p (k c) -> p k c", c=515)
                                if fwd:
                                    h.tensor_copy(xiv[:, :, 0:3], xiv[:, :, 512:515])
                                else:
                                    h.tensor_copy(xiv[:, :, 512:515], xiv[:, :, 0:3])
                            kb.S("V", tails, deps=[])
                        first = False

                        t_ev = {}
                        for f in range(32):
                            def ipmm(h, f=f, cp=cp):
                                for k8 in range(8):
                                    h.matmul(PS[f % 2][:, 0:512],
                                             f1_inw[:, k8 * 4096 + f * 128:
                                                    k8 * 4096 + (f + 1) * 128],
                                             XN[cp][:, k8 * TC:(k8 + 1) * TC],
                                             start=(k8 == 0), stop=(k8 == 7),
                                             skip_group_check=True)
                            tmm = kb.S("T", ipmm, deps=[t_ldxn[ci], t_ldw,
                                                        t_ev.get(f - 2)])
                            if f == 31:
                                t_mm31[ci] = tmm

                            if f < 16:
                                def ipev(h, f=f):
                                    xiv = f1_xi[:].rearrange("p (k c) -> p k c", c=515)
                                    dst = xiv[:, f, 3:515] if fwd else xiv[:, f, 0:512]
                                    h.tensor_copy(dst, PS[f % 2][:, 0:512])
                                t_ev[f] = kb.S("V", ipev, deps=[tmm])
                            else:
                                def zev(h, f=f):
                                    h.tensor_copy(f1_sz[:, (f - 16) * TC:(f - 15) * TC],
                                                  PS[f % 2][:, 0:512])
                                t_ev[f] = kb.S("V", zev, deps=[tmm, t_spills])
                                t_zev[f - 16] = t_ev[f]

                        t_conv_last = None
                        for k in range(16):
                            def conv(h, k=k, d=d):
                                xiv = f1_xi[:].rearrange("p (k c) -> p k c", c=515)
                                cak = f1_ca[:, k * TC:(k + 1) * TC]
                                for j in range(4):
                                    off = j if fwd else (3 - j)
                                    src = xiv[:, k, off:off + 512]
                                    if j == 0:
                                        h.tensor_scalar(
                                            cak, src,
                                            cw_sb[d][:, k * 4 + j:k * 4 + j + 1],
                                            None, ALU.mult)
                                    else:
                                        h.tensor_scalar(
                                            pb_scratch[:], src,
                                            cw_sb[d][:, k * 4 + j:k * 4 + j + 1],
                                            None, ALU.mult)
                                        h.tensor_tensor(cak, cak,
                                                        pb_scratch[:], ALU.add)
                            t_conv_last = kb.S("V", conv, deps=[
                                t_csilu.get(0) if k == 0 else None])

                        def csilu_b(h, d=d):
                            for k in range(16):
                                h.activation(f1_xc[:, k * TC:(k + 1) * TC],
                                             f1_ca[:, k * TC:(k + 1) * TC], AF.Silu,
                                             bias=cb_sb[d][:, k:k + 1])
                            for k in range(16):
                                h.activation(f1_sz[:, k * TC:(k + 1) * TC],
                                             f1_sz[:, k * TC:(k + 1) * TC], AF.Silu)
                        t_csilu[0] = kb.S("S", csilu_b, deps=[
                            t_conv_last, t_ev[31], t_xpmm, t_spills])

                        def xpmm(h):
                            for k in range(16):
                                h.matmul(PS[2][0:96, 0:512],
                                         f1_xw[:, k * 96:(k + 1) * 96],
                                         f1_xc[:, k * TC:(k + 1) * TC],
                                         start=(k == 0), stop=(k == 15),
                                         skip_group_check=True)
                        t_xpmm = kb.S("T", xpmm, deps=[t_csilu[0], t_xdblev, t_ldw])

                        def xdblev(h):
                            h.tensor_copy(f1_xdbl[0:96, 0:TC], PS[2][0:96, 0:512])
                        t_xdblev = kb.S("V", xdblev, deps=[
                            t_xpmm, t_bcspill, t_dt15])

                        def bcspill(h, seq=seq, c0=c0, d=d):
                            h.dma_start(
                                out=bc_sp.ap()[seq, :, :, c0:c0 + TC]
                                .rearrange("c s t -> (c s) t"),
                                in_=f1_xdbl[64:96, 0:TC])
                        t_bcspill = kb.S("Y", bcspill, deps=[t_xdblev])

                        DTBANK = [3, 4, 5, 7]
                        t_dtmm = {}
                        for k in range(16):
                            def dtmm(h, k=k):
                                h.matmul(PS[DTBANK[k % 4]][:, 0:512],
                                         f1_dtw[:, k * 128:(k + 1) * 128],
                                         f1_xdbl[0:64, 0:TC],
                                         start=True, stop=True,
                                         skip_group_check=True)
                            t_dtmm[k] = kb.S("T", dtmm, deps=[
                                t_xdblev, t_sig.get(k // 4 - 1), t_ldw])
                            if k == 15:
                                t_dt15 = t_dtmm[k]

                            if k % 4 == 3:
                                def sig(h, k0=k - 3, d=d):
                                    for kk in range(k0, k0 + 4):
                                        h.activation(
                                            f1_dtt[:, kk * TC:(kk + 1) * TC],
                                            PS[DTBANK[kk % 4]][:, 0:512],
                                            AF.Sigmoid, scale=-1.0,
                                            bias=ndtb_sb[d][:, kk:kk + 1])
                                t_sig[k // 4] = kb.S("S", sig, deps=[
                                    t_dtmm[k], t_spills])

                        def lnb(h):
                            for k in range(16):
                                h.activation(f1_u[:, k * TC:(k + 1) * TC],
                                             f1_dtt[:, k * TC:(k + 1) * TC], AF.Ln)
                        t_lnb = kb.S("S", lnb, deps=[t_sig[3], t_spills])

                        def clamp(h):
                            h.tensor_scalar_max(f1_u[:], f1_u[:], U_CLAMP)
                        t_clamp15 = kb.S("V", clamp, deps=[t_lnb])

                        def spills(h, seq=seq, c0=c0):
                            h.dma_start(
                                out=xc_sp.ap()[seq, :, :, c0:c0 + TC]
                                .rearrange("k p t -> p k t"),
                                in_=f1_xc[:].rearrange("p (k t) -> p k t", t=TC))
                            h.dma_start(
                                out=u_sp.ap()[seq, :, :, c0:c0 + TC]
                                .rearrange("k p t -> p k t"),
                                in_=f1_u[:].rearrange("p (k t) -> p k t", t=TC))
                            h.dma_start(
                                out=sz_sp.ap()[seq, :, :, c0:c0 + TC]
                                .rearrange("k p t -> p k t"),
                                in_=f1_sz[:].rearrange("p (k t) -> p k t", t=TC))
                        t_spills = kb.S("Y", spills, deps=[
                            t_clamp15, t_csilu[0]])
                        ci += 1

        def pass_f2(d):
            kb.barrier()
            fwd = (d == "f")
            W = PC + 2
            with ExitStack() as _es:
                def AL(nm, shape, dt):
                    return _es.enter_context(nc.sbuf_tensor(nm + d, shape, dt))
                f2_ow = AL("f2_ow", [128, 2 * 16 * 128], BF16)
                f2_bcB = AL("f2_bcB", [128, NS * PC], BF16)
                f2_bcC = AL("f2_bcC", [128, NS * PC], BF16)
                f2_a0 = AL("f2_a0", [128, SQ * W], BF16)
                f2_a1 = AL("f2_a1", [128, SQ * W], BF16)
                f2_r0 = AL("f2_r0", [128, SQ * W], BF16)
                f2_r1 = AL("f2_r1", [128, SQ * W], BF16)
                f2_u0 = AL("f2_u0", [128, PC], BF16)
                f2_u1 = AL("f2_u1", [128, PC], BF16)
                f2_u2 = AL("f2_u2", [128, PC], BF16)
                f2_u3 = AL("f2_u3", [128, PC], BF16)
                f2_xc0 = AL("f2_xc0", [128, PC], BF16)
                f2_xc1 = AL("f2_xc1", [128, PC], BF16)
                f2_xc2 = AL("f2_xc2", [128, PC], BF16)
                f2_xc3 = AL("f2_xc3", [128, PC], BF16)
                f2_sz0 = AL("f2_sz0", [128, PC], BF16)
                f2_sz1 = AL("f2_sz1", [128, PC], BF16)
                f2_sz2 = AL("f2_sz2", [128, PC], BF16)
                f2_sz3 = AL("f2_sz3", [128, PC], BF16)
                f2_ut0 = AL("f2_ut0", [128, PC], BF16)
                f2_ut1 = AL("f2_ut1", [128, PC], BF16)
                f2_tb = AL("f2_tb", [128, PC], BF16)
                f2_yf = AL("f2_yf", [128, 16 * PC], BF16)
                f2_yg = AL("f2_yg", [128, 16 * PC], BF16)
                f2_carry = AL("f2_carry", [128, 16 * NS], F32)
                f2_mx0 = AL("f2_mx0", [128, 512], BF16)
                f2_mx1 = AL("f2_mx1", [128, 512], BF16)

                AB = [f2_a0, f2_a1]
                RB = [f2_r0, f2_r1]
                UB = [f2_u0, f2_u1, f2_u2, f2_u3]
                XCB = [f2_xc0, f2_xc1, f2_xc2, f2_xc3]
                SZB = [f2_sz0, f2_sz1, f2_sz2, f2_sz3]
                UTB = [f2_ut0, f2_ut1]
                MXB = [f2_mx0, f2_mx1]
                YF = [f2_yf, f2_yg]

                def zspc(h):
                    for ab in (f2_a0, f2_a1):
                        v = ab[:].rearrange("p (s c) -> p s c", c=W)
                        h.memset(v[:, :, 0:1], 0.0)
                        h.memset(v[:, :, 1:2], 1.0)
                    for rb_ in (f2_r0, f2_r1):
                        v = rb_[:].rearrange("p (s c) -> p s c", c=W)
                        h.memset(v[:, :, 1:2], 0.0)
                t_zspc = kb.S("V", zspc)

                t_aact = {}
                t_V = {}
                t_ymm = {}
                t_eplg = {}
                t_ldt = {}
                lastV_par = {}
                lastV_q = {}
                lastY_par = {}
                last_aact_par = {}
                last_eplg_par = {}
                last_eplg_any = [None]
                t_owr = {}
                last_hc_q = {}

                OPB = [0, 1, 6, 7]
                gidx = [0]
                t_mm_g = {}
                t_opev_g = {}
                t_opsp_g = {}
                last_op_par = {}
                op_pend = [None]

                def emit_ldws(bat, ot):
                    op_ = ot % 2

                    def ldws(h, ot=ot, op_=op_, d=d):
                        h.dma_start(
                            out=f2_ow[:, op_ * 2048:(op_ + 1) * 2048]
                            .rearrange("p (k o) -> p k o", o=128),
                            in_=outwT[d].ap()[:, ot * 128:(ot + 1) * 128]
                            .rearrange("(k p) o -> p k o", p=128))
                    bat["ow"][ot] = kb.S("G", ldws, deps=[t_owr.get(op_)])

                def emit_opmm(bat, idx):
                    ot, hf = idx // 2, idx % 2
                    g = gidx[0]
                    gidx[0] += 1
                    bank = OPB[g % 4]
                    op_ = ot % 2

                    def opmm(h, bat=bat, hf=hf, op_=op_, bank=bank):
                        pp = bat["pp"]
                        for k in range(16):
                            h.matmul(
                                PS[bank][:, 0:512],
                                f2_ow[:, op_ * 2048 + k * 128:
                                      op_ * 2048 + (k + 1) * 128],
                                YF[pp][:, k * PC + hf * 512:
                                       k * PC + (hf + 1) * 512],
                                start=(k == 0), stop=(k == 15),
                                skip_group_check=True)
                    t_mm_g[g] = kb.S("T", opmm, deps=[
                        bat["t_ep"], bat["ow"][ot], t_opev_g.get(g - 4)])
                    bat["mm"][idx] = (g, bank)
                    if hf == 1:
                        t_owr[op_] = t_mm_g[g]
                    if idx == 15:
                        last_op_par[bat["pp"]] = t_mm_g[g]

                def emit_opev(bat, idx):
                    g, bank = bat["mm"][idx]
                    mxi = g % 2
                    ot, hf = idx // 2, idx % 2

                    def opev(h, mxi=mxi, bank=bank):
                        h.tensor_copy(MXB[mxi][:], PS[bank][:, 0:512])
                    t_opev_g[g] = kb.S("V", opev, deps=[
                        t_mm_g[g], t_opsp_g.get(g - 2)])

                    def opsp(h, bat=bat, ot=ot, hf=hf, mxi=mxi, d=d):
                        h.dma_start(
                            out=mix[d].ap()[bat["seq"], ot, :,
                                            bat["p0"] + hf * 512:
                                            bat["p0"] + (hf + 1) * 512],
                            in_=MXB[mxi][:])
                    t_opsp_g[g] = kb.S("Y", opsp, deps=[t_opev_g[g]])

                def op_step(k):
                    bat = op_pend[0]
                    if bat is None:
                        return
                    if k >= 4:
                        emit_opev(bat, k - 4)
                    emit_opmm(bat, k)
                    if k == 0:
                        emit_ldws(bat, 1)
                    elif k % 2 == 1 and (k + 3) // 2 <= 7:
                        emit_ldws(bat, (k + 3) // 2)

                def op_tail():
                    bat = op_pend[0]
                    if bat is None:
                        return
                    for idx in range(12, 16):
                        emit_opev(bat, idx)
                    op_pend[0] = None

                def op_drain():
                    bat = op_pend[0]
                    if bat is None:
                        return
                    for idx in range(16):
                        if idx >= 4:
                            emit_opev(bat, idx - 4)
                        emit_opmm(bat, idx)
                        if idx == 0:
                            emit_ldws(bat, 1)
                        elif idx % 2 == 1 and (idx + 3) // 2 <= 7:
                            emit_ldws(bat, (idx + 3) // 2)
                    for idx in range(12, 16):
                        emit_opev(bat, idx)
                    op_pend[0] = None

                for seq in range(NSEQ):
                    kb.S("V", lambda h: h.memset(f2_carry[:], 0.0))

                    prs = range(NPAIR) if fwd else range(NPAIR - 1, -1, -1)
                    for pi, p in enumerate(prs):
                        p0 = p * PC

                        t_bcq = {}
                        for qq in range(NSQ):
                            def ldbq(h, seq=seq, p0=p0, qq=qq):
                                s0_ = qq * SQ
                                for ci, dst_ in ((0, f2_bcB), (1, f2_bcC)):
                                    h.dma_start(
                                        out=dst_[:, s0_ * PC:(s0_ + SQ) * PC]
                                        .rearrange("p (s t) -> p s t", t=PC),
                                        in_=bc_sp.ap()[seq, ci, s0_:s0_ + SQ,
                                                       p0:p0 + PC]
                                        .rearrange("s t -> () s t")
                                        .broadcast_to((128, SQ, PC)))
                            t_bcq[qq] = kb.S("Y", ldbq, deps=[
                                lastV_q.get(qq), last_hc_q.get(qq)])

                        pend_eplg = [None]
                        pp = pi % 2

                        def emit_eplg(kk):
                            if kk is None:
                                return
                            kp2 = kk % 2

                            def eplg(h, k=kk, k4=kk % 4, d=d, pp=pp):
                                xcv = XCB[k4][:] if fwd else XCB[k4][:, ::-1]
                                szv = SZB[k4][:] if fwd else SZB[k4][:, ::-1]
                                h.tensor_scalar(f2_tb[:], xcv,
                                                dv_sb[d][:, k:k + 1], None, ALU.mult)
                                for hf in range(2):
                                    h.tensor_tensor(
                                        f2_tb[:, hf * 512:(hf + 1) * 512],
                                        PS[2 + hf][:, 0:512],
                                        f2_tb[:, hf * 512:(hf + 1) * 512], ALU.add)
                                h.tensor_tensor(
                                    YF[pp][:, k * PC:(k + 1) * PC],
                                    f2_tb[:], szv, ALU.mult)
                            t_eplg[kk] = kb.S("V", eplg, deps=[t_ymm[(kk, NSQ - 1)],
                                                               last_op_par.get(pp)])
                            last_eplg_par[kk % 4] = t_eplg[kk]
                            last_eplg_any[0] = t_eplg[kk]

                        for k in range(16):
                            kp = k % 2
                            k4 = k % 4

                            def ldt(h, seq=seq, p0=p0, k=k, k4=k4):
                                h.dma_start(out=UB[k4][:],
                                            in_=u_sp.ap()[seq, k, :, p0:p0 + PC])
                                h.dma_start(out=XCB[k4][:],
                                            in_=xc_sp.ap()[seq, k, :, p0:p0 + PC])
                                h.dma_start(out=SZB[k4][:],
                                            in_=sz_sp.ap()[seq, k, :, p0:p0 + PC])
                            t_ldt[k] = kb.S("G", ldt, deps=[
                                last_aact_par.get(k4), last_eplg_par.get(k4)])

                            def utt(h, kp=kp, k4=k4, fwd=fwd):
                                h.scalar_tensor_tensor(UTB[kp][:], UB[k4][:], -1.0,
                                                       XCB[k4][:],
                                                       ALU.mult, ALU.mult)
                                if not fwd:
                                    h.fence()
                            kb.S("V", utt, deps=[t_ldt[k]])

                            for q in range(NSQ):
                                qp = q % 2

                                def aact(h, k=k, q=q, k4=k4, qp=qp, d=d):
                                    for si in range(SQ):
                                        s = q * SQ + si
                                        srcv = (UB[k4][:] if fwd
                                                else UB[k4][:, ::-1])
                                        h.activation(
                                            AB[qp][:, si * W + 2:(si + 1) * W],
                                            srcv, AF.Exp,
                                            scale=expA[d][:, k * NS + s:
                                                          k * NS + s + 1])
                                t_aact[(k, q)] = kb.S("S", aact, deps=[
                                    t_ldt[k], lastV_par.get(qp) or t_zspc])
                                if q == NSQ - 1:
                                    last_aact_par[k4] = t_aact[(k, q)]

                                def bigV(h, k=k, q=q, kp=kp, qp=qp):
                                    utv = (UTB[kp][:] if fwd else UTB[kp][:, ::-1]) \
                                        .rearrange("p (o t) -> p o t", o=1) \
                                        .broadcast_to((128, SQ, PC))
                                    bv = f2_bcB[:].rearrange("p (s t) -> p s t", t=PC)
                                    bv = bv[:, q * SQ:(q + 1) * SQ, :]
                                    if not fwd:
                                        bv = bv[:, :, ::-1]
                                    rv = RB[qp][:].rearrange("p (s c) -> p s c", c=W)
                                    h.tensor_tensor(rv[:, :, 2:W], utv, bv, ALU.mult)
                                    h.memset(rv[:, :, 1:2], 0.0)
                                    h.tensor_copy(
                                        rv[:, :, 0:1].rearrange("p s o -> p (s o)"),
                                        f2_carry[:, k * NS + q * SQ:
                                                 k * NS + (q + 1) * SQ])
                                    h.fence()
                                    h.tensor_tensor_scan(
                                        RB[qp][:], AB[qp][:], RB[qp][:], 0.0,
                                        ALU.mult, ALU.add)
                                    h.fence()
                                    h.tensor_copy(
                                        f2_carry[:, k * NS + q * SQ:
                                                 k * NS + (q + 1) * SQ],
                                        rv[:, :, W - 1:W].rearrange("p s o -> p (s o)"))
                                t_V[(k, q)] = kb.S("V", bigV, deps=[
                                    t_aact[(k, q)], lastY_par.get(qp),
                                    t_bcq[q]])
                                lastV_par[qp] = t_V[(k, q)]
                                lastV_q[q] = t_V[(k, q)]
                                if q == 0 and pend_eplg[0] is not None:
                                    emit_eplg(pend_eplg[0])
                                    pend_eplg[0] = None

                                t_dmp = None
                                if ("f2_dbg" in dbg and d == "b" and seq == 0
                                        and pi == 0 and k == 0 and q == 0):
                                    def f2dmp0(h, qp=qp):
                                        h.dma_start(out=f2_dbg.ap()[0, :, 0:SQ * W],
                                                    in_=AB[qp][:])
                                        h.dma_start(out=f2_dbg.ap()[2, :, 0:SQ * W],
                                                    in_=RB[qp][:])
                                        h.dma_start(out=f2_dbg.ap()[3, :, 0:PC],
                                                    in_=UTB[0][:])
                                        h.dma_start(out=f2_dbg.ap()[3, :, PC:2 * PC],
                                                    in_=f2_bcB[:, 0:PC])
                                        h.dma_start(out=f2_dbg.ap()[3, :, 2 * PC:2 * PC + 256],
                                                    in_=f2_carry[:])
                                        h.dma_start(out=f2_dbg.ap()[3, :, 3 * PC:4 * PC],
                                                    in_=f2_bcC[:, 0:PC])
                                    t_dmp = kb.S("G", f2dmp0, deps=[t_V[(k, q)]])

                                def hcg(h, k=k, q=q, qp=qp):
                                    rv = RB[qp][:].rearrange("p (s c) -> p s c", c=W)
                                    cv = f2_bcC[:].rearrange("p (s t) -> p s t", t=PC)
                                    cv = cv[:, q * SQ:(q + 1) * SQ, :]
                                    if not fwd:
                                        cv = cv[:, :, ::-1]
                                    h.tensor_tensor(rv[:, :, 2:W], rv[:, :, 2:W],
                                                    cv, ALU.mult)
                                t_hc = kb.S("V", hcg, deps=[
                                    t_V[(k, q)], t_dmp, t_bcq[q]])
                                last_hc_q[q] = t_hc

                                def ymm(h, k=k, q=q, qp=qp):
                                    for si in range(SQ):
                                        h.matmul(PS[2][:, 0:512], ident[:],
                                                 RB[qp][:, si * W + 2:si * W + 514],
                                                 start=(q == 0 and si == 0),
                                                 stop=(q == NSQ - 1 and si == SQ - 1),
                                                 skip_group_check=True)
                                        h.matmul(PS[3][:, 0:512], ident[:],
                                                 RB[qp][:, si * W + 514:si * W + 1026],
                                                 start=(q == 0 and si == 0),
                                                 stop=(q == NSQ - 1 and si == SQ - 1),
                                                 skip_group_check=True)
                                t_ymm[(k, q)] = kb.S("T", ymm, deps=[
                                    t_hc,
                                    last_eplg_any[0] if q == 0 else None])
                                lastY_par[qp] = t_ymm[(k, q)]

                            op_step(k)
                            pend_eplg[0] = k

                        emit_eplg(pend_eplg[0])
                        pend_eplg[0] = None
                        op_tail()

                        nb = {"seq": seq, "p0": p0, "pp": pp,
                              "t_ep": t_eplg[15], "ow": {}, "mm": {}}
                        emit_ldws(nb, 0)
                        op_pend[0] = nb
                    op_drain()
                    kb.S("V", lambda h: h.memset(f2_carry[0:1, 0:1], 0.0),
                         deps=[t_opsp_g.get(gidx[0] - 1),
                               t_opsp_g.get(gidx[0] - 2)])

        def pass_c1():
            kb.barrier()
            with ExitStack() as _es:
                def AL(nm, shape, dt):
                    return _es.enter_context(nc.sbuf_tensor(nm, shape, dt))
                c1_w1 = AL("c1_w1", [128, 8 * 4096], BF16)
                c1_w2a = AL("c1_w2a", [128, 32 * 128], BF16)
                c1_w2b = AL("c1_w2b", [128, 32 * 128], BF16)
                c1_xn = AL("c1_xn", [128, 8 * TC], BF16)
                c1_yf = AL("c1_yf", [128, 8 * TC], BF16)
                c1_yb = AL("c1_yb", [128, 8 * TC], BF16)
                c1_sqt = AL("c1_sqt", [128, TC], BF16)
                c1_squ = AL("c1_squ", [128, TC], BF16)
                c1_xln = AL("c1_xln", [128, 8 * TC], BF16)
                c1_h1 = AL("c1_h1", [128, 32 * TC], BF16)
                c1_h1r = AL("c1_h1r", [128, 32 * TC], BF16)
                c1_st = AL("c1_st", [1, 5 * TC], F32)
                c1_mb = AL("c1_mb", [128, 2 * TC], F32)
                c1_ffh = AL("c1_ffh", [128, 8 * TC], BF16)
                W2 = [c1_w2a, c1_w2b]

                def ldw(h):
                    h.dma_start(out=c1_w1[:].rearrange("p (k f) -> p k f", f=4096),
                                in_=ff1T.ap().rearrange("(k p) f -> p k f", p=128))
                t_ldwc1 = kb.S("G", ldw)

                t_lds = None
                t_x2c = None
                t_stat1 = None
                t_sq7 = None
                t_lnap = None
                t_spl = None
                t_f2mm = {}
                t_f2ev = {}
                t_f1ev = {}
                t_ldw2 = {}
                t_mbev = None
                t_silu_b = None
                for seq in range(NSEQ):
                    for ch in range(NCH):
                        c0 = ch * TC
                        p = ch // 2
                        rev0 = p * PC + (512 if ch % 2 == 0 else 0)

                        def lds(h, seq=seq, c0=c0, rev0=rev0):
                            h.dma_start(
                                out=c1_xn[:].rearrange("p (k t) -> p k t", t=TC),
                                in_=xn_fm.ap()[seq, :, :, c0:c0 + TC]
                                .rearrange("k p t -> p k t"))
                            h.dma_start(
                                out=c1_yf[:].rearrange("p (k t) -> p k t", t=TC),
                                in_=mix["f"].ap()[seq, :, :, c0:c0 + TC]
                                .rearrange("k p t -> p k t"))
                            h.dma_start(
                                out=c1_yb[:].rearrange("p (k t) -> p k t", t=TC),
                                in_=mix["b"].ap()[seq, :, :, rev0:rev0 + TC]
                                .rearrange("k p t -> p k t"))
                        t_lds = kb.S("Y", lds, deps=[t_spl, t_lnap, t_stat1, t_sq7])

                        def x2c(h):
                            yfv = c1_yf[:].rearrange("p (k t) -> p k t", t=TC)
                            ybv = c1_yb[:].rearrange("p (k t) -> p k t", t=TC)[:, :, ::-1]
                            h.tensor_tensor(yfv, yfv, ybv, ALU.add)
                            h.tensor_scalar_mul(c1_yb[:], c1_yf[:], 0.5)
                            h.tensor_tensor(c1_yb[:], c1_yb[:], c1_xn[:], ALU.add)
                        t_x2c = kb.S("V", x2c, deps=[t_lds])
                        c1_x2 = c1_yb

                        def statmm1(h):
                            for k8 in range(8):
                                h.matmul(PS[4][0:1, 0:512], ones128b[:],
                                         c1_x2[:, k8 * TC:(k8 + 1) * TC],
                                         start=(k8 == 0), stop=(k8 == 7),
                                         skip_group_check=True)
                        t_stat1 = kb.S("T", statmm1, deps=[t_x2c, t_mbev])

                        SQT = [c1_sqt, c1_squ]
                        t_sq = None
                        t_stat2 = None
                        t_st2d = {}
                        for k8 in range(8):
                            def sqk(h, k8=k8):
                                h.activation(SQT[k8 % 2][:],
                                             c1_x2[:, k8 * TC:(k8 + 1) * TC],
                                             AF.Square)
                            t_sq = kb.S("S", sqk, deps=[t_x2c,
                                                        t_st2d.get(k8 - 2)])

                            def statmm2(h, k8=k8):
                                h.matmul(PS[5][0:1, 0:512], ones128b[:],
                                         SQT[k8 % 2][:],
                                         start=(k8 == 0), stop=(k8 == 7),
                                         skip_group_check=True)
                            t_stat2 = kb.S("T", statmm2, deps=[t_sq, t_mbev])
                            t_st2d[k8] = t_stat2
                        t_sq7 = t_sq

                        def statv(h):
                            h.tensor_scalar_mul(c1_st[0:1, 0:TC], PS[4][0:1, 0:512],
                                                1.0 / D)
                            h.tensor_scalar_mul(c1_st[0:1, TC:2 * TC],
                                                PS[5][0:1, 0:512], 1.0 / D)
                            h.fence()
                            h.tensor_tensor(c1_st[0:1, 2 * TC:3 * TC],
                                            c1_st[0:1, 0:TC], c1_st[0:1, 0:TC],
                                            ALU.mult)
                            h.fence()
                            h.tensor_tensor(c1_st[0:1, 2 * TC:3 * TC],
                                            c1_st[0:1, TC:2 * TC],
                                            c1_st[0:1, 2 * TC:3 * TC], ALU.subtract)
                            h.fence()
                            h.tensor_scalar_add(c1_st[0:1, 2 * TC:3 * TC],
                                                c1_st[0:1, 2 * TC:3 * TC], LN_EPS)
                        t_sv = kb.S("V", statv, deps=[t_stat1, t_stat2])

                        t_sqr = kb.S("S", lambda h: h.activation(
                            c1_st[0:1, 3 * TC:4 * TC],
                            c1_st[0:1, 2 * TC:3 * TC], AF.Sqrt), deps=[t_sv])
                        t_rc = kb.S("V", lambda h: h.reciprocal(
                            c1_st[0:1, 4 * TC:5 * TC],
                            c1_st[0:1, 3 * TC:4 * TC]), deps=[t_sqr])

                        def mbmm(h):
                            h.matmul(PS[4][:, 0:512], ones1f[:], c1_st[0:1, 0:TC],
                                     start=True, stop=True, skip_group_check=True)
                            h.matmul(PS[5][:, 0:512], ones1f[:],
                                     c1_st[0:1, 4 * TC:5 * TC],
                                     start=True, stop=True, skip_group_check=True)
                        t_mbm = kb.S("T", mbmm, deps=[t_rc])

                        def mbev(h):
                            h.copy(c1_mb[:, 0:TC], PS[4][:, 0:512])
                            h.copy(c1_mb[:, TC:2 * TC], PS[5][:, 0:512])
                        t_mbev = kb.S("S", mbev, deps=[t_mbm, t_lnap])

                        def lnap(h):
                            for k8 in range(8):
                                sl = slice(k8 * TC, (k8 + 1) * TC)
                                h.tensor_tensor(c1_xln[:, sl], c1_x2[:, sl],
                                                c1_mb[:, 0:TC], ALU.subtract)
                                h.tensor_tensor(c1_xln[:, sl], c1_xln[:, sl],
                                                c1_mb[:, TC:2 * TC], ALU.mult)
                                h.tensor_scalar(c1_xln[:, sl], c1_xln[:, sl],
                                                lng_sb[:, k8:k8 + 1],
                                                lnb_sb[:, k8:k8 + 1],
                                                ALU.mult, ALU.add)
                        t_lnap = kb.S("V", lnap, deps=[t_mbev, t_f1ev.get(31)])

                        for f in range(32):
                            def f1mm(h, f=f):
                                for k8 in range(8):
                                    h.matmul(PS[f % 2][:, 0:512],
                                             c1_w1[:, k8 * 4096 + f * 128:
                                                   k8 * 4096 + (f + 1) * 128],
                                             c1_xln[:, k8 * TC:(k8 + 1) * TC],
                                             start=(k8 == 0), stop=(k8 == 7),
                                             skip_group_check=True)
                            tfm = kb.S("T", f1mm, deps=[t_lnap, t_f1ev.get(f - 2),
                                                        t_ldwc1])

                            def f1ev(h, f=f):
                                h.tensor_copy(c1_h1r[:, f * TC:(f + 1) * TC],
                                              PS[f % 2][:, 0:512])
                            t_f1ev[f] = kb.S("V", f1ev, deps=[
                                tfm, t_silu_b])

                        def silu_b(h):
                            for f in range(32):
                                h.activation(c1_h1[:, f * TC:(f + 1) * TC],
                                             c1_h1r[:, f * TC:(f + 1) * TC], AF.Silu,
                                             bias=ffb1_sb[:, f:f + 1])
                        t_silu_b = kb.S("S", silu_b, deps=[t_f1ev[31],
                                                           t_f2mm.get(7)])

                        for ot in range(8):
                            op_ = ot % 2

                            def ldw2(h, ot=ot, op_=op_):
                                h.dma_start(
                                    out=W2[op_][:].rearrange("p (k o) -> p k o", o=128),
                                    in_=ff2T.ap()[:, ot * 128:(ot + 1) * 128]
                                    .rearrange("(k p) o -> p k o", p=128))
                            t_ldw2[op_] = kb.S("G", ldw2, deps=[t_f2mm.get(
                                7 if ot < 2 else ot - 2)])

                            def f2mm(h, ot=ot, op_=op_):
                                for k in range(32):
                                    h.matmul(PS[2][:, 0:512],
                                             W2[op_][:, k * 128:(k + 1) * 128],
                                             c1_h1[:, k * TC:(k + 1) * TC],
                                             start=(k == 0), stop=(k == 31),
                                             skip_group_check=True)
                            t_f2mm[ot] = kb.S("T", f2mm, deps=[
                                t_silu_b, t_ldw2[op_], t_f2ev.get(ot - 1)])

                            def f2ev(h, ot=ot):
                                h.tensor_scalar(c1_ffh[:, ot * TC:(ot + 1) * TC],
                                                PS[2][:, 0:512],
                                                ffb2_sb[:, ot:ot + 1], None, ALU.add)
                            t_f2ev[ot] = kb.S("V", f2ev, deps=[t_f2mm[ot], t_spl])

                        def spl(h, seq=seq, c0=c0):
                            h.dma_start(
                                out=x2_sp.ap()[seq, :, :, c0:c0 + TC]
                                .rearrange("k p t -> p k t"),
                                in_=c1_x2[:].rearrange("p (k t) -> p k t", t=TC))
                            h.dma_start(
                                out=ffh_sp.ap()[seq, :, :, c0:c0 + TC]
                                .rearrange("k p t -> p k t"),
                                in_=c1_ffh[:].rearrange("p (k t) -> p k t", t=TC))
                        t_spl = kb.S("Y", spl, deps=[t_f2ev[7]])

        def pass_c2():
            kb.barrier()
            with nc.sbuf_tensor("c2_x2", [128, 8 * TC], BF16) as c2_x2, \
                 nc.sbuf_tensor("c2_ffh", [128, 8 * TC], BF16) as c2_ffh, \
                 nc.sbuf_tensor("c2_moe", [128, 8 * TC], BF16) as c2_moe, \
                 nc.sbuf_tensor("c2_fs", [128, 8 * TC], F32) as c2_fs, \
                 nc.sbuf_tensor("c2_tm", [128, 4 * D], F32) as c2_tm, \
                 nc.sbuf_tensor("c2_x", [128, 4 * D], F32) as c2_x:
                for seq in range(NSEQ):
                    for ch in range(NCH):
                        c0 = ch * TC

                        def lds(h, seq=seq, c0=c0):
                            h.dma_start(
                                out=c2_x2[:].rearrange("p (k t) -> p k t", t=TC),
                                in_=x2_sp.ap()[seq, :, :, c0:c0 + TC]
                                .rearrange("k p t -> p k t"))
                            h.dma_start(
                                out=c2_ffh[:].rearrange("p (k t) -> p k t", t=TC),
                                in_=ffh_sp.ap()[seq, :, :, c0:c0 + TC]
                                .rearrange("k p t -> p k t"))
                            h.dma_start(
                                out=c2_moe[:].rearrange("p (k t) -> p k t", t=TC),
                                in_=moe_fm.ap()[seq, :, :, c0:c0 + TC]
                                .rearrange("k p t -> p k t"))
                            h.dma_start(
                                out=c2_x[:].rearrange("p (g d) -> p g d", d=D),
                                in_=x_in.ap()[seq, c0:c0 + TC, :]
                                .rearrange("(g p) d -> p g d", p=128))
                        kb.S("Y", lds)

                        def fsum(h):
                            h.tensor_tensor(c2_fs[:], c2_x2[:], c2_moe[:], ALU.add)
                            h.tensor_tensor(c2_fs[:], c2_fs[:], c2_ffh[:], ALU.add)
                        kb.S("V", fsum)

                        for g in range(4):
                            def tp(h, g=g):
                                for dt8 in range(8):
                                    bank = 4 + dt8 // 4
                                    h.transpose(
                                        PS[bank][:, (dt8 % 4) * 128:
                                                 (dt8 % 4 + 1) * 128],
                                        c2_fs[:, dt8 * TC + g * 128:
                                              dt8 * TC + (g + 1) * 128],
                                        ident32[:])
                            kb.S("T", tp)

                            def tev(h, g=g):
                                h.tensor_copy(c2_tm[:, g * D:g * D + 512],
                                              PS[4][:, 0:512])
                                h.tensor_copy(c2_tm[:, g * D + 512:(g + 1) * D],
                                              PS[5][:, 0:512])
                            kb.S("V", tev)

                        def addx(h):
                            h.tensor_tensor(c2_tm[:], c2_tm[:], c2_x[:], ALU.add)
                        kb.S("V", addx)

                        def sto(h, seq=seq, c0=c0):
                            h.dma_start(
                                out=out_t.ap()[seq, c0:c0 + TC, :]
                                .rearrange("(g p) d -> p g d", p=128),
                                in_=c2_tm[:].rearrange("p (g d) -> p g d", d=D))
                        kb.S("Y", sto)

        pb_scratch = sb(nc.sbuf_tensor("convtmp", [128, TC], BF16))

        if "f1f" in passes:
            pass_f1("f")
        if "f2f" in passes:
            pass_f2("f")
        if "f1b" in passes:
            pass_f1("b")
        if "f2b" in passes:
            pass_f2("b")
        if "c1" in passes:
            pass_c1()
        if "c2" in passes:
            pass_c2()

        kb.final_wait()
        kb.st.close()

    return nc


def _host_stage(inputs):
    def T2(a):
        return np.ascontiguousarray(np.asarray(a).T)
    g = inputs
    ins = {}
    for d, p in (("f", "fwd"), ("b", "bwd")):
        ins[f"inwT_{d}"] = T2(g[f"L0_attn_{p}_in_w"])
        ins[f"xwT_{d}"] = T2(g[f"L0_attn_{p}_x_w"])
        ins[f"dtwT_{d}"] = np.ascontiguousarray(np.asarray(g[f"L0_attn_{p}_dt_w"]).T)
        ins[f"outwT_{d}"] = T2(g[f"L0_attn_{p}_out_w"])
        ins[f"convw_{d}"] = np.ascontiguousarray(g[f"L0_attn_{p}_conv_w"])
        ins[f"convb_{d}"] = np.ascontiguousarray(g[f"L0_attn_{p}_conv_b"])
        ins[f"dtb_{d}"] = np.ascontiguousarray(g[f"L0_attn_{p}_dt_b"])
        ins[f"alog_{d}"] = np.ascontiguousarray(g[f"L0_attn_{p}_A_log"])
        ins[f"dvec_{d}"] = np.ascontiguousarray(g[f"L0_attn_{p}_D"])
    ins["ff1T"] = T2(g["L0_attn_ff_w1"])
    ins["ff2T"] = T2(g["L0_attn_ff_w2"])
    ins["ffb1"] = np.ascontiguousarray(g["L0_attn_ff_b1"])
    ins["ffb2"] = np.ascontiguousarray(g["L0_attn_ff_b2"])
    ins["lng"] = np.ascontiguousarray(g["L0_attn_ff_ln_g"])
    ins["lnb"] = np.ascontiguousarray(g["L0_attn_ff_ln_b"])
    ins["gw"] = np.ascontiguousarray(np.asarray(g["L0_ffn_gate_w"])[0])
    ins["experts"] = np.ascontiguousarray(g["L0_ffn_experts"])
    ins["ident"] = np.eye(128, dtype=bfloat16)
    ins["ident32"] = np.eye(128, dtype=np.float32)
    ins["ident8"] = np.eye(8, dtype=np.float32)
    s8 = np.zeros((8, 8 * 128), dtype=np.float32)
    for e in range(8):
        s8[e, e * 128:(e + 1) * 128] = 1.0
    ins["sel8"] = s8
    ins["sel8b"] = s8.astype(bfloat16)
    s32 = np.zeros((32, 32 * 128), dtype=np.float32)
    for s in range(32):
        s32[s, s * 128:(s + 1) * 128] = 1.0
    ins["sel32b"] = s32.astype(bfloat16)
    ins["nsel32b"] = (-s32).astype(bfloat16)
    return ins


def kernel(**inputs):
    common = _host_stage(inputs)
    x = np.asarray(inputs["L0_attn_x"], dtype=np.float32)
    nc = build_program()
    in_maps = []
    for c in range(NCORES):
        m = dict(common)
        m["x"] = np.ascontiguousarray(x[2 * c:2 * c + 2])
        in_maps.append(m)
    res = run_bass_kernel_spmd(nc, in_maps, core_ids=list(range(NCORES)))
    return np.concatenate([r["out"] for r in res.results], axis=0)



# version 20
# speedup vs baseline: 1.0110x; 1.0093x over previous
import numpy as np
import ml_dtypes
import concourse.bass as bass
import concourse.mybir as mybir
from concourse.bass_utils import run_bass_kernel_spmd
from contextlib import ExitStack

bfloat16 = ml_dtypes.bfloat16
F32 = mybir.dt.float32
BF16 = mybir.dt.bfloat16
AF = mybir.ActivationFunctionType
ALU = mybir.AluOpType
AXX = mybir.AxisListType.X

T = 4096
D = 1024
DI = 2048
NS = 16
NE = 8
NSEQ = 2
NCORES = 8
TC = 512
PC = 1024
NCH = T // TC
NPAIR = T // PC
SQ = 4
NSQ = NS // SQ
LN_EPS = 1e-5
U_CLAMP = -80.0

PASS_ORDER = ["pre_a", "pre_b", "pre_c", "f1f", "f2f", "f1b", "f2b", "c1", "c2"]


class HW:

    def __init__(self, h, kb, eng):
        self._h = h
        self._kb = kb
        self._eng = eng
        self.last = None

    def fence(self):
        if self.last is None:
            return
        kb, eng = self._kb, self._eng
        kb.mark(eng, self.last)
        sem = kb._msem[eng]
        self._h.wait_ge(kb.sems[sem], kb.cnt[sem])
        kb.hi[eng][sem] = kb.cnt[sem]
        self.last = None

    def __getattr__(self, name):
        if name in ("_h", "_kb", "_eng", "last"):
            raise AttributeError(name)
        f = getattr(self._h, name)
        if not callable(f):
            return f
        kb, eng = self._kb, self._eng

        def g(*a, **k):
            r = f(*a, **k)
            if hasattr(r, "then_inc"):
                if name.startswith("dma_start"):
                    kb.dmark(eng, r)
                else:
                    self.last = r
            return r
        return g


class KB:
    def __init__(self, nc, block):
        self.nc = nc
        self.block = block
        self.st = ExitStack()
        self.sems = {}
        for name in ("mT", "mV", "mS", "mG", "dY", "dG"):
            self.sems[name] = self.st.enter_context(nc.semaphore(name))
        self.cnt = {k: 0 for k in self.sems}
        self.hi = {e: {} for e in ("T", "V", "S", "G", "Y")}
        self._m = {"T": block.tensor, "V": block.vector, "S": block.scalar,
                   "G": block.gpsimd, "Y": block.sync}
        self._msem = {"T": "mT", "V": "mV", "S": "mS", "G": "mG"}
        self._dsem = {"Y": "dY", "G": "dG", "S": "dY", "V": "dY", "T": "dY"}
        self.nsect = 0

    def dmark(self, eng, ins):
        sem = self._dsem[eng]
        self.cnt[sem] += 16
        ins.then_inc(self.sems[sem], 16)

    def mark(self, eng, ins):
        sem = self._msem[eng]
        self.cnt[sem] += 1
        ins.then_inc(self.sems[sem], 1)

    def S(self, eng, fn, deps=None):
        if deps is None:
            toks = [(s, self.cnt[s]) for s in ("mT", "mV", "mS", "mG", "dY", "dG")]
        else:
            flat = []
            for t in deps:
                if t is None:
                    continue
                if isinstance(t, list):
                    flat.extend(x for x in t if x is not None)
                else:
                    flat.append(t)
            best = {}
            for sem, val in flat:
                best[sem] = max(best.get(sem, 0), val)
            toks = sorted(best.items())
        self.nsect += 1

        def wrapper(h):
            for sem, val in toks:
                if val > 0 and self.hi[eng].get(sem, 0) < val:
                    h.wait_ge(self.sems[sem], val)
                    self.hi[eng][sem] = val
            hw = HW(h, self, eng)
            fn(hw)
            if hw.last is not None and eng in self._msem:
                self.mark(eng, hw.last)
        before = dict(self.cnt)
        self._m[eng](wrapper)
        out = []
        if eng in self._msem:
            s = self._msem[eng]
            if self.cnt[s] > before[s]:
                out.append((s, self.cnt[s]))
        s = self._dsem.get(eng)
        if s is not None and self.cnt[s] > before[s]:
            out.append((s, self.cnt[s]))
        if len(out) == 1:
            return out[0]
        return out if out else None

    def barrier(self):
        toks = [(s, self.cnt[s]) for s in ("mT", "mV", "mS", "mG", "dY", "dG")]
        for eng in ("T", "V", "S", "G", "Y"):
            def fn(h, _t=toks):
                for sem, val in _t:
                    if val > 0 and self.hi[eng].get(sem, 0) < val:
                        h.wait_ge(self.sems[sem], val)
                        self.hi[eng][sem] = val
            self._m[eng](fn)

    def final_wait(self):
        toks = [(s, self.cnt[s]) for s in ("mT", "mV", "mS", "mG", "dY", "dG")]
        for eng in ("T", "V", "S", "G", "Y"):
            def fn(h, _t=toks):
                for sem, val in _t:
                    if val > 0 and self.hi[eng].get(sem, 0) < val:
                        h.wait_ge(self.sems[sem], val)
                        self.hi[eng][sem] = val
            self._m[eng](fn)


def build_program(dbg=(), upto=None):
    nc = bass.Bass("TRN2", target_bir_lowering=False, debug=False)
    passes = PASS_ORDER if upto is None else PASS_ORDER[:PASS_ORDER.index(upto) + 1]

    def din(name, shape, dt=F32):
        return nc.dram_tensor(name, shape, dt, kind="ExternalInput")

    x_in = din("x", [NSEQ, T, D])
    inwT = {d: din(f"inwT_{d}", [D, 2 * DI]) for d in "fb"}
    xwT = {d: din(f"xwT_{d}", [DI, 96]) for d in "fb"}
    dtwT = {d: din(f"dtwT_{d}", [64, DI]) for d in "fb"}
    outwT = {d: din(f"outwT_{d}", [DI, D]) for d in "fb"}
    convw = {d: din(f"convw_{d}", [DI, 4]) for d in "fb"}
    convb = {d: din(f"convb_{d}", [DI]) for d in "fb"}
    dtb = {d: din(f"dtb_{d}", [DI]) for d in "fb"}
    alog = {d: din(f"alog_{d}", [DI, NS]) for d in "fb"}
    dvec = {d: din(f"dvec_{d}", [DI]) for d in "fb"}
    ff1T = din("ff1T", [D, 4 * D])
    ff2T = din("ff2T", [4 * D, D])
    ffb1_t = din("ffb1", [4 * D])
    ffb2_t = din("ffb2", [D])
    lng_t = din("lng", [D])
    lnb_t = din("lnb", [D])
    gw_t = din("gw", [D, NE])
    experts_t = din("experts", [NE, D, D])
    ident_in = din("ident", [128, 128], BF16)
    ident32_in = din("ident32", [128, 128], F32)
    ident8_in = din("ident8", [8, 8], F32)
    sel8_in = din("sel8", [8, 8 * 128], F32)
    sel8b_in = din("sel8b", [8, 8 * 128], BF16)
    sel32b_in = din("sel32b", [32, 32 * 128], BF16)
    nsel32b_in = din("nsel32b", [32, 32 * 128], BF16)

    out_t = nc.dram_tensor("out", [NSEQ, T, D], F32, kind="ExternalOutput")

    def scratch(name, shape, dt):
        kind = "ExternalOutput" if name in dbg else "Internal"
        return nc.dram_tensor(name, shape, dt, kind=kind)

    xn_fm = scratch("xn_fm", [NSEQ, 8, 128, T], BF16)
    moe_fm = scratch("moe_fm", [NSEQ, 8, 128, T], BF16)
    xc_sp = scratch("xc_sp", [NSEQ, 16, 128, T], BF16)
    u_sp = scratch("u_sp", [NSEQ, 16, 128, T], BF16)
    sz_sp = scratch("sz_sp", [NSEQ, 16, 128, T], BF16)
    bc_sp = scratch("bc_sp", [NSEQ, 2, NS, T], BF16)
    mix = {d: scratch(f"mix_{d}", [NSEQ, 8, 128, T], BF16) for d in "fb"}
    wm_sp = scratch("wm_sp", [NSEQ, NE, T], BF16)
    st_dbg = scratch("st_dbg", [8, 128, 32], F32)
    f2_dbg = scratch("f2_dbg", [4, 128, 4 * 1028], F32)
    g_dbg = scratch("g_dbg", [NSEQ, NE, T], F32)
    wtm_dbg = scratch("wtm_dbg", [NSEQ, 128, 256], F32)
    cnt_dbg = scratch("cnt_dbg", [NSEQ, 128, 256], F32)
    x2_sp = scratch("x2_sp", [NSEQ, 8, 128, T], BF16)
    ffh_sp = scratch("ffh_sp", [NSEQ, 8, 128, T], BF16)

    with nc.Block() as block:
        kb = KB(nc, block)
        sb = kb.st.enter_context

        ident = sb(nc.sbuf_tensor("identt", [128, 128], BF16))
        ident32 = sb(nc.sbuf_tensor("identt32", [128, 128], F32))
        ident8 = sb(nc.sbuf_tensor("identt8", [8, 8], F32))

        ones1b = sb(nc.sbuf_tensor("ones1b", [1, 128], BF16))
        nones1b = sb(nc.sbuf_tensor("nones1b", [1, 128], BF16))
        ones1f = sb(nc.sbuf_tensor("ones1f", [1, 128], F32))
        ones8f = sb(nc.sbuf_tensor("ones8f", [8, 8], F32))
        ones128b = sb(nc.sbuf_tensor("ones128b", [128, 1], BF16))
        econst = sb(nc.sbuf_tensor("econst", [128, NE], F32))
        expA = {d: sb(nc.sbuf_tensor(f"expA{d}", [128, 16 * NS], F32)) for d in "fb"}
        cw_sb = {d: sb(nc.sbuf_tensor(f"cw{d}", [128, 16 * 4], F32)) for d in "fb"}
        cb_sb = {d: sb(nc.sbuf_tensor(f"cb{d}", [128, 16], F32)) for d in "fb"}
        ndtb_sb = {d: sb(nc.sbuf_tensor(f"ndtb{d}", [128, 16], F32)) for d in "fb"}
        dv_sb = {d: sb(nc.sbuf_tensor(f"dv{d}", [128, 16], F32)) for d in "fb"}
        ffb1_sb = sb(nc.sbuf_tensor("ffb1s", [128, 32], F32))
        ffb2_sb = sb(nc.sbuf_tensor("ffb2s", [128, 8], F32))
        lng_sb = sb(nc.sbuf_tensor("lngs", [128, 8], F32))
        lnb_sb = sb(nc.sbuf_tensor("lnbs", [128, 8], F32))
        gw_sb = sb(nc.sbuf_tensor("gws", [128, 8 * NE], BF16))

        PS = [sb(nc.psum_tensor(f"ps{i}", [128, 512], F32)) for i in range(8)]

        def setup_g(h):
            for d in "fb":
                h.dma_start(out=expA[d][:].rearrange("p (k s) -> p k s", s=NS),
                            in_=alog[d].ap().rearrange("(k p) s -> p k s", p=128))
                h.dma_start(out=cw_sb[d][:].rearrange("p (k j) -> p k j", j=4),
                            in_=convw[d].ap().rearrange("(k p) j -> p k j", p=128))
                h.dma_start(out=cb_sb[d][:],
                            in_=convb[d].ap().rearrange("(k p) -> p k", p=128), allow_slow_non_contiguous=True)
                h.dma_start(out=ndtb_sb[d][:],
                            in_=dtb[d].ap().rearrange("(k p) -> p k", p=128), allow_slow_non_contiguous=True)
                h.dma_start(out=dv_sb[d][:],
                            in_=dvec[d].ap().rearrange("(k p) -> p k", p=128), allow_slow_non_contiguous=True)
            h.dma_start(out=ffb1_sb[:], in_=ffb1_t.ap().rearrange("(k p) -> p k", p=128), allow_slow_non_contiguous=True)
            h.dma_start(out=ffb2_sb[:], in_=ffb2_t.ap().rearrange("(k p) -> p k", p=128), allow_slow_non_contiguous=True)
            h.dma_start(out=lng_sb[:], in_=lng_t.ap().rearrange("(k p) -> p k", p=128), allow_slow_non_contiguous=True)
            h.dma_start(out=lnb_sb[:], in_=lnb_t.ap().rearrange("(k p) -> p k", p=128), allow_slow_non_contiguous=True)
            h.dma_start(out=gw_sb[:].rearrange("p (k e) -> p k e", e=NE),
                        in_=gw_t.ap().rearrange("(k p) e -> p k e", p=128))
            h.dma_start(out=ident[:], in_=ident_in.ap())
            h.dma_start(out=ident32[:], in_=ident32_in.ap())
            h.dma_start(out=ident8[:], in_=ident8_in.ap())

        kb.S("G", setup_g)

        def setup_v(h):
            h.memset(ones1b[:], 1.0)
            h.memset(nones1b[:], -1.0)
            h.memset(ones1f[:], 1.0)
            h.memset(ones8f[:], 1.0)
            h.memset(ones128b[:], 1.0)
            for e in range(NE):
                h.memset(econst[:, e:e + 1], float(e + 1))
            for d in "fb":
                h.tensor_scalar_mul(ndtb_sb[d][:], ndtb_sb[d][:], -1.0)
        kb.S("V", setup_v)

        def setup_s(h):
            for d in "fb":
                h.activation(expA[d][:], expA[d][:], AF.Exp)
        kb.S("S", setup_s)

        if "pre_a" in passes:
            with nc.sbuf_tensor("pa_glog", [NE, NSEQ * T], F32) as pa_glog:
                with nc.sbuf_tensor("pa_x0", [128, 4 * D], F32) as pa_x0, \
                     nc.sbuf_tensor("pa_x1", [128, 4 * D], F32) as pa_x1, \
                     nc.sbuf_tensor("pa_sq0", [128, 4 * D], F32) as pa_sq0, \
                     nc.sbuf_tensor("pa_sq1", [128, 4 * D], F32) as pa_sq1, \
                     nc.sbuf_tensor("pa_st0", [128, 32], F32) as pa_st0, \
                     nc.sbuf_tensor("pa_st1", [128, 32], F32) as pa_st1, \
                     nc.sbuf_tensor("pa_xn0", [128, 4 * D], F32) as pa_xn0, \
                     nc.sbuf_tensor("pa_xn1", [128, 4 * D], F32) as pa_xn1, \
                     nc.sbuf_tensor("pa_stage", [128, 8 * T], BF16) as pa_stage:
                    PX = [pa_x0, pa_x1]
                    PQ = [pa_sq0, pa_sq1]
                    PT = [pa_st0, pa_st1]
                    PN = [pa_xn0, pa_xn1]
                    t_ld = {}
                    t_sq = {}
                    t_stats = {}
                    t_apply = {}
                    t_tpev = {}
                    t_gmm = {}
                    t_gevac = {}
                    t_spill = None
                    gi = 0
                    for seq in range(NSEQ):
                        for xt in range(8):
                            gp_ = gi % 2

                            def ld(h, seq=seq, xt=xt, gp_=gp_):
                                h.dma_start(
                                    out=PX[gp_][:].rearrange("p (g d) -> p g d", d=D),
                                    in_=x_in.ap()[seq, xt * 512:(xt + 1) * 512, :]
                                    .rearrange("(g p) d -> p g d", p=128))
                            t_ld[gi] = kb.S("Y", ld, deps=[
                                t_apply.get(gi - 2), t_sq.get(gi - 2),
                                t_stats.get(gi - 2)])

                            def sq(h, gp_=gp_):
                                h.activation(PQ[gp_][:], PX[gp_][:], AF.Square)
                            t_sq[gi] = kb.S("S", sq, deps=[
                                t_ld[gi], t_stats.get(gi - 2)])

                            def stats(h, gp_=gp_):
                                st_ = PT[gp_]
                                for g in range(4):
                                    h.reduce_sum(st_[:, g:g + 1],
                                                 PX[gp_][:, g * D:(g + 1) * D],
                                                 axis=AXX)
                                    h.reduce_sum(st_[:, 4 + g:5 + g],
                                                 PQ[gp_][:, g * D:(g + 1) * D],
                                                 axis=AXX)
                                h.fence()
                                h.tensor_scalar_mul(st_[:, 8:16], st_[:, 0:8], 1.0 / D)
                                h.fence()
                                h.tensor_tensor(st_[:, 16:20], st_[:, 8:12],
                                                st_[:, 8:12], ALU.mult)
                                h.fence()
                                h.tensor_tensor(st_[:, 16:20], st_[:, 12:16],
                                                st_[:, 16:20], ALU.subtract)
                                h.fence()
                                h.tensor_scalar_add(st_[:, 16:20], st_[:, 16:20],
                                                    LN_EPS)
                            t_stats[gi] = kb.S("V", stats, deps=[
                                t_ld[gi], t_sq[gi], t_apply.get(gi - 2)])

                            t_sqrt = kb.S("S", lambda h, gp_=gp_: h.activation(
                                PT[gp_][:, 20:24], PT[gp_][:, 16:20], AF.Sqrt),
                                deps=[t_stats[gi]])

                            def apply(h, gp_=gp_):
                                st_ = PT[gp_]
                                h.reciprocal(st_[:, 24:28], st_[:, 20:24])
                                h.fence()
                                for g in range(4):
                                    h.tensor_scalar(
                                        PN[gp_][:, g * D:(g + 1) * D],
                                        PX[gp_][:, g * D:(g + 1) * D],
                                        st_[:, 8 + g:9 + g], st_[:, 24 + g:25 + g],
                                        ALU.subtract, ALU.mult)
                            t_apply[gi] = kb.S("V", apply, deps=[
                                t_sqrt, t_tpev.get(gi - 2)])

                            t_tp_prev = {}
                            for dt8 in range(8):
                                bank = 4 + dt8 % 2

                                def tp(h, dt8=dt8, bank=bank, gp_=gp_):
                                    for g in range(4):
                                        h.transpose(
                                            PS[bank][:, g * 128:(g + 1) * 128],
                                            PN[gp_][:, g * D + dt8 * 128:
                                                    g * D + (dt8 + 1) * 128],
                                            ident32[:])
                                ttp = kb.S("T", tp, deps=[
                                    t_apply[gi], t_tp_prev.get(dt8 - 2),
                                    t_gmm.get(gi - 1) if dt8 < 2 else None])

                                def ev(h, dt8=dt8, bank=bank, xt=xt):
                                    h.tensor_copy(
                                        pa_stage[:, dt8 * T + xt * 512:
                                                 dt8 * T + (xt + 1) * 512],
                                        PS[bank][:, 0:512])
                                t_tp_prev[dt8] = kb.S("V", ev, deps=[
                                    ttp, t_spill])
                            t_tpev[gi] = t_tp_prev[7]

                            def gmm(h, xt=xt):
                                for k8 in range(8):
                                    h.matmul(
                                        PS[6][0:NE, 0:512],
                                        gw_sb[:, k8 * NE:(k8 + 1) * NE],
                                        pa_stage[:, k8 * T + xt * 512:
                                                 k8 * T + (xt + 1) * 512],
                                        start=(k8 == 0), stop=(k8 == 7),
                                        skip_group_check=True)
                            t_gmm[gi] = kb.S("T", gmm, deps=[
                                t_tp_prev[7],
                                [t_tp_prev[i] for i in range(8)],
                                t_gevac.get(gi - 1)])

                            def gevac(h, seq=seq, xt=xt):
                                h.tensor_copy(
                                    pa_glog[0:NE, seq * T + xt * 512:
                                            seq * T + (xt + 1) * 512],
                                    PS[6][0:NE, 0:512])
                            t_gevac[gi] = kb.S("V", gevac, deps=[t_gmm[gi]])
                            gi += 1

                        def spill(h, seq=seq):
                            h.dma_start(
                                out=xn_fm.ap()[seq].rearrange("k p t -> p k t"),
                                in_=pa_stage[:].rearrange("p (k t) -> p k t", t=T))
                        t_spill = kb.S("Y", spill, deps=[
                            [t_tpev[seq * 8 + i] for i in range(8)],
                            t_gmm[seq * 8 + 7]])

                if "pre_b" in passes:
                    kb.barrier()
                    with nc.sbuf_tensor("sel8sb", [8, 8 * 128], F32) as sel8, \
                         nc.sbuf_tensor("pb_ex", [NE, T], F32) as pb_ex, \
                         nc.sbuf_tensor("pb_sc", [NE, 4], F32) as pb_sc, \
                         nc.sbuf_tensor("pb_lse", [NE, T], F32) as pb_lse, \
                         nc.sbuf_tensor("pb_gb", [128, 2 * T], F32) as pb_gb, \
                         nc.sbuf_tensor("pb_gt", [128, 32 * 8], F32) as pb_gt, \
                         nc.sbuf_tensor("pb_cnt", [128, 32 * 8], F32) as pb_cnt, \
                         nc.sbuf_tensor("pb_junk", [128, T], F32) as pb_junk, \
                         nc.sbuf_tensor("pb_wk", [128, 48], F32) as pb_wk, \
                         nc.sbuf_tensor("pb_wtm", [128, 32 * 8], F32) as pb_wtm, \
                         nc.sbuf_tensor("pb_wm", [NE, T], BF16) as pb_wm:
                        t_s8 = kb.S("G", lambda h: h.dma_start(out=sel8[:],
                                                               in_=sel8_in.ap()))
                        for seq in range(NSEQ):
                            gsl = pa_glog[0:NE, seq * T:(seq + 1) * T]

                            kb.S("V", lambda h, gsl=gsl:
                                 h.tensor_scalar_max(gsl, gsl, 1e-6))
                            kb.S("S", lambda h, gsl=gsl: h.activation(gsl, gsl, AF.Ln))

                            for it in range(8):
                                def lse_t(h, gsl=gsl):
                                    h.activation(pb_ex[:, 0:T], gsl, AF.Exp,
                                                 accum_out=pb_sc[:, 0:1])
                                    h.fence()
                                    h.activation(pb_sc[:, 1:2], pb_sc[:, 0:1], AF.Ln)
                                kb.S("S", lse_t)

                                kb.S("V", lambda h, gsl=gsl: h.tensor_scalar(
                                    gsl, gsl, pb_sc[:, 1:2], None, ALU.subtract))

                                kb.S("S", lambda h, gsl=gsl:
                                     h.activation(pb_ex[:, 0:T], gsl, AF.Exp))

                                for sl in range(8):
                                    def mm1(h, sl=sl):
                                        h.matmul(PS[6][0:NE, 0:512], ones8f[:],
                                                 pb_ex[:, sl * 512:(sl + 1) * 512],
                                                 start=True, stop=True)
                                    kb.S("T", mm1)

                                    def lnev(h, sl=sl):
                                        h.activation(
                                            pb_lse[:, sl * 512:(sl + 1) * 512],
                                            PS[6][0:NE, 0:512], AF.Ln)
                                    kb.S("S", lnev)

                                kb.S("V", lambda h, gsl=gsl: h.tensor_tensor(
                                    gsl, gsl, pb_lse[:, 0:T], ALU.subtract))

                            t_gates = kb.S("S", lambda h, gsl=gsl:
                                           h.activation(pb_ex[:, 0:T], gsl, AF.Exp))

                            if "g_dbg" in dbg:
                                def gdmp(h, seq=seq):
                                    h.dma_start(out=g_dbg.ap()[seq], in_=pb_ex[0:NE, 0:T])
                                kb.S("Y", gdmp)

                            t_gtev_last = None
                            for tt in range(32):
                                bank = 6 + tt % 2

                                def gtp(h, tt=tt, bank=bank):
                                    h.transpose(PS[bank][:, 0:NE],
                                                pb_ex[0:NE, tt * 128:(tt + 1) * 128],
                                                ident8[:])
                                tgt = kb.S("T", gtp, deps=[t_gates, t_gtev_last])

                                def gtev(h, tt=tt, bank=bank):
                                    h.tensor_copy(pb_gt[:, tt * 8:(tt + 1) * 8],
                                                  PS[bank][:, 0:NE])
                                t_gtev_last = kb.S("V", gtev, deps=[tgt])

                            t_gbev = {}
                            t_cnts = {}
                            t_bank = {6: t_gtev_last, 7: t_gtev_last}
                            for e in range(NE):
                                ep = e % 2
                                gbuf = pb_gb[:, ep * T:(ep + 1) * T]
                                for sl in range(8):
                                    bank = 6 + sl % 2

                                    def gbmm(h, e=e, sl=sl, bank=bank):
                                        h.matmul(PS[bank][:, 0:512],
                                                 sel8[:, e * 128:(e + 1) * 128],
                                                 pb_ex[0:NE, sl * 512:(sl + 1) * 512],
                                                 start=True, stop=True,
                                                 skip_group_check=True)
                                    tg = kb.S("T", gbmm, deps=[
                                        t_bank.get(bank),
                                        t_cnts.get(e - 2), t_gates, t_s8])

                                    def gbev(h, sl=sl, bank=bank, gbuf=gbuf):
                                        h.copy(gbuf[:, sl * 512:(sl + 1) * 512],
                                               PS[bank][:, 0:512])
                                    t_gbev[(e, sl)] = kb.S("S", gbev, deps=[tg])
                                    t_bank[bank] = t_gbev[(e, sl)]

                                def cnts(h, e=e, gbuf=gbuf):
                                    for tt in range(32):
                                        h.tensor_scalar(
                                            pb_junk[:, 0:T], gbuf,
                                            pb_gt[:, tt * 8 + e:tt * 8 + e + 1], 0.0,
                                            ALU.is_gt, ALU.add,
                                            accum_out=pb_cnt[:, tt * 8 + e:
                                                             tt * 8 + e + 1])
                                t_cnts[e] = kb.S("V", cnts, deps=[
                                    t_gbev[(e, 7)], t_gtev_last])

                            if "cnt_dbg" in dbg:
                                def cdmp(h, seq=seq):
                                    h.dma_start(out=cnt_dbg.ap()[seq], in_=pb_cnt[:])
                                kb.S("Y", cdmp)

                            for tt in range(32):
                                def winner(h, tt=tt):
                                    m = pb_cnt[:, tt * 8:(tt + 1) * 8]
                                    sel = pb_wk[:, 0:8]
                                    key = pb_wk[:, 8:16]
                                    win = pb_wk[:, 16:24]
                                    gv = pb_wk[:, 24:32]
                                    kmax = pb_wk[:, 32:33]
                                    gwin = pb_wk[:, 33:34]
                                    hard = pb_wk[:, 34:35]
                                    h.tensor_single_scalar(sel, m, 512.0, ALU.is_lt)
                                    h.tensor_single_scalar(key, m, 8.0, ALU.mult)
                                    h.fence()
                                    h.tensor_tensor(key, key, econst[:], ALU.add)
                                    h.fence()
                                    h.tensor_tensor(key, key, sel, ALU.mult)
                                    h.fence()
                                    h.reduce_max(kmax, key, axis=AXX)
                                    h.fence()
                                    h.tensor_scalar(win, key, kmax, None, ALU.is_equal)
                                    h.fence()
                                    h.tensor_tensor(win, win, sel, ALU.mult)
                                    h.fence()
                                    h.tensor_tensor(gv, win,
                                                    pb_gt[:, tt * 8:(tt + 1) * 8],
                                                    ALU.mult)
                                    h.fence()
                                    h.reduce_sum(gwin, gv, axis=AXX)
                                    h.fence()
                                    h.tensor_single_scalar(hard, gwin, 0.5, ALU.is_gt)
                                    h.fence()
                                    h.tensor_scalar(pb_wtm[:, tt * 8:(tt + 1) * 8],
                                                    win, hard, None, ALU.mult)
                                kb.S("V", winner)

                            if "wtm_dbg" in dbg:
                                def wtdmp(h, seq=seq):
                                    h.dma_start(out=wtm_dbg.ap()[seq], in_=pb_wtm[:])
                                kb.S("Y", wtdmp)

                            for tt in range(32):
                                def wtp(h, tt=tt):
                                    h.transpose(PS[6][0:NE, 0:128],
                                                pb_wtm[:, tt * 8:(tt + 1) * 8],
                                                ident32[:])
                                kb.S("T", wtp)

                                def wtev(h, tt=tt):
                                    h.copy(pb_wm[0:NE, tt * 128:(tt + 1) * 128],
                                           PS[6][0:NE, 0:128])
                                kb.S("S", wtev)

                            def wspill(h, seq=seq):
                                h.dma_start(out=wm_sp.ap()[seq], in_=pb_wm[0:NE, 0:T])
                            kb.S("Y", wspill)

        if "pre_c" in passes:
            kb.barrier()
            with nc.sbuf_tensor("pc_ex", [128, 64 * 1024], BF16) as pc_ex, \
                 nc.sbuf_tensor("pc_xn0", [128, 8 * TC], BF16) as pc_xn0, \
                 nc.sbuf_tensor("pc_xn1", [128, 8 * TC], BF16) as pc_xn1, \
                 nc.sbuf_tensor("pc_mb0", [128, NE * TC], BF16) as pc_mb0, \
                 nc.sbuf_tensor("pc_mb1", [128, NE * TC], BF16) as pc_mb1, \
                 nc.sbuf_tensor("pc_xnm0", [128, 8 * TC], BF16) as pc_xnm0, \
                 nc.sbuf_tensor("pc_xnm1", [128, 8 * TC], BF16) as pc_xnm1, \
                 nc.sbuf_tensor("pc_out", [128, 8 * TC], BF16) as pc_out:
                XNB = [pc_xn0, pc_xn1]
                MBB = [pc_mb0, pc_mb1]
                XNM = [pc_xnm0, pc_xnm1]

                def ldex(h):
                    h.dma_start(
                        out=pc_ex[:].rearrange("p (e i o) -> p e i o", e=NE, i=8),
                        in_=experts_t.ap().rearrange("e (i p) o -> p e i o", p=128))
                t_ldex = kb.S("G", ldex)

                NCHT = NSEQ * NCH
                t_mbb = {}
                t_xnm = {}
                t_emm = {}
                t_moev = {}
                t_moesp = {}
                t_ldxn = {}

                for gi in range(NCHT):
                    seq, ch = divmod(gi, NCH)
                    c0 = ch * TC
                    cp = gi % 2

                    def ldxn(h, seq=seq, c0=c0, cp=cp):
                        h.dma_start(
                            out=XNB[cp][:].rearrange("p (k t) -> p k t", t=TC),
                            in_=xn_fm.ap()[seq, :, :, c0:c0 + TC]
                            .rearrange("k p t -> p k t"))
                    t_ldxn[gi] = kb.S("Y", ldxn, deps=[
                        t_xnm.get((gi - 2) * 8 + 7)])

                    def mbb(h, seq=seq, c0=c0, cp=cp):
                        h.dma_start(
                            out=MBB[cp][:].rearrange("p (e t) -> p e t", t=TC),
                            in_=wm_sp.ap()[seq, :, c0:c0 + TC]
                            .rearrange("e t -> () e t")
                            .broadcast_to((128, NE, TC)))
                    t_mbb[gi] = kb.S("Y", mbb, deps=[
                        t_xnm.get((gi - 2) * 8 + 7)])

                    for e in range(NE):
                        ge = gi * 8 + e
                        ep = e % 2

                        def xnm(h, e=e, cp=cp, ep=ep):
                            for k8 in range(8):
                                h.tensor_tensor(
                                    XNM[ep][:, k8 * TC:(k8 + 1) * TC],
                                    XNB[cp][:, k8 * TC:(k8 + 1) * TC],
                                    MBB[cp][:, e * TC:(e + 1) * TC], ALU.mult)
                        t_xnm[ge] = kb.S("V", xnm, deps=[
                            t_ldxn[gi], t_mbb[gi], t_emm.get(ge - 2),
                            t_moev.get(gi - 1) if e == 0 else None])

                        def emm(h, e=e, ep=ep):
                            for ot in range(8):
                                for k8 in range(8):
                                    h.matmul(
                                        PS[ot][:, 0:512],
                                        pc_ex[:, (e * 64 + k8 * 8 + ot) * 128:
                                              (e * 64 + k8 * 8 + ot + 1) * 128],
                                        XNM[ep][:, k8 * TC:(k8 + 1) * TC],
                                        start=(e == 0 and k8 == 0),
                                        stop=(e == NE - 1 and k8 == 7),
                                        skip_group_check=True)
                        t_emm[ge] = kb.S("T", emm, deps=[
                            t_xnm[ge], t_ldex,
                            (t_moev.get(gi - 1)) if e == 0 else None])

                    def moev(h):
                        for ot in range(8):
                            h.tensor_copy(pc_out[:, ot * TC:(ot + 1) * TC],
                                          PS[ot][:, 0:512])
                    t_moev[gi] = kb.S("V", moev, deps=[t_emm[gi * 8 + 7],
                                                       t_moesp.get(gi - 1)])

                    def moesp(h, seq=seq, c0=c0):
                        h.dma_start(
                            out=moe_fm.ap()[seq, :, :, c0:c0 + TC]
                            .rearrange("k p t -> p k t"),
                            in_=pc_out[:].rearrange("p (k t) -> p k t", t=TC))
                    t_moesp[gi] = kb.S("Y", moesp, deps=[t_moev[gi]])

        def pass_f1(d):
            kb.barrier()
            with ExitStack() as _es:
                def AL(nm, shape, dt):
                    return _es.enter_context(nc.sbuf_tensor(nm + d, shape, dt))
                f1_inw = AL("f1_inw", [128, 8 * 4096], BF16)
                f1_xw = AL("f1_xw", [128, 16 * 96], BF16)
                f1_dtw = AL("f1_dtw", [64, 16 * 128], BF16)
                f1_xn0 = AL("f1_xn0", [128, 8 * TC], BF16)
                f1_xn1 = AL("f1_xn1", [128, 8 * TC], BF16)
                f1_xi = AL("f1_xi", [128, 16 * 515], BF16)
                f1_ca = AL("f1_ca", [128, 16 * TC], BF16)
                f1_xc = AL("f1_xc", [128, 16 * TC], BF16)
                f1_sz = AL("f1_sz", [128, 16 * TC], BF16)
                f1_xdbl = AL("f1_xdbl", [96, TC], BF16)
                f1_dtt = AL("f1_dtt", [128, 16 * TC], F32)
                f1_u = AL("f1_u", [128, 16 * TC], BF16)
                XN = [f1_xn0, f1_xn1]

                def ldw(h, d=d):
                    h.dma_start(out=f1_inw[:].rearrange("p (k f) -> p k f", f=4096),
                                in_=inwT[d].ap().rearrange("(k p) f -> p k f", p=128))
                    h.dma_start(out=f1_xw[:].rearrange("p (k f) -> p k f", f=96),
                                in_=xwT[d].ap().rearrange("(k p) f -> p k f", p=128))
                    h.dma_start(out=f1_dtw[:].rearrange("p (k f) -> p k f", f=128),
                                in_=dtwT[d].ap().rearrange("r (k p) -> r k p", p=128))
                t_ldw = kb.S("G", ldw)

                fwd = (d == "f")
                t_ldxn = {}
                t_mm31 = {}
                t_csilu = {}
                t_zev = {}
                t_xpmm = None
                t_xdblev = None
                t_bcspill = None
                t_dt15 = None
                t_sig = {}
                t_clamp15 = None
                t_spills = None
                ci = 0
                for seq in range(NSEQ):
                    def ztail(h):
                        if fwd:
                            h.memset(f1_xi[:].rearrange("p (k c) -> p k c", c=515)
                                     [:, :, 0:3], 0.0)
                        else:
                            h.memset(f1_xi[:].rearrange("p (k c) -> p k c", c=515)
                                     [:, :, 512:515], 0.0)
                    kb.S("V", ztail, deps=None)

                    chs = range(NCH) if fwd else range(NCH - 1, -1, -1)
                    first = True
                    for ch in chs:
                        c0 = ch * TC
                        cp = ci % 2

                        def ldxn(h, seq=seq, c0=c0, cp=cp):
                            h.dma_start(
                                out=XN[cp][:].rearrange("p (k t) -> p k t", t=TC),
                                in_=xn_fm.ap()[seq, :, :, c0:c0 + TC]
                                .rearrange("k p t -> p k t"))
                        t_ldxn[ci] = kb.S("Y", ldxn, deps=[t_mm31.get(ci - 2)])

                        if not first:
                            def tails(h):
                                xiv = f1_xi[:].rearrange("p (k c) -> p k c", c=515)
                                if fwd:
                                    h.tensor_copy(xiv[:, :, 0:3], xiv[:, :, 512:515])
                                else:
                                    h.tensor_copy(xiv[:, :, 512:515], xiv[:, :, 0:3])
                            kb.S("V", tails, deps=[])
                        first = False

                        t_ev = {}
                        for f in range(32):
                            def ipmm(h, f=f, cp=cp):
                                for k8 in range(8):
                                    h.matmul(PS[f % 2][:, 0:512],
                                             f1_inw[:, k8 * 4096 + f * 128:
                                                    k8 * 4096 + (f + 1) * 128],
                                             XN[cp][:, k8 * TC:(k8 + 1) * TC],
                                             start=(k8 == 0), stop=(k8 == 7),
                                             skip_group_check=True)
                            tmm = kb.S("T", ipmm, deps=[t_ldxn[ci], t_ldw,
                                                        t_ev.get(f - 2)])
                            if f == 31:
                                t_mm31[ci] = tmm

                            if f < 16:
                                def ipev(h, f=f):
                                    xiv = f1_xi[:].rearrange("p (k c) -> p k c", c=515)
                                    dst = xiv[:, f, 3:515] if fwd else xiv[:, f, 0:512]
                                    h.tensor_copy(dst, PS[f % 2][:, 0:512])
                                t_ev[f] = kb.S("V", ipev, deps=[tmm])
                            else:
                                def zev(h, f=f):
                                    h.tensor_copy(f1_sz[:, (f - 16) * TC:(f - 15) * TC],
                                                  PS[f % 2][:, 0:512])
                                t_ev[f] = kb.S("V", zev, deps=[tmm, t_spills])
                                t_zev[f - 16] = t_ev[f]

                        t_conv_last = None
                        for k in range(16):
                            def conv(h, k=k, d=d):
                                xiv = f1_xi[:].rearrange("p (k c) -> p k c", c=515)
                                cak = f1_ca[:, k * TC:(k + 1) * TC]
                                for j in range(4):
                                    off = j if fwd else (3 - j)
                                    src = xiv[:, k, off:off + 512]
                                    if j == 0:
                                        h.tensor_scalar(
                                            cak, src,
                                            cw_sb[d][:, k * 4 + j:k * 4 + j + 1],
                                            None, ALU.mult)
                                    else:
                                        h.tensor_scalar(
                                            pb_scratch[:], src,
                                            cw_sb[d][:, k * 4 + j:k * 4 + j + 1],
                                            None, ALU.mult)
                                        h.tensor_tensor(cak, cak,
                                                        pb_scratch[:], ALU.add)
                            t_conv_last = kb.S("V", conv, deps=[
                                t_csilu.get(0) if k == 0 else None])

                        def csilu_b(h, d=d):
                            for k in range(16):
                                h.activation(f1_xc[:, k * TC:(k + 1) * TC],
                                             f1_ca[:, k * TC:(k + 1) * TC], AF.Silu,
                                             bias=cb_sb[d][:, k:k + 1])
                            for k in range(16):
                                h.activation(f1_sz[:, k * TC:(k + 1) * TC],
                                             f1_sz[:, k * TC:(k + 1) * TC], AF.Silu)
                        t_csilu[0] = kb.S("S", csilu_b, deps=[
                            t_conv_last, t_ev[31], t_xpmm, t_spills])

                        def xpmm(h):
                            for k in range(16):
                                h.matmul(PS[2][0:96, 0:512],
                                         f1_xw[:, k * 96:(k + 1) * 96],
                                         f1_xc[:, k * TC:(k + 1) * TC],
                                         start=(k == 0), stop=(k == 15),
                                         skip_group_check=True)
                        t_xpmm = kb.S("T", xpmm, deps=[t_csilu[0], t_xdblev, t_ldw])

                        def xdblev(h):
                            h.tensor_copy(f1_xdbl[0:96, 0:TC], PS[2][0:96, 0:512])
                        t_xdblev = kb.S("V", xdblev, deps=[
                            t_xpmm, t_bcspill, t_dt15])

                        def bcspill(h, seq=seq, c0=c0, d=d):
                            h.dma_start(
                                out=bc_sp.ap()[seq, :, :, c0:c0 + TC]
                                .rearrange("c s t -> (c s) t"),
                                in_=f1_xdbl[64:96, 0:TC])
                        t_bcspill = kb.S("Y", bcspill, deps=[t_xdblev])

                        DTBANK = [3, 4, 5, 7]
                        t_dtmm = {}
                        for k in range(16):
                            def dtmm(h, k=k):
                                h.matmul(PS[DTBANK[k % 4]][:, 0:512],
                                         f1_dtw[:, k * 128:(k + 1) * 128],
                                         f1_xdbl[0:64, 0:TC],
                                         start=True, stop=True,
                                         skip_group_check=True)
                            t_dtmm[k] = kb.S("T", dtmm, deps=[
                                t_xdblev, t_sig.get(k // 4 - 1), t_ldw])
                            if k == 15:
                                t_dt15 = t_dtmm[k]

                            if k % 4 == 3:
                                def sig(h, k0=k - 3, d=d):
                                    for kk in range(k0, k0 + 4):
                                        h.activation(
                                            f1_dtt[:, kk * TC:(kk + 1) * TC],
                                            PS[DTBANK[kk % 4]][:, 0:512],
                                            AF.Sigmoid, scale=-1.0,
                                            bias=ndtb_sb[d][:, kk:kk + 1])
                                t_sig[k // 4] = kb.S("S", sig, deps=[
                                    t_dtmm[k], t_spills])

                        def lnb(h):
                            for k in range(16):
                                h.activation(f1_u[:, k * TC:(k + 1) * TC],
                                             f1_dtt[:, k * TC:(k + 1) * TC], AF.Ln)
                        t_lnb = kb.S("S", lnb, deps=[t_sig[3], t_spills])

                        def clamp(h):
                            h.tensor_scalar_max(f1_u[:], f1_u[:], U_CLAMP)
                        t_clamp15 = kb.S("V", clamp, deps=[t_lnb])

                        def spills(h, seq=seq, c0=c0):
                            h.dma_start(
                                out=xc_sp.ap()[seq, :, :, c0:c0 + TC]
                                .rearrange("k p t -> p k t"),
                                in_=f1_xc[:].rearrange("p (k t) -> p k t", t=TC))
                            h.dma_start(
                                out=u_sp.ap()[seq, :, :, c0:c0 + TC]
                                .rearrange("k p t -> p k t"),
                                in_=f1_u[:].rearrange("p (k t) -> p k t", t=TC))
                            h.dma_start(
                                out=sz_sp.ap()[seq, :, :, c0:c0 + TC]
                                .rearrange("k p t -> p k t"),
                                in_=f1_sz[:].rearrange("p (k t) -> p k t", t=TC))
                        t_spills = kb.S("Y", spills, deps=[
                            t_clamp15, t_csilu[0]])
                        ci += 1

        def pass_f2(d):
            kb.barrier()
            fwd = (d == "f")
            W = PC + 2
            with ExitStack() as _es:
                def AL(nm, shape, dt):
                    return _es.enter_context(nc.sbuf_tensor(nm + d, shape, dt))
                f2_ow = AL("f2_ow", [128, 2 * 16 * 128], BF16)
                f2_bcB = AL("f2_bcB", [128, NS * PC], BF16)
                f2_bcC = AL("f2_bcC", [128, NS * PC], BF16)
                f2_a0 = AL("f2_a0", [128, SQ * W], BF16)
                f2_a1 = AL("f2_a1", [128, SQ * W], BF16)
                f2_r0 = AL("f2_r0", [128, SQ * W], BF16)
                f2_r1 = AL("f2_r1", [128, SQ * W], BF16)
                f2_u0 = AL("f2_u0", [128, PC], BF16)
                f2_u1 = AL("f2_u1", [128, PC], BF16)
                f2_u2 = AL("f2_u2", [128, PC], BF16)
                f2_u3 = AL("f2_u3", [128, PC], BF16)
                f2_xc0 = AL("f2_xc0", [128, PC], BF16)
                f2_xc1 = AL("f2_xc1", [128, PC], BF16)
                f2_xc2 = AL("f2_xc2", [128, PC], BF16)
                f2_xc3 = AL("f2_xc3", [128, PC], BF16)
                f2_sz0 = AL("f2_sz0", [128, PC], BF16)
                f2_sz1 = AL("f2_sz1", [128, PC], BF16)
                f2_sz2 = AL("f2_sz2", [128, PC], BF16)
                f2_sz3 = AL("f2_sz3", [128, PC], BF16)
                f2_ut0 = AL("f2_ut0", [128, PC], BF16)
                f2_ut1 = AL("f2_ut1", [128, PC], BF16)
                f2_tb = AL("f2_tb", [128, PC], BF16)
                f2_yf = AL("f2_yf", [128, 16 * PC], BF16)
                f2_yg = AL("f2_yg", [128, 16 * PC], BF16)
                f2_carry = AL("f2_carry", [128, 16 * NS], F32)
                f2_mx0 = AL("f2_mx0", [128, 512], BF16)
                f2_mx1 = AL("f2_mx1", [128, 512], BF16)

                AB = [f2_a0, f2_a1]
                RB = [f2_r0, f2_r1]
                UB = [f2_u0, f2_u1, f2_u2, f2_u3]
                XCB = [f2_xc0, f2_xc1, f2_xc2, f2_xc3]
                SZB = [f2_sz0, f2_sz1, f2_sz2, f2_sz3]
                UTB = [f2_ut0, f2_ut1]
                MXB = [f2_mx0, f2_mx1]
                YF = [f2_yf, f2_yg]

                def zspc(h):
                    for ab in (f2_a0, f2_a1):
                        v = ab[:].rearrange("p (s c) -> p s c", c=W)
                        h.memset(v[:, :, 0:1], 0.0)
                        h.memset(v[:, :, 1:2], 1.0)
                    for rb_ in (f2_r0, f2_r1):
                        v = rb_[:].rearrange("p (s c) -> p s c", c=W)
                        h.memset(v[:, :, 1:2], 0.0)
                t_zspc = kb.S("V", zspc)

                t_aact = {}
                t_V = {}
                t_ymm = {}
                t_eplg = {}
                t_ldt = {}
                lastV_par = {}
                lastV_q = {}
                lastY_par = {}
                last_aact_par = {}
                last_eplg_par = {}
                last_eplg_any = [None]
                t_owr = {}
                last_hc_q = {}

                OPB = [0, 1, 6, 7]
                gidx = [0]
                t_mm_g = {}
                t_opev_g = {}
                t_opsp_g = {}
                last_op_par = {}
                op_pend = [None]

                def emit_ldws(bat, ot):
                    op_ = ot % 2

                    def ldws(h, ot=ot, op_=op_, d=d):
                        h.dma_start(
                            out=f2_ow[:, op_ * 2048:(op_ + 1) * 2048]
                            .rearrange("p (k o) -> p k o", o=128),
                            in_=outwT[d].ap()[:, ot * 128:(ot + 1) * 128]
                            .rearrange("(k p) o -> p k o", p=128))
                    bat["ow"][ot] = kb.S("G", ldws, deps=[t_owr.get(op_)])

                def emit_opmm(bat, idx):
                    ot, hf = idx // 2, idx % 2
                    g = gidx[0]
                    gidx[0] += 1
                    bank = OPB[g % 4]
                    op_ = ot % 2

                    def opmm(h, bat=bat, hf=hf, op_=op_, bank=bank):
                        pp = bat["pp"]
                        for k in range(16):
                            h.matmul(
                                PS[bank][:, 0:512],
                                f2_ow[:, op_ * 2048 + k * 128:
                                      op_ * 2048 + (k + 1) * 128],
                                YF[pp][:, k * PC + hf * 512:
                                       k * PC + (hf + 1) * 512],
                                start=(k == 0), stop=(k == 15),
                                skip_group_check=True)
                    t_mm_g[g] = kb.S("T", opmm, deps=[
                        bat["t_ep"], bat["ow"][ot], t_opev_g.get(g - 4)])
                    bat["mm"][idx] = (g, bank)
                    if hf == 1:
                        t_owr[op_] = t_mm_g[g]
                    if idx == 15:
                        last_op_par[bat["pp"]] = t_mm_g[g]

                def emit_opev(bat, idx):
                    g, bank = bat["mm"][idx]
                    mxi = g % 2
                    ot, hf = idx // 2, idx % 2

                    def opev(h, mxi=mxi, bank=bank):
                        h.tensor_copy(MXB[mxi][:], PS[bank][:, 0:512])
                    t_opev_g[g] = kb.S("V", opev, deps=[
                        t_mm_g[g], t_opsp_g.get(g - 2)])

                    def opsp(h, bat=bat, ot=ot, hf=hf, mxi=mxi, d=d):
                        h.dma_start(
                            out=mix[d].ap()[bat["seq"], ot, :,
                                            bat["p0"] + hf * 512:
                                            bat["p0"] + (hf + 1) * 512],
                            in_=MXB[mxi][:])
                    t_opsp_g[g] = kb.S("Y", opsp, deps=[t_opev_g[g]])

                def op_step(k):
                    bat = op_pend[0]
                    if bat is None:
                        return
                    if k >= 4:
                        emit_opev(bat, k - 4)
                    emit_opmm(bat, k)
                    if k == 0:
                        emit_ldws(bat, 1)
                    elif k % 2 == 1 and (k + 3) // 2 <= 7:
                        emit_ldws(bat, (k + 3) // 2)

                def op_tail():
                    bat = op_pend[0]
                    if bat is None:
                        return
                    for idx in range(12, 16):
                        emit_opev(bat, idx)
                    op_pend[0] = None

                def op_drain():
                    bat = op_pend[0]
                    if bat is None:
                        return
                    for idx in range(16):
                        if idx >= 4:
                            emit_opev(bat, idx - 4)
                        emit_opmm(bat, idx)
                        if idx == 0:
                            emit_ldws(bat, 1)
                        elif idx % 2 == 1 and (idx + 3) // 2 <= 7:
                            emit_ldws(bat, (idx + 3) // 2)
                    for idx in range(12, 16):
                        emit_opev(bat, idx)
                    op_pend[0] = None

                for seq in range(NSEQ):
                    kb.S("V", lambda h: h.memset(f2_carry[:], 0.0), deps=[])

                    prs = range(NPAIR) if fwd else range(NPAIR - 1, -1, -1)
                    for pi, p in enumerate(prs):
                        p0 = p * PC

                        t_bcq = {}
                        for qq in range(NSQ):
                            def ldbq(h, seq=seq, p0=p0, qq=qq):
                                s0_ = qq * SQ
                                for ci, dst_ in ((0, f2_bcB), (1, f2_bcC)):
                                    h.dma_start(
                                        out=dst_[:, s0_ * PC:(s0_ + SQ) * PC]
                                        .rearrange("p (s t) -> p s t", t=PC),
                                        in_=bc_sp.ap()[seq, ci, s0_:s0_ + SQ,
                                                       p0:p0 + PC]
                                        .rearrange("s t -> () s t")
                                        .broadcast_to((128, SQ, PC)))
                            t_bcq[qq] = kb.S("Y", ldbq, deps=[
                                lastV_q.get(qq), last_hc_q.get(qq)])

                        pend_eplg = [None]
                        pp = pi % 2

                        def emit_eplg(kk):
                            if kk is None:
                                return
                            kp2 = kk % 2

                            def eplg(h, k=kk, k4=kk % 4, d=d, pp=pp):
                                xcv = XCB[k4][:] if fwd else XCB[k4][:, ::-1]
                                szv = SZB[k4][:] if fwd else SZB[k4][:, ::-1]
                                h.tensor_scalar(f2_tb[:], xcv,
                                                dv_sb[d][:, k:k + 1], None, ALU.mult)
                                for hf in range(2):
                                    h.tensor_tensor(
                                        f2_tb[:, hf * 512:(hf + 1) * 512],
                                        PS[2 + hf][:, 0:512],
                                        f2_tb[:, hf * 512:(hf + 1) * 512], ALU.add)
                                h.tensor_tensor(
                                    YF[pp][:, k * PC:(k + 1) * PC],
                                    f2_tb[:], szv, ALU.mult)
                            t_eplg[kk] = kb.S("V", eplg, deps=[t_ymm[(kk, NSQ - 1)],
                                                               last_op_par.get(pp)])
                            last_eplg_par[kk % 4] = t_eplg[kk]
                            last_eplg_any[0] = t_eplg[kk]

                        for k in range(16):
                            kp = k % 2
                            k4 = k % 4

                            def ldt(h, seq=seq, p0=p0, k=k, k4=k4):
                                h.dma_start(out=UB[k4][:],
                                            in_=u_sp.ap()[seq, k, :, p0:p0 + PC])
                                h.dma_start(out=XCB[k4][:],
                                            in_=xc_sp.ap()[seq, k, :, p0:p0 + PC])
                                h.dma_start(out=SZB[k4][:],
                                            in_=sz_sp.ap()[seq, k, :, p0:p0 + PC])
                            t_ldt[k] = kb.S("G", ldt, deps=[
                                last_aact_par.get(k4), last_eplg_par.get(k4)])

                            def utt(h, kp=kp, k4=k4, fwd=fwd):
                                h.scalar_tensor_tensor(UTB[kp][:], UB[k4][:], -1.0,
                                                       XCB[k4][:],
                                                       ALU.mult, ALU.mult)
                                if not fwd:
                                    h.fence()
                            kb.S("V", utt, deps=[t_ldt[k]])

                            for q in range(NSQ):
                                qp = q % 2

                                def aact(h, k=k, q=q, k4=k4, qp=qp, d=d):
                                    for si in range(SQ):
                                        s = q * SQ + si
                                        srcv = (UB[k4][:] if fwd
                                                else UB[k4][:, ::-1])
                                        h.activation(
                                            AB[qp][:, si * W + 2:(si + 1) * W],
                                            srcv, AF.Exp,
                                            scale=expA[d][:, k * NS + s:
                                                          k * NS + s + 1])
                                t_aact[(k, q)] = kb.S("S", aact, deps=[
                                    t_ldt[k], lastV_par.get(qp) or t_zspc])
                                if q == NSQ - 1:
                                    last_aact_par[k4] = t_aact[(k, q)]

                                def bigV(h, k=k, q=q, kp=kp, qp=qp):
                                    utv = (UTB[kp][:] if fwd else UTB[kp][:, ::-1]) \
                                        .rearrange("p (o t) -> p o t", o=1) \
                                        .broadcast_to((128, SQ, PC))
                                    bv = f2_bcB[:].rearrange("p (s t) -> p s t", t=PC)
                                    bv = bv[:, q * SQ:(q + 1) * SQ, :]
                                    if not fwd:
                                        bv = bv[:, :, ::-1]
                                    rv = RB[qp][:].rearrange("p (s c) -> p s c", c=W)
                                    h.tensor_tensor(rv[:, :, 2:W], utv, bv, ALU.mult)
                                    h.memset(rv[:, :, 1:2], 0.0)
                                    h.tensor_copy(
                                        rv[:, :, 0:1].rearrange("p s o -> p (s o)"),
                                        f2_carry[:, k * NS + q * SQ:
                                                 k * NS + (q + 1) * SQ])
                                    h.fence()
                                    h.tensor_tensor_scan(
                                        RB[qp][:], AB[qp][:], RB[qp][:], 0.0,
                                        ALU.mult, ALU.add)
                                    h.fence()
                                    h.tensor_copy(
                                        f2_carry[:, k * NS + q * SQ:
                                                 k * NS + (q + 1) * SQ],
                                        rv[:, :, W - 1:W].rearrange("p s o -> p (s o)"))
                                t_V[(k, q)] = kb.S("V", bigV, deps=[
                                    t_aact[(k, q)], lastY_par.get(qp),
                                    t_bcq[q]])
                                lastV_par[qp] = t_V[(k, q)]
                                lastV_q[q] = t_V[(k, q)]
                                if q == 0 and pend_eplg[0] is not None:
                                    emit_eplg(pend_eplg[0])
                                    pend_eplg[0] = None

                                t_dmp = None
                                if ("f2_dbg" in dbg and d == "b" and seq == 0
                                        and pi == 0 and k == 0 and q == 0):
                                    def f2dmp0(h, qp=qp):
                                        h.dma_start(out=f2_dbg.ap()[0, :, 0:SQ * W],
                                                    in_=AB[qp][:])
                                        h.dma_start(out=f2_dbg.ap()[2, :, 0:SQ * W],
                                                    in_=RB[qp][:])
                                        h.dma_start(out=f2_dbg.ap()[3, :, 0:PC],
                                                    in_=UTB[0][:])
                                        h.dma_start(out=f2_dbg.ap()[3, :, PC:2 * PC],
                                                    in_=f2_bcB[:, 0:PC])
                                        h.dma_start(out=f2_dbg.ap()[3, :, 2 * PC:2 * PC + 256],
                                                    in_=f2_carry[:])
                                        h.dma_start(out=f2_dbg.ap()[3, :, 3 * PC:4 * PC],
                                                    in_=f2_bcC[:, 0:PC])
                                    t_dmp = kb.S("G", f2dmp0, deps=[t_V[(k, q)]])

                                def hcg(h, k=k, q=q, qp=qp):
                                    rv = RB[qp][:].rearrange("p (s c) -> p s c", c=W)
                                    cv = f2_bcC[:].rearrange("p (s t) -> p s t", t=PC)
                                    cv = cv[:, q * SQ:(q + 1) * SQ, :]
                                    if not fwd:
                                        cv = cv[:, :, ::-1]
                                    h.tensor_tensor(rv[:, :, 2:W], rv[:, :, 2:W],
                                                    cv, ALU.mult)
                                t_hc = kb.S("V", hcg, deps=[
                                    t_V[(k, q)], t_dmp, t_bcq[q]])
                                last_hc_q[q] = t_hc

                                def ymm(h, k=k, q=q, qp=qp):
                                    for si in range(SQ):
                                        h.matmul(PS[2][:, 0:512], ident[:],
                                                 RB[qp][:, si * W + 2:si * W + 514],
                                                 start=(q == 0 and si == 0),
                                                 stop=(q == NSQ - 1 and si == SQ - 1),
                                                 skip_group_check=True)
                                        h.matmul(PS[3][:, 0:512], ident[:],
                                                 RB[qp][:, si * W + 514:si * W + 1026],
                                                 start=(q == 0 and si == 0),
                                                 stop=(q == NSQ - 1 and si == SQ - 1),
                                                 skip_group_check=True)
                                t_ymm[(k, q)] = kb.S("T", ymm, deps=[
                                    t_hc,
                                    last_eplg_any[0] if q == 0 else None])
                                lastY_par[qp] = t_ymm[(k, q)]

                            op_step(k)
                            pend_eplg[0] = k

                        emit_eplg(pend_eplg[0])
                        pend_eplg[0] = None
                        op_tail()

                        nb = {"seq": seq, "p0": p0, "pp": pp,
                              "t_ep": t_eplg[15], "ow": {}, "mm": {}}
                        emit_ldws(nb, 0)
                        op_pend[0] = nb
                    op_drain()
                    kb.S("V", lambda h: h.memset(f2_carry[0:1, 0:1], 0.0),
                         deps=[t_opsp_g.get(gidx[0] - 1),
                               t_opsp_g.get(gidx[0] - 2)])

        def pass_c1():
            kb.barrier()
            with ExitStack() as _es:
                def AL(nm, shape, dt):
                    return _es.enter_context(nc.sbuf_tensor(nm, shape, dt))
                c1_w1 = AL("c1_w1", [128, 8 * 4096], BF16)
                c1_w2a = AL("c1_w2a", [128, 32 * 128], BF16)
                c1_w2b = AL("c1_w2b", [128, 32 * 128], BF16)
                c1_xn = AL("c1_xn", [128, 8 * TC], BF16)
                c1_yf = AL("c1_yf", [128, 8 * TC], BF16)
                c1_yb = AL("c1_yb", [128, 8 * TC], BF16)
                c1_sqt = AL("c1_sqt", [128, TC], BF16)
                c1_squ = AL("c1_squ", [128, TC], BF16)
                c1_xln = AL("c1_xln", [128, 8 * TC], BF16)
                c1_h1 = AL("c1_h1", [128, 32 * TC], BF16)
                c1_h1r = AL("c1_h1r", [128, 32 * TC], BF16)
                c1_st = AL("c1_st", [1, 5 * TC], F32)
                c1_mb = AL("c1_mb", [128, 2 * TC], F32)
                c1_ffh = AL("c1_ffh", [128, 8 * TC], BF16)
                W2 = [c1_w2a, c1_w2b]

                def ldw(h):
                    h.dma_start(out=c1_w1[:].rearrange("p (k f) -> p k f", f=4096),
                                in_=ff1T.ap().rearrange("(k p) f -> p k f", p=128))
                t_ldwc1 = kb.S("G", ldw)

                t_lds = None
                t_x2c = None
                t_stat1 = None
                t_sq7 = None
                t_lnap = None
                t_spl = None
                t_f2mm = {}
                t_f2ev = {}
                t_f1ev = {}
                t_ldw2 = {}
                t_mbev = None
                t_silu_b = None
                for seq in range(NSEQ):
                    for ch in range(NCH):
                        c0 = ch * TC
                        p = ch // 2
                        rev0 = p * PC + (512 if ch % 2 == 0 else 0)

                        def lds(h, seq=seq, c0=c0, rev0=rev0):
                            h.dma_start(
                                out=c1_xn[:].rearrange("p (k t) -> p k t", t=TC),
                                in_=xn_fm.ap()[seq, :, :, c0:c0 + TC]
                                .rearrange("k p t -> p k t"))
                            h.dma_start(
                                out=c1_yf[:].rearrange("p (k t) -> p k t", t=TC),
                                in_=mix["f"].ap()[seq, :, :, c0:c0 + TC]
                                .rearrange("k p t -> p k t"))
                            h.dma_start(
                                out=c1_yb[:].rearrange("p (k t) -> p k t", t=TC),
                                in_=mix["b"].ap()[seq, :, :, rev0:rev0 + TC]
                                .rearrange("k p t -> p k t"))
                        t_lds = kb.S("Y", lds, deps=[t_spl, t_lnap, t_stat1, t_sq7])

                        def x2c(h):
                            yfv = c1_yf[:].rearrange("p (k t) -> p k t", t=TC)
                            ybv = c1_yb[:].rearrange("p (k t) -> p k t", t=TC)[:, :, ::-1]
                            h.tensor_tensor(yfv, yfv, ybv, ALU.add)
                            h.tensor_scalar_mul(c1_yb[:], c1_yf[:], 0.5)
                            h.tensor_tensor(c1_yb[:], c1_yb[:], c1_xn[:], ALU.add)
                        t_x2c = kb.S("V", x2c, deps=[t_lds])
                        c1_x2 = c1_yb

                        def statmm1(h):
                            for k8 in range(8):
                                h.matmul(PS[4][0:1, 0:512], ones128b[:],
                                         c1_x2[:, k8 * TC:(k8 + 1) * TC],
                                         start=(k8 == 0), stop=(k8 == 7),
                                         skip_group_check=True)
                        t_stat1 = kb.S("T", statmm1, deps=[t_x2c, t_mbev])

                        SQT = [c1_sqt, c1_squ]
                        t_sq = None
                        t_stat2 = None
                        t_st2d = {}
                        for k8 in range(8):
                            def sqk(h, k8=k8):
                                h.activation(SQT[k8 % 2][:],
                                             c1_x2[:, k8 * TC:(k8 + 1) * TC],
                                             AF.Square)
                            t_sq = kb.S("S", sqk, deps=[t_x2c,
                                                        t_st2d.get(k8 - 2)])

                            def statmm2(h, k8=k8):
                                h.matmul(PS[5][0:1, 0:512], ones128b[:],
                                         SQT[k8 % 2][:],
                                         start=(k8 == 0), stop=(k8 == 7),
                                         skip_group_check=True)
                            t_stat2 = kb.S("T", statmm2, deps=[t_sq, t_mbev])
                            t_st2d[k8] = t_stat2
                        t_sq7 = t_sq

                        def statv(h):
                            h.tensor_scalar_mul(c1_st[0:1, 0:TC], PS[4][0:1, 0:512],
                                                1.0 / D)
                            h.tensor_scalar_mul(c1_st[0:1, TC:2 * TC],
                                                PS[5][0:1, 0:512], 1.0 / D)
                            h.fence()
                            h.tensor_tensor(c1_st[0:1, 2 * TC:3 * TC],
                                            c1_st[0:1, 0:TC], c1_st[0:1, 0:TC],
                                            ALU.mult)
                            h.fence()
                            h.tensor_tensor(c1_st[0:1, 2 * TC:3 * TC],
                                            c1_st[0:1, TC:2 * TC],
                                            c1_st[0:1, 2 * TC:3 * TC], ALU.subtract)
                            h.fence()
                            h.tensor_scalar_add(c1_st[0:1, 2 * TC:3 * TC],
                                                c1_st[0:1, 2 * TC:3 * TC], LN_EPS)
                        t_sv = kb.S("V", statv, deps=[t_stat1, t_stat2])

                        t_sqr = kb.S("S", lambda h: h.activation(
                            c1_st[0:1, 3 * TC:4 * TC],
                            c1_st[0:1, 2 * TC:3 * TC], AF.Sqrt), deps=[t_sv])
                        t_rc = kb.S("V", lambda h: h.reciprocal(
                            c1_st[0:1, 4 * TC:5 * TC],
                            c1_st[0:1, 3 * TC:4 * TC]), deps=[t_sqr])

                        def mbmm(h):
                            h.matmul(PS[4][:, 0:512], ones1f[:], c1_st[0:1, 0:TC],
                                     start=True, stop=True, skip_group_check=True)
                            h.matmul(PS[5][:, 0:512], ones1f[:],
                                     c1_st[0:1, 4 * TC:5 * TC],
                                     start=True, stop=True, skip_group_check=True)
                        t_mbm = kb.S("T", mbmm, deps=[t_rc])

                        def mbev(h):
                            h.copy(c1_mb[:, 0:TC], PS[4][:, 0:512])
                            h.copy(c1_mb[:, TC:2 * TC], PS[5][:, 0:512])
                        t_mbev = kb.S("S", mbev, deps=[t_mbm, t_lnap])

                        def lnap(h):
                            for k8 in range(8):
                                sl = slice(k8 * TC, (k8 + 1) * TC)
                                h.tensor_tensor(c1_xln[:, sl], c1_x2[:, sl],
                                                c1_mb[:, 0:TC], ALU.subtract)
                                h.tensor_tensor(c1_xln[:, sl], c1_xln[:, sl],
                                                c1_mb[:, TC:2 * TC], ALU.mult)
                                h.tensor_scalar(c1_xln[:, sl], c1_xln[:, sl],
                                                lng_sb[:, k8:k8 + 1],
                                                lnb_sb[:, k8:k8 + 1],
                                                ALU.mult, ALU.add)
                        t_lnap = kb.S("V", lnap, deps=[t_mbev, t_f1ev.get(31)])

                        for f in range(32):
                            def f1mm(h, f=f):
                                for k8 in range(8):
                                    h.matmul(PS[f % 2][:, 0:512],
                                             c1_w1[:, k8 * 4096 + f * 128:
                                                   k8 * 4096 + (f + 1) * 128],
                                             c1_xln[:, k8 * TC:(k8 + 1) * TC],
                                             start=(k8 == 0), stop=(k8 == 7),
                                             skip_group_check=True)
                            tfm = kb.S("T", f1mm, deps=[t_lnap, t_f1ev.get(f - 2),
                                                        t_ldwc1])

                            def f1ev(h, f=f):
                                h.tensor_copy(c1_h1r[:, f * TC:(f + 1) * TC],
                                              PS[f % 2][:, 0:512])
                            t_f1ev[f] = kb.S("V", f1ev, deps=[
                                tfm, t_silu_b])

                        def silu_b(h):
                            for f in range(32):
                                h.activation(c1_h1[:, f * TC:(f + 1) * TC],
                                             c1_h1r[:, f * TC:(f + 1) * TC], AF.Silu,
                                             bias=ffb1_sb[:, f:f + 1])
                        t_silu_b = kb.S("S", silu_b, deps=[t_f1ev[31],
                                                           t_f2mm.get(7)])

                        for ot in range(8):
                            op_ = ot % 2

                            def ldw2(h, ot=ot, op_=op_):
                                h.dma_start(
                                    out=W2[op_][:].rearrange("p (k o) -> p k o", o=128),
                                    in_=ff2T.ap()[:, ot * 128:(ot + 1) * 128]
                                    .rearrange("(k p) o -> p k o", p=128))
                            t_ldw2[op_] = kb.S("G", ldw2, deps=[t_f2mm.get(
                                7 if ot < 2 else ot - 2)])

                            def f2mm(h, ot=ot, op_=op_):
                                for k in range(32):
                                    h.matmul(PS[2][:, 0:512],
                                             W2[op_][:, k * 128:(k + 1) * 128],
                                             c1_h1[:, k * TC:(k + 1) * TC],
                                             start=(k == 0), stop=(k == 31),
                                             skip_group_check=True)
                            t_f2mm[ot] = kb.S("T", f2mm, deps=[
                                t_silu_b, t_ldw2[op_], t_f2ev.get(ot - 1)])

                            def f2ev(h, ot=ot):
                                h.tensor_scalar(c1_ffh[:, ot * TC:(ot + 1) * TC],
                                                PS[2][:, 0:512],
                                                ffb2_sb[:, ot:ot + 1], None, ALU.add)
                            t_f2ev[ot] = kb.S("V", f2ev, deps=[t_f2mm[ot], t_spl])

                        def spl(h, seq=seq, c0=c0):
                            h.dma_start(
                                out=x2_sp.ap()[seq, :, :, c0:c0 + TC]
                                .rearrange("k p t -> p k t"),
                                in_=c1_x2[:].rearrange("p (k t) -> p k t", t=TC))
                            h.dma_start(
                                out=ffh_sp.ap()[seq, :, :, c0:c0 + TC]
                                .rearrange("k p t -> p k t"),
                                in_=c1_ffh[:].rearrange("p (k t) -> p k t", t=TC))
                        t_spl = kb.S("Y", spl, deps=[t_f2ev[7]])

        def pass_c2():
            kb.barrier()
            with nc.sbuf_tensor("c2_x2", [128, 8 * TC], BF16) as c2_x2, \
                 nc.sbuf_tensor("c2_ffh", [128, 8 * TC], BF16) as c2_ffh, \
                 nc.sbuf_tensor("c2_moe", [128, 8 * TC], BF16) as c2_moe, \
                 nc.sbuf_tensor("c2_fs", [128, 8 * TC], F32) as c2_fs, \
                 nc.sbuf_tensor("c2_tm", [128, 4 * D], F32) as c2_tm, \
                 nc.sbuf_tensor("c2_x", [128, 4 * D], F32) as c2_x:
                for seq in range(NSEQ):
                    for ch in range(NCH):
                        c0 = ch * TC

                        def lds(h, seq=seq, c0=c0):
                            h.dma_start(
                                out=c2_x2[:].rearrange("p (k t) -> p k t", t=TC),
                                in_=x2_sp.ap()[seq, :, :, c0:c0 + TC]
                                .rearrange("k p t -> p k t"))
                            h.dma_start(
                                out=c2_ffh[:].rearrange("p (k t) -> p k t", t=TC),
                                in_=ffh_sp.ap()[seq, :, :, c0:c0 + TC]
                                .rearrange("k p t -> p k t"))
                            h.dma_start(
                                out=c2_moe[:].rearrange("p (k t) -> p k t", t=TC),
                                in_=moe_fm.ap()[seq, :, :, c0:c0 + TC]
                                .rearrange("k p t -> p k t"))
                            h.dma_start(
                                out=c2_x[:].rearrange("p (g d) -> p g d", d=D),
                                in_=x_in.ap()[seq, c0:c0 + TC, :]
                                .rearrange("(g p) d -> p g d", p=128))
                        kb.S("Y", lds)

                        def fsum(h):
                            h.tensor_tensor(c2_fs[:], c2_x2[:], c2_moe[:], ALU.add)
                            h.tensor_tensor(c2_fs[:], c2_fs[:], c2_ffh[:], ALU.add)
                        kb.S("V", fsum)

                        for g in range(4):
                            def tp(h, g=g):
                                for dt8 in range(8):
                                    bank = 4 + dt8 // 4
                                    h.transpose(
                                        PS[bank][:, (dt8 % 4) * 128:
                                                 (dt8 % 4 + 1) * 128],
                                        c2_fs[:, dt8 * TC + g * 128:
                                              dt8 * TC + (g + 1) * 128],
                                        ident32[:])
                            kb.S("T", tp)

                            def tev(h, g=g):
                                h.tensor_copy(c2_tm[:, g * D:g * D + 512],
                                              PS[4][:, 0:512])
                                h.tensor_copy(c2_tm[:, g * D + 512:(g + 1) * D],
                                              PS[5][:, 0:512])
                            kb.S("V", tev)

                        def addx(h):
                            h.tensor_tensor(c2_tm[:], c2_tm[:], c2_x[:], ALU.add)
                        kb.S("V", addx)

                        def sto(h, seq=seq, c0=c0):
                            h.dma_start(
                                out=out_t.ap()[seq, c0:c0 + TC, :]
                                .rearrange("(g p) d -> p g d", p=128),
                                in_=c2_tm[:].rearrange("p (g d) -> p g d", d=D))
                        kb.S("Y", sto)

        pb_scratch = sb(nc.sbuf_tensor("convtmp", [128, TC], BF16))

        if "f1f" in passes:
            pass_f1("f")
        if "f2f" in passes:
            pass_f2("f")
        if "f1b" in passes:
            pass_f1("b")
        if "f2b" in passes:
            pass_f2("b")
        if "c1" in passes:
            pass_c1()
        if "c2" in passes:
            pass_c2()

        kb.final_wait()
        kb.st.close()

    return nc


def _host_stage(inputs):
    def T2(a):
        return np.ascontiguousarray(np.asarray(a).T)
    g = inputs
    ins = {}
    for d, p in (("f", "fwd"), ("b", "bwd")):
        ins[f"inwT_{d}"] = T2(g[f"L0_attn_{p}_in_w"])
        ins[f"xwT_{d}"] = T2(g[f"L0_attn_{p}_x_w"])
        ins[f"dtwT_{d}"] = np.ascontiguousarray(np.asarray(g[f"L0_attn_{p}_dt_w"]).T)
        ins[f"outwT_{d}"] = T2(g[f"L0_attn_{p}_out_w"])
        ins[f"convw_{d}"] = np.ascontiguousarray(g[f"L0_attn_{p}_conv_w"])
        ins[f"convb_{d}"] = np.ascontiguousarray(g[f"L0_attn_{p}_conv_b"])
        ins[f"dtb_{d}"] = np.ascontiguousarray(g[f"L0_attn_{p}_dt_b"])
        ins[f"alog_{d}"] = np.ascontiguousarray(g[f"L0_attn_{p}_A_log"])
        ins[f"dvec_{d}"] = np.ascontiguousarray(g[f"L0_attn_{p}_D"])
    ins["ff1T"] = T2(g["L0_attn_ff_w1"])
    ins["ff2T"] = T2(g["L0_attn_ff_w2"])
    ins["ffb1"] = np.ascontiguousarray(g["L0_attn_ff_b1"])
    ins["ffb2"] = np.ascontiguousarray(g["L0_attn_ff_b2"])
    ins["lng"] = np.ascontiguousarray(g["L0_attn_ff_ln_g"])
    ins["lnb"] = np.ascontiguousarray(g["L0_attn_ff_ln_b"])
    ins["gw"] = np.ascontiguousarray(np.asarray(g["L0_ffn_gate_w"])[0])
    ins["experts"] = np.ascontiguousarray(g["L0_ffn_experts"])
    ins["ident"] = np.eye(128, dtype=bfloat16)
    ins["ident32"] = np.eye(128, dtype=np.float32)
    ins["ident8"] = np.eye(8, dtype=np.float32)
    s8 = np.zeros((8, 8 * 128), dtype=np.float32)
    for e in range(8):
        s8[e, e * 128:(e + 1) * 128] = 1.0
    ins["sel8"] = s8
    ins["sel8b"] = s8.astype(bfloat16)
    s32 = np.zeros((32, 32 * 128), dtype=np.float32)
    for s in range(32):
        s32[s, s * 128:(s + 1) * 128] = 1.0
    ins["sel32b"] = s32.astype(bfloat16)
    ins["nsel32b"] = (-s32).astype(bfloat16)
    return ins


def kernel(**inputs):
    common = _host_stage(inputs)
    x = np.asarray(inputs["L0_attn_x"], dtype=np.float32)
    nc = build_program()
    in_maps = []
    for c in range(NCORES):
        m = dict(common)
        m["x"] = np.ascontiguousarray(x[2 * c:2 * c + 2])
        in_maps.append(m)
    res = run_bass_kernel_spmd(nc, in_maps, core_ids=list(range(NCORES)))
    return np.concatenate([r["out"] for r in res.results], axis=0)

